# version 15
# baseline (speedup 1.0000x reference)
import contextlib
import numpy as np
import concourse.bass as bass
import concourse.mybir as mybir
from concourse.bass_utils import run_bass_kernel_spmd

F32 = mybir.dt.float32
BF16 = mybir.dt.bfloat16
AF = mybir.ActivationFunctionType
ALU = mybir.AluOpType
AXX = mybir.AxisListType.X

LN_EPS = 1e-5
ATT_EPS = 1e-6
ROPE_BASE = 10000.0


class Cfg:
    def __init__(self, NC=8, TBW=512, D=2048, H=16, KVH=4, HID=1408, E=8):
        self.NC = NC
        self.TBW = TBW
        self.D = D
        self.H = H
        self.KVH = KVH
        self.HD = 128
        self.HID = HID
        self.E = E
        self.B = 2
        self.NDT = D // 128
        self.NHT = HID // 128
        self.GQ = H // KVH
        self.CH = TBW // 128
        self.NPRE = NC - 1
        self.S = NC * TBW


FULL = Cfg()


class Sched:

    ENGS = ("sync", "gpsimd", "vector", "scalar", "tensor")

    def __init__(self, nc, ctx):
        self.nc = nc
        self.ctx = ctx
        self.sems = {e: ctx.enter_context(nc.semaphore(f"ch_{e}")) for e in self.ENGS}
        self.cnt = {e: 0 for e in self.ENGS}
        self.progs = {e: [] for e in self.ENGS}
        self.wm = {e: {} for e in self.ENGS}
        self.writer = {}
        self.readers = {}

    def _stream_sem(self, key):
        if key not in self.sems:
            self.sems[key] = self.ctx.enter_context(self.nc.semaphore(f"d_{key}"))
            self.cnt[key] = 0
        return self.sems[key]

    def _wait(self, eng, tgt, val):
        if val <= 0 or self.wm[eng].get(tgt, 0) >= val:
            return
        if tgt == eng and val > self.cnt.get(eng, 0):
            return
        assert val <= self.cnt.get(tgt, 0), (
            f"{eng} would wait on {tgt}>={val} before that inc is emitted "
            f"(cnt={self.cnt.get(tgt, 0)}): open accumulation group spans a "
            f"buffer refill -> deadlock")
        self.wm[eng][tgt] = val
        sem = self.sems[tgt]
        self.progs[eng].append(lambda e, s=sem, v=val: e.wait_ge(s, v))

    def _deps(self, eng, reads, writes):
        for t in reads:
            w = self.writer.get(t)
            if w is not None:
                self._wait(eng, w[0], w[1])
        for t in writes:
            w = self.writer.get(t)
            if w is not None:
                self._wait(eng, w[0], w[1])
            for (re, rv) in self.readers.get(t, ()):
                self._wait(eng, re, rv)

    def _record(self, stream, val, reads, writes):
        for t in reads:
            self.readers.setdefault(t, []).append((stream, val))
        for t in writes:
            self.writer[t] = (stream, val)
            self.readers[t] = []

    def op(self, eng, fn, reads=(), writes=(), inc=True):
        self._deps(eng, reads, writes)
        if inc:
            self.cnt[eng] += 1
            val = self.cnt[eng]
            sem = self.sems[eng]
            self.progs[eng].append(lambda e, f=fn, s=sem: f(e).then_inc(s, 1))
        else:
            val = self.cnt[eng] + 1
            self.progs[eng].append(lambda e, f=fn: f(e))
        self._record(eng, val, reads, writes)

    def dma(self, eng, fn, key, reads=(), writes=()):
        self._deps(eng, reads, writes)
        sem = self._stream_sem(key)
        self.cnt[key] += 16
        val = self.cnt[key]
        self.progs[eng].append(lambda e, f=fn, s=sem: f(e).then_inc(s, 16))
        self._record(key, val, reads, writes)

    def counts(self):
        return dict(self.cnt)

    def wait_counts(self, eng, snap):
        for tgt, v in snap.items():
            if tgt != eng:
                self._wait(eng, tgt, v)

    def barrier(self, engines=None):
        engines = engines or self.ENGS
        snap = self.counts()
        for e in engines:
            self.wait_counts(e, snap)

    def finish(self):
        self.wait_counts("sync", self.counts())
        v = self.cnt["sync"]
        if v:
            sem = self.sems["sync"]
            self.progs["sync"].append(lambda e, s=sem, vv=v: e.wait_ge(s, vv))


def build_program(cfg, phases=("A", "C", "G", "F", "L2")):
    c = cfg
    nc = bass.Bass("TRN2", target_bir_lowering=False)

    dr = lambda name, shp: nc.dram_tensor(name, list(shp), F32, kind="ExternalInput").ap()
    ins = {}
    ins["xs"] = dr("xs", (c.B, 128, c.NDT, c.TBW))
    ins["cosc"] = dr("cosc", (c.B, 128, c.TBW))
    ins["sinmc"] = dr("sinmc", (c.B, 128, c.TBW))
    ins["pmask"] = dr("pmask", (128, 2 * c.NC))
    ins["wq_pk"] = dr("wq_pk", (c.H, 128, c.NDT, 128))
    ins["wk_pk"] = dr("wk_pk", (128, c.NDT, c.KVH, 128))
    ins["wv_pk"] = dr("wv_pk", (128, c.NDT, c.KVH, 128))
    ins["wo_pk"] = dr("wo_pk", (c.NDT, 128, c.H, 128))
    ins["gate_pk"] = dr("gate_pk", (128, c.NDT, c.E))
    ins["bias_row"] = dr("bias_row", (1, c.E))
    ins["w1_pk"] = dr("w1_pk", (c.E, c.NHT, 128, c.NDT, 128))
    ins["w3_pk"] = dr("w3_pk", (c.E, c.NHT, 128, c.NDT, 128))
    ins["w2_pk"] = dr("w2_pk", (c.E, c.NDT, 128, c.NHT, 128))
    ins["sw1_pk"] = dr("sw1_pk", (c.NHT, 128, c.NDT, 128))
    ins["sw3_pk"] = dr("sw3_pk", (c.NHT, 128, c.NDT, 128))
    ins["sw2_pk"] = dr("sw2_pk", (c.NDT, 128, c.NHT, 128))
    ins["id_f"] = dr("id_f", (128, 128))
    ins["r64"] = dr("r64", (128, 128))
    ins["esel"] = dr("esel", (c.E, c.E * 128))
    ins["maskg"] = dr("maskg", (128, c.GQ * 128))

    out_dram = nc.dram_tensor("out", [c.D, c.B * c.TBW], F32, kind="ExternalOutput").ap()
    SZC = 2 * c.B * c.KVH * 128
    cin = nc.dram_tensor("cin", [128, SZC], BF16, kind="Internal").ap()
    cout = nc.dram_tensor("cout", [c.NC * 128, SZC], BF16, kind="Internal",
                          addr_space="Shared" if c.NC > 4 else "Local").ap()

    with contextlib.ExitStack() as ctx:
        sbuf = lambda n, cols, dt: ctx.enter_context(nc.sbuf_tensor(n, [128, cols], dt))
        psum = lambda n, cols, dt: ctx.enter_context(nc.psum_tensor(n, [128, cols], dt))

        NT = c.NDT * c.TBW
        AR = max(2 * NT + 4 * c.TBW,
                 2 * c.H * c.TBW + 4 * c.KVH * c.TBW,
                 c.NHT * c.B * c.TBW + 4 * c.NDT * 128 + 2 * c.NHT * 128)

        x_bf = sbuf("x_bf", c.B * NT, BF16)
        xr = sbuf("xr", c.B * NT, BF16)
        ar = sbuf("ar", AR, BF16)
        wk_sb = sbuf("wk_sb", c.NDT * c.KVH * 128, BF16)
        wv_sb = sbuf("wv_sb", c.NDT * c.KVH * 128, BF16)
        wqo = [sbuf(f"wqo{i}", max(c.NDT, c.H) * 128, BF16) for i in range(2)]
        gate_sb = sbuf("gate_sb", c.NDT * c.E, BF16)
        cosc_sb = sbuf("cosc_sb", c.B * c.TBW, BF16)
        sinmc_sb = sbuf("sinmc_sb", c.B * c.TBW, BF16)
        rb = [sbuf("rb0", 2 * c.B * c.KVH * 128, BF16)]
        pmask_sb = sbuf("pmask_sb", 2 * c.NC, F32)
        SCR = max(c.TBW, c.GQ * 128)
        rsA = [sbuf(f"rsA{i}", SCR, F32) for i in range(2)]
        rsB = [sbuf(f"rsB{i}", SCR, F32) for i in range(2)]
        act_s = [sbuf(f"act{i}", c.TBW, F32) for i in range(2)]
        silu_s = [sbuf(f"silu{i}", c.TBW, BF16) for i in range(2)]
        sq_s = [sbuf(f"sq{i}", c.TBW, BF16) for i in range(2)]
        S_bf = sbuf("S_bf", c.B * c.KVH * 128, BF16)
        Z_bf = sbuf("Z_bf", c.B * c.KVH * 128, BF16)
        m_b = sbuf("m_b", c.TBW, F32)
        rstd_b = sbuf("rstd_b", c.TBW, F32)
        G_T = ctx.enter_context(nc.sbuf_tensor("G_T", [c.E, c.B * c.TBW], F32))
        scor = ctx.enter_context(nc.sbuf_tensor("scor", [c.E, c.B * c.TBW], F32))
        amst = sbuf("amst", max(c.KVH * c.GQ * 128, 2 * c.B * c.KVH * 128), BF16)
        sloc = amst
        a2 = [silu_s[0], silu_s[1], sq_s[0], sq_s[1]]
        sct = sbuf("sct", c.E, F32)
        sbt = sbuf("sbt", c.E, F32)
        is1 = sbuf("is1", c.E, F32)
        sbm = sbuf("sbm", c.E, F32)
        m2t = sbuf("m2t", c.E, F32)
        gtk = sbuf("gtk", c.E, F32)
        colv = sbuf("colv", 1, F32)
        colv2 = sbuf("colv2", 1, F32)
        id_f = sbuf("id_f_sb", 128, F32)
        id_bf = sbuf("id_bf", 128, BF16)
        r64_bf = sbuf("r64_bf", 128, BF16)
        maskg_sb = sbuf("maskg_sb", c.GQ * 128, F32)
        ones128 = sbuf("ones128", 128, BF16)
        onescol = sbuf("onescol", 1, BF16)
        ones_row_f = ctx.enter_context(nc.sbuf_tensor("ones_row_f", [1, 128], F32))
        epsc = sbuf("epsc", 1, F32)
        biasb = sbuf("biasb", c.E, F32)
        bias_row = ctx.enter_context(nc.sbuf_tensor("bias_row_sb", [1, c.E], F32))
        esel_sb = ctx.enter_context(nc.sbuf_tensor("esel_sb", [c.E, c.E * 128], F32))

        PB = [psum(f"PB_{i}", c.TBW, F32) for i in range(6)]
        PT = [ctx.enter_context(nc.psum_tensor(f"PT_{i}", [128, 1024], BF16))
              for i in range(2)]

        s = Sched(nc, ctx)

        def mm(out_ap, lhs_ap, rhs_ap, bank, reads, start=True, stop=True, inc=True):
            s.op("tensor",
                 lambda t, o=out_ap, l=lhs_ap, r=rhs_ap, a=start, b=stop:
                     t.matmul(o, l, r, start=a, stop=b),
                 reads=reads, writes=(bank,), inc=inc)

        def tr(out_ap, in_ap, ident_ap, bank, reads, inc=True):
            s.op("tensor",
                 lambda t, o=out_ap, i=in_ap, d=ident_ap: t.transpose(o, i, d),
                 reads=reads, writes=(bank,), inc=inc)

        def vop(fn, reads, writes, eng="vector", inc=True):
            s.op(eng, fn, reads=reads, writes=writes, inc=inc)

        def gdma(out_ap, in_ap, wtok):
            s.dma("gpsimd", lambda g, o=out_ap, i=in_ap: g.dma_start(out=o, in_=i),
                  key="L" + wtok, writes=(wtok,))

        def sdma_in(out_ap, in_ap, wtok):
            s.dma("sync", lambda g, o=out_ap, i=in_ap: g.dma_start(out=o, in_=i),
                  key="L" + wtok, writes=(wtok,))

        def sdma_out(out_ap, in_ap, rtok):
            s.dma("sync", lambda g, o=out_ap, i=in_ap: g.dma_start(out=o, in_=i),
                  key="S" + rtok, reads=(rtok,))

        flat = lambda ap: ap.rearrange("p a b -> p (a b)")

        sdma_in(id_f[:], ins["id_f"][:, :], "idf")
        sdma_in(maskg_sb[:], ins["maskg"][:, :], "maskg")
        sdma_in(pmask_sb[:], ins["pmask"][:, :], "pmask")
        sdma_in(bias_row[:], ins["bias_row"][:, :], "bias_row")
        sdma_in(esel_sb[:], ins["esel"][:, :], "esel")
        gdma(id_bf[:], ins["id_f"][:, :], "id_bf")
        gdma(r64_bf[:], ins["r64"][:, :], "r64")
        gdma(wk_sb[:], ins["wk_pk"].rearrange("p a b c -> p (a b c)"), "wk")
        gdma(wv_sb[:], ins["wv_pk"].rearrange("p a b c -> p (a b c)"), "wv")

        def emit_own_loads():
            for tb in range(c.B):
                gdma(cosc_sb[:, tb * c.TBW:(tb + 1) * c.TBW], ins["cosc"][tb], "cosc")
                gdma(sinmc_sb[:, tb * c.TBW:(tb + 1) * c.TBW], ins["sinmc"][tb], "sinmc")
            for tb in range(c.B):
                gdma(x_bf[:, tb * NT:(tb + 1) * NT], flat(ins["xs"][tb]), "x")
            gdma(gate_sb[:], flat(ins["gate_pk"]), "gate")
        vop(lambda v: v.memset(ones128[:], 1.0), (), ("ones128",))
        vop(lambda v: v.memset(onescol[:], 1.0), (), ("onescol",))
        vop(lambda v: v.memset(ones_row_f[:], 1.0), (), ("onesrow",))
        vop(lambda v: v.memset(epsc[:], ATT_EPS), (), ("epsc",))
        vop(lambda v: v.memset(S_bf[:], 0.0), (), ("Sbf",))
        vop(lambda v: v.memset(Z_bf[:], 0.0), (), ("Zbf",))
        mm(PB[5][:, 0:c.E], ones_row_f[0:1, :], bias_row[0:1, :], "PB5",
           ("onesrow", "bias_row"))
        vop(lambda v: v.tensor_copy(biasb[:], PB[5][:, 0:c.E]), ("PB5",), ("biasb",))

        def rope_elu(pb_ap, bank_tok, cos_ap, sin_ap, costoks, dst_ap, dsttok, pipe,
                     rot_bank=2):
            t1, qr, acts = (rsA if pipe == 0 else rsB)[0], \
                           (rsA if pipe == 0 else rsB)[1], act_s[pipe]
            tok1, tokq, toka = (f"rs{pipe}0", f"rs{pipe}1", f"act{pipe}")
            W = dst_ap.shape[-1]
            vop(lambda v, o=t1, p=pb_ap, sn=sin_ap:
                    v.tensor_mul(o[0:64, 0:W], p[64:128, :], sn[64:128, :]),
                (bank_tok,) + costoks, (tok1,), inc=False)
            vop(lambda v, o=t1, p=pb_ap, sn=sin_ap:
                    v.tensor_mul(o[64:128, 0:W], p[0:64, :], sn[0:64, :]),
                (bank_tok,) + costoks, (tok1,))
            vop(lambda v, o=qr, p=pb_ap, cs=cos_ap:
                    v.tensor_mul(o[:, 0:W], p[:], cs[:]),
                (bank_tok,) + costoks, (tokq,))
            vop(lambda v, o=qr, i=t1: v.tensor_add(o[:, 0:W], o[:, 0:W], i[:, 0:W]),
                (tok1, tokq), (tokq,))
            s.op("scalar", lambda a, o=acts, i=qr:
                     a.activation(o[:, 0:W], i[:, 0:W], AF.Relu, scale=-1.0),
                 reads=(tokq,), writes=(toka,))
            s.op("scalar", lambda a, o=acts:
                     a.activation(o[:, 0:W], o[:, 0:W], AF.Exp, scale=-1.0),
                 reads=(toka,), writes=(toka,))
            vop(lambda v, o=dst_ap, i=qr, e=acts:
                    v.scalar_tensor_tensor(o, i[:, 0:W], 0.0, e[:, 0:W], ALU.max, ALU.add),
                (tokq, toka), (dsttok,))

        HT = c.H * c.TBW
        KT = c.KVH * c.TBW
        c_qf = ar[:, 0:HT]
        c_ai = ar[:, HT:2 * HT]
        c_kf = ar[:, 2 * HT:2 * HT + KT]
        c_v = ar[:, 2 * HT + KT:2 * HT + 2 * KT]
        c_kft = ar[:, 2 * HT + 2 * KT:2 * HT + 3 * KT]
        c_vt = ar[:, 2 * HT + 3 * KT:2 * HT + 4 * KT]
        SZH = c.B * c.KVH * 128

        emit_own_loads()

        def emit_kv_block(tb, kv, dsts, toks):
            xb_ = x_bf[:, tb * NT:(tb + 1) * NT]
            pk_, pv_ = 2 * (kv % 2), 2 * (kv % 2) + 1
            pipe = kv % 2
            for dt in range(c.NDT):
                mm(PB[pk_][:, 0:c.TBW], wk_sb[:, (dt * c.KVH + kv) * 128:(dt * c.KVH + kv + 1) * 128],
                   xb_[:, dt * c.TBW:(dt + 1) * c.TBW], f"PB{pk_}", ("wk", "x"),
                   start=(dt == 0), stop=(dt == c.NDT - 1), inc=(dt == c.NDT - 1))
            for dt in range(c.NDT):
                mm(PB[pv_][:, 0:c.TBW], wv_sb[:, (dt * c.KVH + kv) * 128:(dt * c.KVH + kv + 1) * 128],
                   xb_[:, dt * c.TBW:(dt + 1) * c.TBW], f"PB{pv_}", ("wv", "x"),
                   start=(dt == 0), stop=(dt == c.NDT - 1), inc=(dt == c.NDT - 1))
            cosb_ = cosc_sb[:, tb * c.TBW:(tb + 1) * c.TBW]
            sinb_ = sinmc_sb[:, tb * c.TBW:(tb + 1) * c.TBW]
            rope_elu(PB[pk_][:, 0:c.TBW], f"PB{pk_}", cosb_, sinb_,
                     ("cosc", "sinmc"), dsts[0], toks[0], pipe)
            vop(lambda v, o=dsts[1], p=PB[pv_]: v.tensor_copy(o, p[:, 0:c.TBW]),
                (f"PB{pv_}",), (toks[1],))
            for ci in range(c.CH):
                tr(PT[0][:, 512 + ci * 128:512 + (ci + 1) * 128],
                   dsts[0][:, ci * 128:(ci + 1) * 128], id_bf[:], "PT0",
                   (toks[0], "id_bf"), inc=(ci == c.CH - 1))
                tr(PT[1][:, 512 + ci * 128:512 + (ci + 1) * 128],
                   dsts[1][:, ci * 128:(ci + 1) * 128], id_bf[:], "PT1",
                   (toks[1], "id_bf"), inc=(ci == c.CH - 1))
            vop(lambda v, o=dsts[2], p=PT[0]: v.tensor_copy(o, p[:, 512:512 + c.TBW]),
                ("PT0",), (toks[2],))
            vop(lambda v, o=dsts[3], p=PT[1]: v.tensor_copy(o, p[:, 512:512 + c.TBW]),
                ("PT1",), (toks[3],))
            so = (tb * c.KVH + kv) * 128
            for ci in range(c.CH):
                mm(PB[4][:, 0:128], dsts[2][:, ci * 128:(ci + 1) * 128],
                   dsts[3][:, ci * 128:(ci + 1) * 128], "PB4", (toks[2], toks[3]),
                   start=(ci == 0), stop=(ci == c.CH - 1), inc=(ci == c.CH - 1))
            vop(lambda v, q=so: v.tensor_copy(sloc[:, q:q + 128], PB[4][:, 0:128]),
                ("PB4",), ("sloc",))
            for ci in range(c.CH):
                mm(PB[5][:, 0:128], dsts[2][:, ci * 128:(ci + 1) * 128],
                   ones128[:], "PB5", (toks[2], "ones128"),
                   start=(ci == 0), stop=(ci == c.CH - 1), inc=(ci == c.CH - 1))
            vop(lambda v, q=so: v.tensor_copy(sloc[:, SZH + q:SZH + q + 128],
                                              PB[5][:, 0:128]),
                ("PB5",), ("sloc",))

        for kv in range(c.KVH):
            emit_kv_block(0, kv, (c_kf[:, kv * c.TBW:(kv + 1) * c.TBW],
                                  c_v[:, kv * c.TBW:(kv + 1) * c.TBW],
                                  c_kft[:, kv * c.TBW:(kv + 1) * c.TBW],
                                  c_vt[:, kv * c.TBW:(kv + 1) * c.TBW]),
                          ("ckf", "cv", "ckft", "cvt"))
        for kv in range(c.KVH):
            emit_kv_block(1, kv, (a2[0][:, 0:c.TBW], a2[1][:, 0:c.TBW],
                                  a2[2][:, 0:c.TBW], a2[3][:, 0:c.TBW]),
                          ("silu0", "silu1", "sq0", "sq1"))
        s.dma("sync", lambda g: g.dma_start(out=cin[:, :], in_=sloc[:]),
              key="Lcin", reads=("sloc",), writes=("cin_d",))
        s.op("gpsimd", lambda g: g.collective_compute(
                "AllGather", ALU.bypass, ins=[cin[:, :]], outs=[cout[:, :]],
                replica_groups=[list(range(c.NC))]),
             reads=("cin_d",), writes=("cout_d",))

        def emit_prefix_accum():
            KW = c.KVH * 128
            for n in range(c.NC):
                sl = 0
                s.dma("sync", lambda g, o=rb[sl][:], i=cout[n * 128:(n + 1) * 128, :]:
                          g.dma_start(out=o, in_=i),
                      key=f"Lrb{sl}", reads=("cout_d",), writes=(f"rb{sl}",))
                for tbp in range(c.B):
                    pm = pmask_sb[:, tbp * c.NC + n:tbp * c.NC + n + 1]
                    q = tbp * KW
                    vop(lambda v, r=rb[sl], p=pm, qq=q:
                            v.scalar_tensor_tensor(S_bf[:, qq:qq + KW], r[:, qq:qq + KW],
                                                   p, S_bf[:, qq:qq + KW],
                                                   ALU.mult, ALU.add),
                        (f"rb{sl}", "pmask", "Sbf"), ("Sbf",))
                    vop(lambda v, r=rb[sl], p=pm, qq=q:
                            v.scalar_tensor_tensor(Z_bf[:, qq:qq + KW],
                                                   r[:, SZH + qq:SZH + qq + KW],
                                                   p, Z_bf[:, qq:qq + KW],
                                                   ALU.mult, ALU.add),
                        (f"rb{sl}", "pmask", "Zbf"), ("Zbf",))

        def ln_stats_and_norm(tb, src_tok, write_stage, bk=(0, 1)):
            xrb = xr[:, tb * NT:(tb + 1) * NT]
            for dt in range(c.NDT):
                sl = dt % 2
                s.op("scalar", lambda a, o=sq_s[sl], i=xrb, d=dt:
                         a.activation(o[:], i[:, d * c.TBW:(d + 1) * c.TBW], AF.Square),
                     reads=(src_tok,), writes=(f"sq{sl}",))
                mm(PB[bk[0]][:, 0:c.TBW], ones128[:], xrb[:, dt * c.TBW:(dt + 1) * c.TBW],
                   f"PB{bk[0]}", ("ones128", src_tok), start=(dt == 0), stop=(dt == c.NDT - 1))
                mm(PB[bk[1]][:, 0:c.TBW], ones128[:], sq_s[sl][:], f"PB{bk[1]}",
                   ("ones128", f"sq{sl}"), start=(dt == 0), stop=(dt == c.NDT - 1))
            inv_d = 1.0 / c.D
            vop(lambda v: v.tensor_scalar_mul(m_b[:], PB[bk[0]][:, 0:c.TBW], inv_d),
                (f"PB{bk[0]}",), ("m",))
            vop(lambda v: v.tensor_mul(rsA[0][:, 0:c.TBW], m_b[:], m_b[:]), ("m",), ("rs00",))
            vop(lambda v: v.scalar_tensor_tensor(rsA[1][:, 0:c.TBW], PB[bk[1]][:, 0:c.TBW], inv_d,
                                                 rsA[0][:, 0:c.TBW], ALU.mult, ALU.subtract),
                (f"PB{bk[1]}", "rs00"), ("rs01",))
            vop(lambda v: v.tensor_scalar_add(rsA[1][:, 0:c.TBW], rsA[1][:, 0:c.TBW], LN_EPS),
                ("rs01",), ("rs01",))
            vop(lambda v: v.reciprocal(rsA[1][:, 0:c.TBW], rsA[1][:, 0:c.TBW]), ("rs01",), ("rs01",))
            s.op("scalar", lambda a: a.activation(rstd_b[:], rsA[1][:, 0:c.TBW], AF.Sqrt),
                 reads=("rs01",), writes=("rstd",))
            for dt in range(c.NDT):
                sl = dt % 2
                xt = xrb[:, dt * c.TBW:(dt + 1) * c.TBW]
                vop(lambda v, o=rsB[sl], i=xt: v.tensor_sub(o[:, 0:c.TBW], i, m_b[:]),
                    (src_tok, "m"), (f"rs1{sl}",))
                if write_stage is None:
                    vop(lambda v, o=xt, i=rsB[sl]: v.tensor_mul(o, i[:, 0:c.TBW], rstd_b[:]),
                        (f"rs1{sl}", "rstd", src_tok), (src_tok,))
                else:
                    write_stage(tb, dt, sl)

        for tb in range(c.B if "C" in phases else 0):
            xb = x_bf[:, tb * NT:(tb + 1) * NT]
            cosb = cosc_sb[:, tb * c.TBW:(tb + 1) * c.TBW]
            sinb = sinmc_sb[:, tb * c.TBW:(tb + 1) * c.TBW]
            for kv in range(c.KVH if (tb > 0 and "C1" not in phases) else 0):
                for dt in range(c.NDT):
                    mm(PB[0][:, 0:c.TBW], wk_sb[:, (dt * c.KVH + kv) * 128:(dt * c.KVH + kv + 1) * 128],
                       xb[:, dt * c.TBW:(dt + 1) * c.TBW], "PB0", ("wk", "x"),
                       start=(dt == 0), stop=(dt == c.NDT - 1), inc=(dt == c.NDT - 1))
                for dt in range(c.NDT):
                    mm(PB[1][:, 0:c.TBW], wv_sb[:, (dt * c.KVH + kv) * 128:(dt * c.KVH + kv + 1) * 128],
                       xb[:, dt * c.TBW:(dt + 1) * c.TBW], "PB1", ("wv", "x"),
                       start=(dt == 0), stop=(dt == c.NDT - 1), inc=(dt == c.NDT - 1))
                kfs = c_kf[:, kv * c.TBW:(kv + 1) * c.TBW]
                rope_elu(PB[0][:, 0:c.TBW], "PB0", cosb, sinb, ("cosc", "sinmc"),
                         kfs, "ckf", kv % 2, rot_bank=2 + kv % 2)
                vop(lambda v, o=c_v, p=PB[1], q=kv:
                        v.tensor_copy(o[:, q * c.TBW:(q + 1) * c.TBW], p[:, 0:c.TBW]),
                    ("PB1",), ("cv",))
                for ci in range(c.CH):
                    off = kv * c.TBW + ci * 128
                    tr(PT[0][:, 512 + ci * 128:512 + (ci + 1) * 128], c_kf[:, off:off + 128],
                       id_bf[:], "PT0", ("ckf", "id_bf"), inc=(ci == c.CH - 1))
                    tr(PT[1][:, 512 + ci * 128:512 + (ci + 1) * 128], c_v[:, off:off + 128],
                       id_bf[:], "PT1", ("cv", "id_bf"), inc=(ci == c.CH - 1))
                vop(lambda v, o=c_kft, p=PT[0], q=kv:
                        v.tensor_copy(o[:, q * c.TBW:(q + 1) * c.TBW], p[:, 512:512 + c.TBW]),
                    ("PT0",), ("ckft",))
                vop(lambda v, o=c_vt, p=PT[1], q=kv:
                        v.tensor_copy(o[:, q * c.TBW:(q + 1) * c.TBW], p[:, 512:512 + c.TBW]),
                    ("PT1",), ("cvt",))
            for h in range(c.H if "C2" not in phases else 0):
                sl = h % 2
                bank = sl
                gdma(wqo[sl][:, 0:c.NDT * 128], flat(ins["wq_pk"][h]), f"wqo{sl}")
                for dt in range(c.NDT):
                    mm(PB[bank][:, 0:c.TBW], wqo[sl][:, dt * 128:(dt + 1) * 128],
                       xb[:, dt * c.TBW:(dt + 1) * c.TBW], f"PB{bank}",
                       (f"wqo{sl}", "x"), start=(dt == 0), stop=(dt == c.NDT - 1),
                       inc=(dt == c.NDT - 1))
                rope_elu(PB[bank][:, 0:c.TBW], f"PB{bank}", cosb, sinb,
                         ("cosc", "sinmc"), c_qf[:, h * c.TBW:(h + 1) * c.TBW], "cqf",
                         sl, rot_bank=2 + sl)
            if tb == 0:
                emit_prefix_accum()
            GW = c.GQ * 128
            denr_sl = [rsA[0], rsA[1], act_s[0], act_s[1]]
            denr_tk = ["rs00", "rs01", "act0", "act1"]
            for ci in range(c.CH if "C3" not in phases else 0):
                for kv in range(c.KVH):
                    for hl in range(c.GQ):
                        h = kv * c.GQ + hl
                        mm(PB[kv][:, hl * 128:(hl + 1) * 128],
                           c_kf[:, kv * c.TBW + ci * 128:kv * c.TBW + (ci + 1) * 128],
                           c_qf[:, h * c.TBW + ci * 128:h * c.TBW + (ci + 1) * 128],
                           f"PB{kv}", ("ckf", "cqf"), inc=(hl == c.GQ - 1))
                for kv in range(c.KVH):
                    vop(lambda v, q=kv: v.tensor_mul(amst[:, q * GW:(q + 1) * GW],
                                                     PB[q][:, 0:GW], maskg_sb[:]),
                        (f"PB{kv}", "maskg", "sloc"), (f"amst{kv}",))
                for kv in range(c.KVH):
                    so = (tb * c.KVH + kv) * 128
                    for hl in range(c.GQ):
                        h = kv * c.GQ + hl
                        mm(PB[kv][0:1, hl * 128:(hl + 1) * 128], onescol[:],
                           amst[:, kv * GW + hl * 128:kv * GW + (hl + 1) * 128],
                           f"PB{kv}", ("onescol", f"amst{kv}"), start=True,
                           stop=False, inc=False)
                        mm(PB[kv][0:1, hl * 128:(hl + 1) * 128], Z_bf[:, so:so + 1],
                           c_qf[:, h * c.TBW + ci * 128:h * c.TBW + (ci + 1) * 128],
                           f"PB{kv}", ("Zbf", "cqf"), start=False, stop=True,
                           inc=(hl == c.GQ - 1))
                for kv in range(c.KVH):
                    s.op("scalar", lambda a, d=denr_sl[kv], q=kv: a.activation(
                            d[0:1, 0:GW], PB[q][0:1, 0:GW], AF.Ln,
                            bias=epsc[0:1, 0:1]),
                         reads=(f"PB{kv}", "epsc"), writes=(denr_tk[kv],))
                    s.op("scalar", lambda a, d=denr_sl[kv]: a.activation(
                            d[0:1, 0:GW], d[0:1, 0:GW], AF.Exp, scale=-1.0),
                         reads=(denr_tk[kv],), writes=(denr_tk[kv],))
                for pair in range(0, c.KVH, 2):
                    kvs = [k for k in (pair, pair + 1) if k < c.KVH]
                    for kv in kvs:
                        so = (tb * c.KVH + kv) * 128
                        nb = 4 + (kv % 2)
                        for hl in range(c.GQ):
                            h = kv * c.GQ + hl
                            mm(PB[nb][:, hl * 128:(hl + 1) * 128],
                               c_vt[:, kv * c.TBW + ci * 128:kv * c.TBW + (ci + 1) * 128],
                               amst[:, kv * GW + hl * 128:kv * GW + (hl + 1) * 128],
                               f"PB{nb}", ("cvt", f"amst{kv}"), start=True,
                               stop=False, inc=False)
                            mm(PB[nb][:, hl * 128:(hl + 1) * 128],
                               S_bf[:, so:so + 128],
                               c_qf[:, h * c.TBW + ci * 128:h * c.TBW + (ci + 1) * 128],
                               f"PB{nb}", ("Sbf", "cqf"), start=False, stop=True,
                               inc=(hl == c.GQ - 1))
                    for kv in kvs:
                        for hl in range(c.GQ):
                            mm(PB[kv][:, hl * 128:(hl + 1) * 128], ones_row_f[0:1, :],
                               denr_sl[kv][0:1, hl * 128:(hl + 1) * 128], f"PB{kv}",
                               ("onesrow", denr_tk[kv]), inc=(hl == c.GQ - 1))
                    for kv in kvs:
                        nb = 4 + (kv % 2)
                        vop(lambda v, q=kv: v.tensor_copy(rsB[q % 2][:, 0:GW],
                                                          PB[q][:, 0:GW]),
                            (f"PB{kv}",), (f"rs1{kv % 2}",))
                        ai3 = c_ai.rearrange("p (h t) -> p h t", h=c.H)[
                            :, kv * c.GQ:(kv + 1) * c.GQ, ci * 128:(ci + 1) * 128]
                        n3 = PB[nb][:, 0:GW].rearrange("p (h t) -> p h t", h=c.GQ)
                        d3 = rsB[kv % 2][:, 0:GW].rearrange("p (h t) -> p h t", h=c.GQ)
                        vop(lambda v, o=ai3, a=n3, b=d3: v.tensor_mul(o, a, b),
                            (f"PB{nb}", f"rs1{kv % 2}"), ("cai",))
                if ci < c.CH - 1:
                    for kv in range(c.KVH):
                        so = (tb * c.KVH + kv) * 128
                        nb = 4 + (kv % 2)
                        mm(PB[kv][:, 0:128],
                           c_kft[:, kv * c.TBW + ci * 128:kv * c.TBW + (ci + 1) * 128],
                           c_vt[:, kv * c.TBW + ci * 128:kv * c.TBW + (ci + 1) * 128],
                           f"PB{kv}", ("ckft", "cvt"))
                        mm(PB[nb][:, 0:128],
                           c_kft[:, kv * c.TBW + ci * 128:kv * c.TBW + (ci + 1) * 128],
                           ones128[:], f"PB{nb}", ("ckft", "ones128"))
                        vop(lambda v, q=so, b=kv: v.tensor_add(
                                S_bf[:, q:q + 128], S_bf[:, q:q + 128], PB[b][:, 0:128]),
                            (f"PB{kv}", "Sbf"), ("Sbf",))
                        vop(lambda v, q=so, b=nb: v.tensor_add(
                                Z_bf[:, q:q + 128], Z_bf[:, q:q + 128], PB[b][:, 0:128]),
                            (f"PB{nb}", "Zbf"), ("Zbf",))
            for dt in range(c.NDT if "C4" not in phases else 0):
                sl = dt % 2
                bank = 2 + sl
                gdma(wqo[sl][:, 0:c.H * 128], flat(ins["wo_pk"][dt]), f"wqo{sl}")
                for h in range(c.H):
                    mm(PB[bank][:, 0:c.TBW], wqo[sl][:, h * 128:(h + 1) * 128],
                       c_ai[:, h * c.TBW:(h + 1) * c.TBW], f"PB{bank}",
                       (f"wqo{sl}", "cai"), start=(h == 0), stop=(h == c.H - 1),
                       inc=(h == c.H - 1))
                xro = xr[:, tb * NT + dt * c.TBW:tb * NT + (dt + 1) * c.TBW]
                xio = x_bf[:, tb * NT + dt * c.TBW:tb * NT + (dt + 1) * c.TBW]
                vop(lambda v, o=xro, b=bank, i=xio:
                        v.tensor_add(o, PB[b][:, 0:c.TBW], i),
                    (f"PB{bank}", "x"), ("xr",))
            if "C5" not in phases:
                ln_stats_and_norm(tb, "xr", None, bk=(4, 5))

        def emit_gate():
            for tb in range(c.B):
                for dt in range(c.NDT):
                    mm(PB[3][0:c.E, 0:c.TBW], gate_sb[:, dt * c.E:(dt + 1) * c.E],
                       xr[:, tb * NT + dt * c.TBW:tb * NT + (dt + 1) * c.TBW], "PB3",
                       ("gate", "xr"), start=(dt == 0), stop=(dt == c.NDT - 1),
                       inc=(dt == c.NDT - 1))
                s.op("scalar", lambda a, t=tb: a.activation(
                        scor[0:c.E, t * c.TBW:(t + 1) * c.TBW], PB[3][0:c.E, 0:c.TBW],
                        AF.Sigmoid),
                     reads=("PB3",), writes=("scor",))
            for tc in range(c.B * c.CH):
                tr(PB[4][:, 0:c.E], scor[0:c.E, tc * 128:(tc + 1) * 128],
                   id_f[0:c.E, 0:c.E], "PB4", ("scor", "idf"))
                vop(lambda v: v.tensor_copy(sct[:], PB[4][:, 0:c.E]), ("PB4",), ("sct",))
                vop(lambda v: v.tensor_add(sbt[:], sct[:], biasb[:]), ("sct", "biasb"), ("sbt",))
                vop(lambda v: v.tensor_reduce(colv[:], sbt[:], AXX, ALU.max), ("sbt",), ("colv",))
                vop(lambda v: v.tensor_single_scalar(is1[:], sbt[:], colv[:, 0:1], ALU.is_ge),
                    ("sbt", "colv"), ("is1",))
                vop(lambda v: v.scalar_tensor_tensor(sbm[:], is1[:], -1e30, sbt[:],
                                                     ALU.mult, ALU.add),
                    ("is1", "sbt"), ("sbm",))
                vop(lambda v: v.tensor_reduce(colv2[:], sbm[:], AXX, ALU.max), ("sbm",), ("colv2",))
                vop(lambda v: v.tensor_single_scalar(m2t[:], sbt[:], colv2[:, 0:1], ALU.is_ge),
                    ("sbt", "colv2"), ("m2t",))
                vop(lambda v: v.tensor_mul(gtk[:], m2t[:], sct[:]), ("m2t", "sct"), ("gtk",))
                tr(PB[5][0:c.E, 0:128], gtk[:, 0:c.E], id_f[:], "PB5", ("gtk", "idf"))
                vop(lambda v, q=tc: v.tensor_copy(G_T[0:c.E, q * 128:(q + 1) * 128],
                                                  PB[5][0:c.E, 0:128]),
                    ("PB5",), ("GT",))

        s.barrier()

        HHT = c.NHT * c.B * c.TBW
        f_H = ar[:, 0:HHT]
        W1C = c.NDT * 128
        f_w1 = [ar[:, HHT + i * W1C:HHT + (i + 1) * W1C] for i in range(2)]
        f_w3 = [ar[:, HHT + 2 * W1C + i * W1C:HHT + 2 * W1C + (i + 1) * W1C] for i in range(2)]
        W2C = c.NHT * 128
        w2o = HHT + 4 * W1C
        f_w2 = [ar[:, w2o + i * W2C:w2o + (i + 1) * W2C] for i in range(2)]

        for ei in range(c.E + 1 if "F" in phases else 0):
            e = ei - 1
            routed = e >= 0
            if ei == 1 and "G" in phases:
                emit_gate()
            if routed:
                for tb in range(c.B):
                    mm(PB[5][:, 0:c.TBW], esel_sb[:, e * 128:(e + 1) * 128],
                       G_T[0:c.E, tb * c.TBW:(tb + 1) * c.TBW], "PB5",
                       ("esel", "GT"))
                    vop(lambda v, t=tb: v.tensor_copy(rsA[t][:, 0:c.TBW], PB[5][:, 0:c.TBW]),
                        ("PB5",), (f"rs0{tb}",))
            for ht in range(c.NHT):
                sl = ht % 2
                w1src = ins["w1_pk"][e, ht] if routed else ins["sw1_pk"][ht]
                w3src = ins["w3_pk"][e, ht] if routed else ins["sw3_pk"][ht]
                gdma(f_w1[sl], flat(w1src), f"w1{sl}")
                gdma(f_w3[sl], flat(w3src), f"w3{sl}")
                for tb in range(c.B):
                    bu = tb
                    bv = 2 + tb
                    x1b = xr[:, tb * NT:(tb + 1) * NT]
                    for dt in range(c.NDT):
                        mm(PB[bu][:, 0:c.TBW], f_w1[sl][:, dt * 128:(dt + 1) * 128],
                           x1b[:, dt * c.TBW:(dt + 1) * c.TBW], f"PB{bu}",
                           (f"w1{sl}", "xr"), start=(dt == 0), stop=(dt == c.NDT - 1),
                           inc=(dt == c.NDT - 1))
                    for dt in range(c.NDT):
                        mm(PB[bv][:, 0:c.TBW], f_w3[sl][:, dt * 128:(dt + 1) * 128],
                           x1b[:, dt * c.TBW:(dt + 1) * c.TBW], f"PB{bv}",
                           (f"w3{sl}", "xr"), start=(dt == 0), stop=(dt == c.NDT - 1),
                           inc=(dt == c.NDT - 1))
                    hsl = f_H[:, (ht * c.B + tb) * c.TBW:(ht * c.B + tb + 1) * c.TBW]
                    if routed:
                        vop(lambda v, o=act_s[tb], b=bu, t=tb:
                                v.tensor_mul(o[:], PB[b][:, 0:c.TBW], rsA[t][:, 0:c.TBW]),
                            (f"PB{bu}", f"rs0{tb}"), (f"act{tb}",))
                        s.op("scalar", lambda a, o=silu_s[tb], i=act_s[tb]:
                                 a.activation(o[:], i[:], AF.Silu),
                             reads=(f"act{tb}",), writes=(f"silu{tb}",))
                        vop(lambda v, o=sq_s[tb], b=bv, t=tb:
                                v.tensor_mul(o[:], PB[b][:, 0:c.TBW], rsA[t][:, 0:c.TBW]),
                            (f"PB{bv}", f"rs0{tb}"), (f"sq{tb}",))
                        vop(lambda v, o=hsl, a=silu_s[tb], b=sq_s[tb]:
                                v.tensor_mul(o, a[:], b[:]),
                            (f"silu{tb}", f"sq{tb}"), ("H",))
                    else:
                        s.op("scalar", lambda a, o=silu_s[tb], b=bu:
                                 a.activation(o[:], PB[b][:, 0:c.TBW], AF.Silu),
                             reads=(f"PB{bu}",), writes=(f"silu{tb}",))
                        vop(lambda v, o=sq_s[tb], b=bv:
                                v.tensor_copy(o[:], PB[b][:, 0:c.TBW]),
                            (f"PB{bv}",), (f"sq{tb}",))
                        vop(lambda v, o=hsl, a=silu_s[tb], b=sq_s[tb]:
                                v.tensor_mul(o, a[:], b[:]),
                            (f"silu{tb}", f"sq{tb}"), ("H",))
            def ws(tb_, dt_, sl_):
                ss = dt_ % 2
                vop(lambda v, o=rsA[ss], i=rsB[sl_]:
                        v.tensor_mul(o[:, 0:c.TBW], i[:, 0:c.TBW], rstd_b[:]),
                    (f"rs1{sl_}", "rstd"), (f"rs0{ss}",))
                sdma_out(out_dram[dt_ * 128:(dt_ + 1) * 128, tb_ * c.TBW:(tb_ + 1) * c.TBW],
                         rsA[ss][:, 0:c.TBW], f"rs0{ss}")

            last = ei == c.E and "L2" in phases
            for dt in range(c.NDT):
                sl = dt % 2
                w2src = ins["w2_pk"][e, dt] if routed else ins["sw2_pk"][dt]
                gdma(f_w2[sl], flat(w2src), f"w2{sl}")
                for tb in range(c.B):
                    bank = 4 + tb
                    for ht in range(c.NHT):
                        mm(PB[bank][:, 0:c.TBW], f_w2[sl][:, ht * 128:(ht + 1) * 128],
                           f_H[:, (ht * c.B + tb) * c.TBW:(ht * c.B + tb + 1) * c.TBW],
                           f"PB{bank}", (f"w2{sl}", "H"), start=(ht == 0),
                           stop=(ht == c.NHT - 1), inc=(ht == c.NHT - 1))
                    rsl = x_bf[:, tb * NT + dt * c.TBW:tb * NT + (dt + 1) * c.TBW]
                    if ei == 0:
                        vop(lambda v, o=rsl, b=bank: v.tensor_copy(o, PB[b][:, 0:c.TBW]),
                            (f"PB{bank}",), ("racc",))
                    else:
                        vop(lambda v, o=rsl, b=bank: v.tensor_add(o, o, PB[b][:, 0:c.TBW]),
                            (f"PB{bank}", "racc"), ("racc",))
                    if last:
                        off = tb * NT + dt * c.TBW
                        vop(lambda v, q=off: v.tensor_add(
                                xr[:, q:q + c.TBW], xr[:, q:q + c.TBW],
                                x_bf[:, q:q + c.TBW]),
                            ("xr", "racc"), ("xr",))
            if last:
                ln_stats_and_norm(0, "xr", ws)
                ln_stats_and_norm(1, "xr", ws)

        s.finish()

        block = ctx.enter_context(nc.Block())
        block.sync(lambda e: [f(e) for f in s.progs["sync"]])
        block.gpsimd(lambda e: [f(e) for f in s.progs["gpsimd"]])
        block.vector(lambda e: [f(e) for f in s.progs["vector"]])
        block.scalar(lambda e: [f(e) for f in s.progs["scalar"]])
        block.tensor(lambda e: [f(e) for f in s.progs["tensor"]])

    return nc


def _rope_tables(cfg):
    hd = cfg.HD
    inv = 1.0 / ROPE_BASE ** (np.arange(0, hd, 2, dtype=np.float64) / hd)
    t = np.arange(cfg.S, dtype=np.float64)
    fr = np.outer(t, inv)
    emb = np.concatenate([fr, fr], -1)
    cosT = np.cos(emb).T.astype(np.float32)
    sinT = np.sin(emb).T.astype(np.float32)
    sinmT = sinT.copy()
    sinmT[: hd // 2] *= -1.0
    sinm2T = np.roll(sinmT, -64, axis=0)
    return cosT, sinm2T


def _pack_weights(cfg, inp):
    c = cfg
    pk = {}
    Wq = inp["L0_attn_Wq"]
    pk["wq_pk"] = np.ascontiguousarray(
        Wq.reshape(c.H, 128, c.NDT, 128).transpose(0, 3, 2, 1))
    Wk = inp["L0_attn_Wk"]
    pk["wk_pk"] = np.ascontiguousarray(
        Wk.reshape(c.KVH, 128, c.NDT, 128).transpose(3, 2, 0, 1))
    Wv = inp["L0_attn_Wv"]
    pk["wv_pk"] = np.ascontiguousarray(
        Wv.reshape(c.KVH, 128, c.NDT, 128).transpose(3, 2, 0, 1))
    Wo = inp["L0_attn_Wo"]
    pk["wo_pk"] = np.ascontiguousarray(
        Wo.reshape(c.NDT, 128, c.H, 128).transpose(0, 3, 2, 1))
    gw = inp["L0_ffn_gate_w"]
    pk["gate_pk"] = np.ascontiguousarray(
        gw.reshape(c.E, c.NDT, 128).transpose(2, 1, 0))
    pk["bias_row"] = np.ascontiguousarray(
        inp["L0_ffn_expert_bias"].reshape(1, c.E))
    for nm, w in (("w1_pk", inp["L0_ffn_w1"]), ("w3_pk", inp["L0_ffn_w3"])):
        pk[nm] = np.ascontiguousarray(
            w.reshape(c.E, c.NHT, 128, c.NDT, 128).transpose(0, 1, 4, 3, 2))
    pk["w2_pk"] = np.ascontiguousarray(
        inp["L0_ffn_w2"].reshape(c.E, c.NDT, 128, c.NHT, 128).transpose(0, 1, 4, 3, 2))
    for nm, w in (("sw1_pk", inp["L0_ffn_sw1"]), ("sw3_pk", inp["L0_ffn_sw3"])):
        pk[nm] = np.ascontiguousarray(
            w.reshape(c.NHT, 128, c.NDT, 128).transpose(0, 3, 2, 1))
    pk["sw2_pk"] = np.ascontiguousarray(
        inp["L0_ffn_sw2"].reshape(c.NDT, 128, c.NHT, 128).transpose(0, 3, 2, 1))
    pk["id_f"] = np.eye(128, dtype=np.float32)
    r64 = np.zeros((128, 128), np.float32)
    for mm_ in range(128):
        r64[(mm_ + 64) % 128, mm_] = 1.0
    pk["r64"] = r64
    es = np.zeros((c.E, c.E * 128), np.float32)
    for e in range(c.E):
        es[e, e * 128:(e + 1) * 128] = 1.0
    pk["esel"] = es
    jj = np.arange(128)
    m1 = (jj[:, None] <= jj[None, :]).astype(np.float32)
    pk["maskg"] = np.ascontiguousarray(np.tile(m1, (1, c.GQ)))
    return pk


def _core_inputs(cfg, inp, pk, cosT, sinmT, core):
    c = cfg
    x = inp["L0_attn_x"]
    starts = [core * c.TBW, (c.NC - 1 - core) * c.TBW]
    d = {}

    def xpack(tb, t0):
        blk = x[tb, t0:t0 + c.TBW, :]
        return np.ascontiguousarray(
            blk.reshape(c.TBW, c.NDT, 128).transpose(2, 1, 0))

    d["xs"] = np.stack([xpack(tb, starts[tb]) for tb in range(c.B)])
    d["cosc"] = np.stack([np.ascontiguousarray(cosT[:, t0:t0 + c.TBW]) for t0 in starts])
    d["sinmc"] = np.stack([np.ascontiguousarray(sinmT[:, t0:t0 + c.TBW]) for t0 in starts])
    pm = np.zeros((2, c.NC), np.float32)
    pm[0, :core] = 1.0
    pm[1, core + 1:] = 1.0
    d["pmask"] = np.ascontiguousarray(
        np.broadcast_to(pm.reshape(1, -1), (128, 2 * c.NC)).astype(np.float32))
    d.update(pk)
    return d


def run(cfg, inp, sim=None, phases=("A", "C", "G", "F", "L2")):
    c = cfg
    nc = build_program(c, phases=phases)
    cosT, sinmT = _rope_tables(c)
    pk = _pack_weights(c, inp)
    in_maps = [_core_inputs(c, inp, pk, cosT, sinmT, core) for core in range(c.NC)]
    runner = sim or (lambda n, m: run_bass_kernel_spmd(n, m, core_ids=list(range(c.NC))))
    res = runner(nc, in_maps)
    out = np.empty((c.B, c.S, c.D), np.float32)
    for core in range(c.NC):
        o = res.results[core]["out"]
        out[0, core * c.TBW:(core + 1) * c.TBW, :] = o[:, 0:c.TBW].T
        s1 = (c.NC - 1 - core) * c.TBW
        out[1, s1:s1 + c.TBW, :] = o[:, c.TBW:].T
    return out


def kernel(**inputs):
    return run(FULL, inputs)


# version 16
# speedup vs baseline: 1.0005x; 1.0005x over previous
import contextlib
import numpy as np
import concourse.bass as bass
import concourse.mybir as mybir
from concourse.bass_utils import run_bass_kernel_spmd

F32 = mybir.dt.float32
BF16 = mybir.dt.bfloat16
AF = mybir.ActivationFunctionType
ALU = mybir.AluOpType
AXX = mybir.AxisListType.X

LN_EPS = 1e-5
ATT_EPS = 1e-6
ROPE_BASE = 10000.0


class Cfg:
    def __init__(self, NC=8, TBW=512, D=2048, H=16, KVH=4, HID=1408, E=8):
        self.NC = NC
        self.TBW = TBW
        self.D = D
        self.H = H
        self.KVH = KVH
        self.HD = 128
        self.HID = HID
        self.E = E
        self.B = 2
        self.NDT = D // 128
        self.NHT = HID // 128
        self.GQ = H // KVH
        self.CH = TBW // 128
        self.NPRE = NC - 1
        self.S = NC * TBW


FULL = Cfg()


class Sched:

    ENGS = ("sync", "gpsimd", "vector", "scalar", "tensor")

    def __init__(self, nc, ctx):
        self.nc = nc
        self.ctx = ctx
        self.sems = {e: ctx.enter_context(nc.semaphore(f"ch_{e}")) for e in self.ENGS}
        self.cnt = {e: 0 for e in self.ENGS}
        self.progs = {e: [] for e in self.ENGS}
        self.wm = {e: {} for e in self.ENGS}
        self.writer = {}
        self.readers = {}

    def _stream_sem(self, key):
        if key not in self.sems:
            self.sems[key] = self.ctx.enter_context(self.nc.semaphore(f"d_{key}"))
            self.cnt[key] = 0
        return self.sems[key]

    def _wait(self, eng, tgt, val):
        if val <= 0 or self.wm[eng].get(tgt, 0) >= val:
            return
        if tgt == eng and val > self.cnt.get(eng, 0):
            return
        assert val <= self.cnt.get(tgt, 0), (
            f"{eng} would wait on {tgt}>={val} before that inc is emitted "
            f"(cnt={self.cnt.get(tgt, 0)}): open accumulation group spans a "
            f"buffer refill -> deadlock")
        self.wm[eng][tgt] = val
        sem = self.sems[tgt]
        self.progs[eng].append(lambda e, s=sem, v=val: e.wait_ge(s, v))

    def _deps(self, eng, reads, writes):
        for t in reads:
            w = self.writer.get(t)
            if w is not None:
                self._wait(eng, w[0], w[1])
        for t in writes:
            w = self.writer.get(t)
            if w is not None:
                self._wait(eng, w[0], w[1])
            for (re, rv) in self.readers.get(t, ()):
                self._wait(eng, re, rv)

    def _record(self, stream, val, reads, writes):
        for t in reads:
            self.readers.setdefault(t, []).append((stream, val))
        for t in writes:
            self.writer[t] = (stream, val)
            self.readers[t] = []

    def op(self, eng, fn, reads=(), writes=(), inc=True):
        self._deps(eng, reads, writes)
        if inc:
            self.cnt[eng] += 1
            val = self.cnt[eng]
            sem = self.sems[eng]
            self.progs[eng].append(lambda e, f=fn, s=sem: f(e).then_inc(s, 1))
        else:
            val = self.cnt[eng] + 1
            self.progs[eng].append(lambda e, f=fn: f(e))
        self._record(eng, val, reads, writes)

    def dma(self, eng, fn, key, reads=(), writes=()):
        self._deps(eng, reads, writes)
        sem = self._stream_sem(key)
        self.cnt[key] += 16
        val = self.cnt[key]
        self.progs[eng].append(lambda e, f=fn, s=sem: f(e).then_inc(s, 16))
        self._record(key, val, reads, writes)

    def counts(self):
        return dict(self.cnt)

    def wait_counts(self, eng, snap):
        for tgt, v in snap.items():
            if tgt != eng:
                self._wait(eng, tgt, v)

    def barrier(self, engines=None):
        engines = engines or self.ENGS
        snap = self.counts()
        for e in engines:
            self.wait_counts(e, snap)

    def finish(self):
        self.wait_counts("sync", self.counts())
        v = self.cnt["sync"]
        if v:
            sem = self.sems["sync"]
            self.progs["sync"].append(lambda e, s=sem, vv=v: e.wait_ge(s, vv))


def build_program(cfg, phases=("A", "C", "G", "F", "L2")):
    c = cfg
    nc = bass.Bass("TRN2", target_bir_lowering=False)

    dr = lambda name, shp: nc.dram_tensor(name, list(shp), F32, kind="ExternalInput").ap()
    ins = {}
    ins["xs"] = dr("xs", (c.B, 128, c.NDT, c.TBW))
    ins["cosc"] = dr("cosc", (c.B, 128, c.TBW))
    ins["sinmc"] = dr("sinmc", (c.B, 128, c.TBW))
    ins["pmask"] = dr("pmask", (128, 2 * c.NC))
    ins["wq_pk"] = dr("wq_pk", (c.H, 128, c.NDT, 128))
    ins["wk_pk"] = dr("wk_pk", (128, c.NDT, c.KVH, 128))
    ins["wv_pk"] = dr("wv_pk", (128, c.NDT, c.KVH, 128))
    ins["wo_pk"] = dr("wo_pk", (c.NDT, 128, c.H, 128))
    ins["gate_pk"] = dr("gate_pk", (128, c.NDT, c.E))
    ins["bias_row"] = dr("bias_row", (1, c.E))
    ins["w1_pk"] = dr("w1_pk", (c.E, c.NHT, 128, c.NDT, 128))
    ins["w3_pk"] = dr("w3_pk", (c.E, c.NHT, 128, c.NDT, 128))
    ins["w2_pk"] = dr("w2_pk", (c.E, c.NDT, 128, c.NHT, 128))
    ins["sw1_pk"] = dr("sw1_pk", (c.NHT, 128, c.NDT, 128))
    ins["sw3_pk"] = dr("sw3_pk", (c.NHT, 128, c.NDT, 128))
    ins["sw2_pk"] = dr("sw2_pk", (c.NDT, 128, c.NHT, 128))
    ins["id_f"] = dr("id_f", (128, 128))
    ins["r64"] = dr("r64", (128, 128))
    ins["esel"] = dr("esel", (c.E, c.E * 128))
    ins["maskg"] = dr("maskg", (128, c.GQ * 128))

    out_dram = nc.dram_tensor("out", [c.D, c.B * c.TBW], F32, kind="ExternalOutput").ap()
    SZC = 2 * c.KVH * 128
    cin = [nc.dram_tensor(f"cin{t}", [128, SZC], BF16, kind="Internal").ap()
           for t in range(c.B)]
    cout = [nc.dram_tensor(f"cout{t}", [c.NC * 128, SZC], BF16, kind="Internal",
                           addr_space="Shared" if c.NC > 4 else "Local").ap()
            for t in range(c.B)]

    with contextlib.ExitStack() as ctx:
        sbuf = lambda n, cols, dt: ctx.enter_context(nc.sbuf_tensor(n, [128, cols], dt))
        psum = lambda n, cols, dt: ctx.enter_context(nc.psum_tensor(n, [128, cols], dt))

        NT = c.NDT * c.TBW
        AR = max(2 * NT + 4 * c.TBW,
                 2 * c.H * c.TBW + 4 * c.KVH * c.TBW,
                 c.NHT * c.B * c.TBW + 4 * c.NDT * 128 + 2 * c.NHT * 128)

        x_bf = sbuf("x_bf", c.B * NT, BF16)
        xr = sbuf("xr", c.B * NT, BF16)
        ar = sbuf("ar", AR, BF16)
        wk_sb = sbuf("wk_sb", c.NDT * c.KVH * 128, BF16)
        wv_sb = sbuf("wv_sb", c.NDT * c.KVH * 128, BF16)
        wqo = [sbuf(f"wqo{i}", max(c.NDT, c.H) * 128, BF16) for i in range(2)]
        gate_sb = sbuf("gate_sb", c.NDT * c.E, BF16)
        cosc_sb = sbuf("cosc_sb", c.B * c.TBW, BF16)
        sinmc_sb = sbuf("sinmc_sb", c.B * c.TBW, BF16)
        rb = [sbuf("rb0", 2 * c.B * c.KVH * 128, BF16)]
        pmask_sb = sbuf("pmask_sb", 2 * c.NC, F32)
        SCR = max(c.TBW, c.GQ * 128)
        rsA = [sbuf(f"rsA{i}", SCR, F32) for i in range(2)]
        rsB = [sbuf(f"rsB{i}", SCR, F32) for i in range(2)]
        act_s = [sbuf(f"act{i}", c.TBW, F32) for i in range(2)]
        silu_s = [sbuf(f"silu{i}", c.TBW, BF16) for i in range(2)]
        sq_s = [sbuf(f"sq{i}", c.TBW, BF16) for i in range(2)]
        S_bf = sbuf("S_bf", c.B * c.KVH * 128, BF16)
        Z_bf = sbuf("Z_bf", c.B * c.KVH * 128, BF16)
        m_b = sbuf("m_b", c.TBW, F32)
        rstd_b = sbuf("rstd_b", c.TBW, F32)
        G_T = ctx.enter_context(nc.sbuf_tensor("G_T", [c.E, c.B * c.TBW], F32))
        scor = ctx.enter_context(nc.sbuf_tensor("scor", [c.E, c.B * c.TBW], F32))
        amst = sbuf("amst", max(c.KVH * c.GQ * 128, 2 * c.B * c.KVH * 128), BF16)
        sloc = amst
        a2 = [silu_s[0], silu_s[1], sq_s[0], sq_s[1]]
        sct = sbuf("sct", c.E, F32)
        sbt = sbuf("sbt", c.E, F32)
        is1 = sbuf("is1", c.E, F32)
        sbm = sbuf("sbm", c.E, F32)
        m2t = sbuf("m2t", c.E, F32)
        gtk = sbuf("gtk", c.E, F32)
        colv = sbuf("colv", 1, F32)
        colv2 = sbuf("colv2", 1, F32)
        id_f = sbuf("id_f_sb", 128, F32)
        id_bf = sbuf("id_bf", 128, BF16)
        r64_bf = sbuf("r64_bf", 128, BF16)
        maskg_sb = sbuf("maskg_sb", c.GQ * 128, F32)
        ones128 = sbuf("ones128", 128, BF16)
        onescol = sbuf("onescol", 1, BF16)
        ones_row_f = ctx.enter_context(nc.sbuf_tensor("ones_row_f", [1, 128], F32))
        epsc = sbuf("epsc", 1, F32)
        biasb = sbuf("biasb", c.E, F32)
        bias_row = ctx.enter_context(nc.sbuf_tensor("bias_row_sb", [1, c.E], F32))
        esel_sb = ctx.enter_context(nc.sbuf_tensor("esel_sb", [c.E, c.E * 128], F32))

        PB = [psum(f"PB_{i}", c.TBW, F32) for i in range(6)]
        PT = [ctx.enter_context(nc.psum_tensor(f"PT_{i}", [128, 1024], BF16))
              for i in range(2)]

        s = Sched(nc, ctx)

        def mm(out_ap, lhs_ap, rhs_ap, bank, reads, start=True, stop=True, inc=True):
            s.op("tensor",
                 lambda t, o=out_ap, l=lhs_ap, r=rhs_ap, a=start, b=stop:
                     t.matmul(o, l, r, start=a, stop=b),
                 reads=reads, writes=(bank,), inc=inc)

        def tr(out_ap, in_ap, ident_ap, bank, reads, inc=True):
            s.op("tensor",
                 lambda t, o=out_ap, i=in_ap, d=ident_ap: t.transpose(o, i, d),
                 reads=reads, writes=(bank,), inc=inc)

        def vop(fn, reads, writes, eng="vector", inc=True):
            s.op(eng, fn, reads=reads, writes=writes, inc=inc)

        def gdma(out_ap, in_ap, wtok):
            s.dma("gpsimd", lambda g, o=out_ap, i=in_ap: g.dma_start(out=o, in_=i),
                  key="L" + wtok, writes=(wtok,))

        def sdma_in(out_ap, in_ap, wtok):
            s.dma("sync", lambda g, o=out_ap, i=in_ap: g.dma_start(out=o, in_=i),
                  key="L" + wtok, writes=(wtok,))

        def sdma_out(out_ap, in_ap, rtok):
            s.dma("sync", lambda g, o=out_ap, i=in_ap: g.dma_start(out=o, in_=i),
                  key="S" + rtok, reads=(rtok,))

        flat = lambda ap: ap.rearrange("p a b -> p (a b)")

        sdma_in(id_f[:], ins["id_f"][:, :], "idf")
        sdma_in(maskg_sb[:], ins["maskg"][:, :], "maskg")
        sdma_in(pmask_sb[:], ins["pmask"][:, :], "pmask")
        sdma_in(bias_row[:], ins["bias_row"][:, :], "bias_row")
        sdma_in(esel_sb[:], ins["esel"][:, :], "esel")
        gdma(id_bf[:], ins["id_f"][:, :], "id_bf")
        gdma(r64_bf[:], ins["r64"][:, :], "r64")
        gdma(wk_sb[:], ins["wk_pk"].rearrange("p a b c -> p (a b c)"), "wk")
        gdma(wv_sb[:], ins["wv_pk"].rearrange("p a b c -> p (a b c)"), "wv")

        def emit_own_loads():
            for tb in range(c.B):
                gdma(cosc_sb[:, tb * c.TBW:(tb + 1) * c.TBW], ins["cosc"][tb], "cosc")
                gdma(sinmc_sb[:, tb * c.TBW:(tb + 1) * c.TBW], ins["sinmc"][tb], "sinmc")
            for tb in range(c.B):
                gdma(x_bf[:, tb * NT:(tb + 1) * NT], flat(ins["xs"][tb]), "x")
            gdma(gate_sb[:], flat(ins["gate_pk"]), "gate")
        vop(lambda v: v.memset(ones128[:], 1.0), (), ("ones128",))
        vop(lambda v: v.memset(onescol[:], 1.0), (), ("onescol",))
        vop(lambda v: v.memset(ones_row_f[:], 1.0), (), ("onesrow",))
        vop(lambda v: v.memset(epsc[:], ATT_EPS), (), ("epsc",))
        vop(lambda v: v.memset(S_bf[:], 0.0), (), ("Sbf",))
        vop(lambda v: v.memset(Z_bf[:], 0.0), (), ("Zbf",))
        mm(PB[5][:, 0:c.E], ones_row_f[0:1, :], bias_row[0:1, :], "PB5",
           ("onesrow", "bias_row"))
        vop(lambda v: v.tensor_copy(biasb[:], PB[5][:, 0:c.E]), ("PB5",), ("biasb",))

        def rope_elu(pb_ap, bank_tok, cos_ap, sin_ap, costoks, dst_ap, dsttok, pipe,
                     rot_bank=2):
            t1, qr, acts = (rsA if pipe == 0 else rsB)[0], \
                           (rsA if pipe == 0 else rsB)[1], act_s[pipe]
            tok1, tokq, toka = (f"rs{pipe}0", f"rs{pipe}1", f"act{pipe}")
            W = dst_ap.shape[-1]
            vop(lambda v, o=t1, p=pb_ap, sn=sin_ap:
                    v.tensor_mul(o[0:64, 0:W], p[64:128, :], sn[64:128, :]),
                (bank_tok,) + costoks, (tok1,), inc=False)
            vop(lambda v, o=t1, p=pb_ap, sn=sin_ap:
                    v.tensor_mul(o[64:128, 0:W], p[0:64, :], sn[0:64, :]),
                (bank_tok,) + costoks, (tok1,))
            vop(lambda v, o=qr, p=pb_ap, cs=cos_ap:
                    v.tensor_mul(o[:, 0:W], p[:], cs[:]),
                (bank_tok,) + costoks, (tokq,))
            vop(lambda v, o=qr, i=t1: v.tensor_add(o[:, 0:W], o[:, 0:W], i[:, 0:W]),
                (tok1, tokq), (tokq,))
            s.op("scalar", lambda a, o=acts, i=qr:
                     a.activation(o[:, 0:W], i[:, 0:W], AF.Relu, scale=-1.0),
                 reads=(tokq,), writes=(toka,))
            s.op("scalar", lambda a, o=acts:
                     a.activation(o[:, 0:W], o[:, 0:W], AF.Exp, scale=-1.0),
                 reads=(toka,), writes=(toka,))
            vop(lambda v, o=dst_ap, i=qr, e=acts:
                    v.scalar_tensor_tensor(o, i[:, 0:W], 0.0, e[:, 0:W], ALU.max, ALU.add),
                (tokq, toka), (dsttok,))

        HT = c.H * c.TBW
        KT = c.KVH * c.TBW
        c_qf = ar[:, 0:HT]
        c_ai = ar[:, HT:2 * HT]
        c_kf = ar[:, 2 * HT:2 * HT + KT]
        c_v = ar[:, 2 * HT + KT:2 * HT + 2 * KT]
        c_kft = ar[:, 2 * HT + 2 * KT:2 * HT + 3 * KT]
        c_vt = ar[:, 2 * HT + 3 * KT:2 * HT + 4 * KT]
        SZH = c.B * c.KVH * 128

        emit_own_loads()

        def emit_kv_block(tb, kv, dsts, toks):
            xb_ = x_bf[:, tb * NT:(tb + 1) * NT]
            pk_, pv_ = 2 * (kv % 2), 2 * (kv % 2) + 1
            pipe = kv % 2
            for dt in range(c.NDT):
                mm(PB[pk_][:, 0:c.TBW], wk_sb[:, (dt * c.KVH + kv) * 128:(dt * c.KVH + kv + 1) * 128],
                   xb_[:, dt * c.TBW:(dt + 1) * c.TBW], f"PB{pk_}", ("wk", "x"),
                   start=(dt == 0), stop=(dt == c.NDT - 1), inc=(dt == c.NDT - 1))
            for dt in range(c.NDT):
                mm(PB[pv_][:, 0:c.TBW], wv_sb[:, (dt * c.KVH + kv) * 128:(dt * c.KVH + kv + 1) * 128],
                   xb_[:, dt * c.TBW:(dt + 1) * c.TBW], f"PB{pv_}", ("wv", "x"),
                   start=(dt == 0), stop=(dt == c.NDT - 1), inc=(dt == c.NDT - 1))
            cosb_ = cosc_sb[:, tb * c.TBW:(tb + 1) * c.TBW]
            sinb_ = sinmc_sb[:, tb * c.TBW:(tb + 1) * c.TBW]
            rope_elu(PB[pk_][:, 0:c.TBW], f"PB{pk_}", cosb_, sinb_,
                     ("cosc", "sinmc"), dsts[0], toks[0], pipe)
            vop(lambda v, o=dsts[1], p=PB[pv_]: v.tensor_copy(o, p[:, 0:c.TBW]),
                (f"PB{pv_}",), (toks[1],))
            for ci in range(c.CH):
                tr(PT[0][:, 512 + ci * 128:512 + (ci + 1) * 128],
                   dsts[0][:, ci * 128:(ci + 1) * 128], id_bf[:], "PT0",
                   (toks[0], "id_bf"), inc=(ci == c.CH - 1))
                tr(PT[1][:, 512 + ci * 128:512 + (ci + 1) * 128],
                   dsts[1][:, ci * 128:(ci + 1) * 128], id_bf[:], "PT1",
                   (toks[1], "id_bf"), inc=(ci == c.CH - 1))
            vop(lambda v, o=dsts[2], p=PT[0]: v.tensor_copy(o, p[:, 512:512 + c.TBW]),
                ("PT0",), (toks[2],))
            vop(lambda v, o=dsts[3], p=PT[1]: v.tensor_copy(o, p[:, 512:512 + c.TBW]),
                ("PT1",), (toks[3],))
            so = tb * SZC + kv * 128
            zo = so + c.KVH * 128
            for ci in range(c.CH):
                mm(PB[4][:, 0:128], dsts[2][:, ci * 128:(ci + 1) * 128],
                   dsts[3][:, ci * 128:(ci + 1) * 128], "PB4", (toks[2], toks[3]),
                   start=(ci == 0), stop=(ci == c.CH - 1), inc=(ci == c.CH - 1))
            vop(lambda v, q=so: v.tensor_copy(sloc[:, q:q + 128], PB[4][:, 0:128]),
                ("PB4",), ("sloc",))
            for ci in range(c.CH):
                mm(PB[5][:, 0:128], dsts[2][:, ci * 128:(ci + 1) * 128],
                   ones128[:], "PB5", (toks[2], "ones128"),
                   start=(ci == 0), stop=(ci == c.CH - 1), inc=(ci == c.CH - 1))
            vop(lambda v, q=zo: v.tensor_copy(sloc[:, q:q + 128],
                                              PB[5][:, 0:128]),
                ("PB5",), ("sloc",))

        def emit_exchange(tb):
            off = tb * SZC
            s.dma("sync", lambda g, t=tb, q=off: g.dma_start(
                      out=cin[t][:, :], in_=sloc[:, q:q + SZC]),
                  key=f"Lcin{tb}", reads=("sloc",), writes=(f"cin_d{tb}",))
            s.op("gpsimd", lambda g, t=tb: g.collective_compute(
                    "AllGather", ALU.bypass, ins=[cin[t][:, :]], outs=[cout[t][:, :]],
                    replica_groups=[list(range(c.NC))]),
                 reads=(f"cin_d{tb}",), writes=(f"cout_d{tb}",))

        for kv in range(c.KVH):
            emit_kv_block(0, kv, (c_kf[:, kv * c.TBW:(kv + 1) * c.TBW],
                                  c_v[:, kv * c.TBW:(kv + 1) * c.TBW],
                                  c_kft[:, kv * c.TBW:(kv + 1) * c.TBW],
                                  c_vt[:, kv * c.TBW:(kv + 1) * c.TBW]),
                          ("ckf", "cv", "ckft", "cvt"))
        emit_exchange(0)
        for kv in range(c.KVH):
            emit_kv_block(1, kv, (a2[0][:, 0:c.TBW], a2[1][:, 0:c.TBW],
                                  a2[2][:, 0:c.TBW], a2[3][:, 0:c.TBW]),
                          ("silu0", "silu1", "sq0", "sq1"))

        def emit_prefix_accum(tb):
            KW = c.KVH * 128
            for n in range(c.NC):
                s.dma("sync", lambda g, o=rb[0][:, 0:SZC],
                          i=cout[tb][n * 128:(n + 1) * 128, :]:
                          g.dma_start(out=o, in_=i),
                      key="Lrb0", reads=(f"cout_d{tb}",), writes=("rb0",))
                pm = pmask_sb[:, tb * c.NC + n:tb * c.NC + n + 1]
                q = tb * KW
                vop(lambda v, p=pm, qq=q:
                        v.scalar_tensor_tensor(S_bf[:, qq:qq + KW], rb[0][:, 0:KW],
                                               p, S_bf[:, qq:qq + KW],
                                               ALU.mult, ALU.add),
                    ("rb0", "pmask", "Sbf"), ("Sbf",))
                vop(lambda v, p=pm, qq=q:
                        v.scalar_tensor_tensor(Z_bf[:, qq:qq + KW],
                                               rb[0][:, KW:2 * KW],
                                               p, Z_bf[:, qq:qq + KW],
                                               ALU.mult, ALU.add),
                    ("rb0", "pmask", "Zbf"), ("Zbf",))

        def ln_stats_and_norm(tb, src_tok, write_stage, bk=(0, 1)):
            xrb = xr[:, tb * NT:(tb + 1) * NT]
            for dt in range(c.NDT):
                sl = dt % 2
                s.op("scalar", lambda a, o=sq_s[sl], i=xrb, d=dt:
                         a.activation(o[:], i[:, d * c.TBW:(d + 1) * c.TBW], AF.Square),
                     reads=(src_tok,), writes=(f"sq{sl}",))
                mm(PB[bk[0]][:, 0:c.TBW], ones128[:], xrb[:, dt * c.TBW:(dt + 1) * c.TBW],
                   f"PB{bk[0]}", ("ones128", src_tok), start=(dt == 0), stop=(dt == c.NDT - 1))
                mm(PB[bk[1]][:, 0:c.TBW], ones128[:], sq_s[sl][:], f"PB{bk[1]}",
                   ("ones128", f"sq{sl}"), start=(dt == 0), stop=(dt == c.NDT - 1))
            inv_d = 1.0 / c.D
            vop(lambda v: v.tensor_scalar_mul(m_b[:], PB[bk[0]][:, 0:c.TBW], inv_d),
                (f"PB{bk[0]}",), ("m",))
            vop(lambda v: v.tensor_mul(rsA[0][:, 0:c.TBW], m_b[:], m_b[:]), ("m",), ("rs00",))
            vop(lambda v: v.scalar_tensor_tensor(rsA[1][:, 0:c.TBW], PB[bk[1]][:, 0:c.TBW], inv_d,
                                                 rsA[0][:, 0:c.TBW], ALU.mult, ALU.subtract),
                (f"PB{bk[1]}", "rs00"), ("rs01",))
            vop(lambda v: v.tensor_scalar_add(rsA[1][:, 0:c.TBW], rsA[1][:, 0:c.TBW], LN_EPS),
                ("rs01",), ("rs01",))
            vop(lambda v: v.reciprocal(rsA[1][:, 0:c.TBW], rsA[1][:, 0:c.TBW]), ("rs01",), ("rs01",))
            s.op("scalar", lambda a: a.activation(rstd_b[:], rsA[1][:, 0:c.TBW], AF.Sqrt),
                 reads=("rs01",), writes=("rstd",))
            for dt in range(c.NDT):
                sl = dt % 2
                xt = xrb[:, dt * c.TBW:(dt + 1) * c.TBW]
                vop(lambda v, o=rsB[sl], i=xt: v.tensor_sub(o[:, 0:c.TBW], i, m_b[:]),
                    (src_tok, "m"), (f"rs1{sl}",))
                if write_stage is None:
                    vop(lambda v, o=xt, i=rsB[sl]: v.tensor_mul(o, i[:, 0:c.TBW], rstd_b[:]),
                        (f"rs1{sl}", "rstd", src_tok), (src_tok,))
                else:
                    write_stage(tb, dt, sl)

        for tb in range(c.B if "C" in phases else 0):
            xb = x_bf[:, tb * NT:(tb + 1) * NT]
            cosb = cosc_sb[:, tb * c.TBW:(tb + 1) * c.TBW]
            sinb = sinmc_sb[:, tb * c.TBW:(tb + 1) * c.TBW]
            for kv in range(c.KVH if (tb > 0 and "C1" not in phases) else 0):
                for dt in range(c.NDT):
                    mm(PB[0][:, 0:c.TBW], wk_sb[:, (dt * c.KVH + kv) * 128:(dt * c.KVH + kv + 1) * 128],
                       xb[:, dt * c.TBW:(dt + 1) * c.TBW], "PB0", ("wk", "x"),
                       start=(dt == 0), stop=(dt == c.NDT - 1), inc=(dt == c.NDT - 1))
                for dt in range(c.NDT):
                    mm(PB[1][:, 0:c.TBW], wv_sb[:, (dt * c.KVH + kv) * 128:(dt * c.KVH + kv + 1) * 128],
                       xb[:, dt * c.TBW:(dt + 1) * c.TBW], "PB1", ("wv", "x"),
                       start=(dt == 0), stop=(dt == c.NDT - 1), inc=(dt == c.NDT - 1))
                kfs = c_kf[:, kv * c.TBW:(kv + 1) * c.TBW]
                rope_elu(PB[0][:, 0:c.TBW], "PB0", cosb, sinb, ("cosc", "sinmc"),
                         kfs, "ckf", kv % 2, rot_bank=2 + kv % 2)
                vop(lambda v, o=c_v, p=PB[1], q=kv:
                        v.tensor_copy(o[:, q * c.TBW:(q + 1) * c.TBW], p[:, 0:c.TBW]),
                    ("PB1",), ("cv",))
                for ci in range(c.CH):
                    off = kv * c.TBW + ci * 128
                    tr(PT[0][:, 512 + ci * 128:512 + (ci + 1) * 128], c_kf[:, off:off + 128],
                       id_bf[:], "PT0", ("ckf", "id_bf"), inc=(ci == c.CH - 1))
                    tr(PT[1][:, 512 + ci * 128:512 + (ci + 1) * 128], c_v[:, off:off + 128],
                       id_bf[:], "PT1", ("cv", "id_bf"), inc=(ci == c.CH - 1))
                vop(lambda v, o=c_kft, p=PT[0], q=kv:
                        v.tensor_copy(o[:, q * c.TBW:(q + 1) * c.TBW], p[:, 512:512 + c.TBW]),
                    ("PT0",), ("ckft",))
                vop(lambda v, o=c_vt, p=PT[1], q=kv:
                        v.tensor_copy(o[:, q * c.TBW:(q + 1) * c.TBW], p[:, 512:512 + c.TBW]),
                    ("PT1",), ("cvt",))
            for h in range(c.H if "C2" not in phases else 0):
                sl = h % 2
                bank = sl
                gdma(wqo[sl][:, 0:c.NDT * 128], flat(ins["wq_pk"][h]), f"wqo{sl}")
                for dt in range(c.NDT):
                    mm(PB[bank][:, 0:c.TBW], wqo[sl][:, dt * 128:(dt + 1) * 128],
                       xb[:, dt * c.TBW:(dt + 1) * c.TBW], f"PB{bank}",
                       (f"wqo{sl}", "x"), start=(dt == 0), stop=(dt == c.NDT - 1),
                       inc=(dt == c.NDT - 1))
                rope_elu(PB[bank][:, 0:c.TBW], f"PB{bank}", cosb, sinb,
                         ("cosc", "sinmc"), c_qf[:, h * c.TBW:(h + 1) * c.TBW], "cqf",
                         sl, rot_bank=2 + sl)
            if tb == 0:
                emit_exchange(1)
            emit_prefix_accum(tb)
            GW = c.GQ * 128
            denr_sl = [rsA[0], rsA[1], act_s[0], act_s[1]]
            denr_tk = ["rs00", "rs01", "act0", "act1"]
            for ci in range(c.CH if "C3" not in phases else 0):
                for kv in range(c.KVH):
                    for hl in range(c.GQ):
                        h = kv * c.GQ + hl
                        mm(PB[kv][:, hl * 128:(hl + 1) * 128],
                           c_kf[:, kv * c.TBW + ci * 128:kv * c.TBW + (ci + 1) * 128],
                           c_qf[:, h * c.TBW + ci * 128:h * c.TBW + (ci + 1) * 128],
                           f"PB{kv}", ("ckf", "cqf"), inc=(hl == c.GQ - 1))
                for kv in range(c.KVH):
                    vop(lambda v, q=kv: v.tensor_mul(amst[:, q * GW:(q + 1) * GW],
                                                     PB[q][:, 0:GW], maskg_sb[:]),
                        (f"PB{kv}", "maskg"), (f"amst{kv}", "sloc"))
                for kv in range(c.KVH):
                    so = (tb * c.KVH + kv) * 128
                    for hl in range(c.GQ):
                        h = kv * c.GQ + hl
                        mm(PB[kv][0:1, hl * 128:(hl + 1) * 128], onescol[:],
                           amst[:, kv * GW + hl * 128:kv * GW + (hl + 1) * 128],
                           f"PB{kv}", ("onescol", f"amst{kv}"), start=True,
                           stop=False, inc=False)
                        mm(PB[kv][0:1, hl * 128:(hl + 1) * 128], Z_bf[:, so:so + 1],
                           c_qf[:, h * c.TBW + ci * 128:h * c.TBW + (ci + 1) * 128],
                           f"PB{kv}", ("Zbf", "cqf"), start=False, stop=True,
                           inc=(hl == c.GQ - 1))
                for kv in range(c.KVH):
                    s.op("scalar", lambda a, d=denr_sl[kv], q=kv: a.activation(
                            d[0:1, 0:GW], PB[q][0:1, 0:GW], AF.Ln,
                            bias=epsc[0:1, 0:1]),
                         reads=(f"PB{kv}", "epsc"), writes=(denr_tk[kv],))
                    s.op("scalar", lambda a, d=denr_sl[kv]: a.activation(
                            d[0:1, 0:GW], d[0:1, 0:GW], AF.Exp, scale=-1.0),
                         reads=(denr_tk[kv],), writes=(denr_tk[kv],))
                for pair in range(0, c.KVH, 2):
                    kvs = [k for k in (pair, pair + 1) if k < c.KVH]
                    for kv in kvs:
                        so = (tb * c.KVH + kv) * 128
                        nb = 4 + (kv % 2)
                        for hl in range(c.GQ):
                            h = kv * c.GQ + hl
                            mm(PB[nb][:, hl * 128:(hl + 1) * 128],
                               c_vt[:, kv * c.TBW + ci * 128:kv * c.TBW + (ci + 1) * 128],
                               amst[:, kv * GW + hl * 128:kv * GW + (hl + 1) * 128],
                               f"PB{nb}", ("cvt", f"amst{kv}"), start=True,
                               stop=False, inc=False)
                            mm(PB[nb][:, hl * 128:(hl + 1) * 128],
                               S_bf[:, so:so + 128],
                               c_qf[:, h * c.TBW + ci * 128:h * c.TBW + (ci + 1) * 128],
                               f"PB{nb}", ("Sbf", "cqf"), start=False, stop=True,
                               inc=(hl == c.GQ - 1))
                    for kv in kvs:
                        for hl in range(c.GQ):
                            mm(PB[kv][:, hl * 128:(hl + 1) * 128], ones_row_f[0:1, :],
                               denr_sl[kv][0:1, hl * 128:(hl + 1) * 128], f"PB{kv}",
                               ("onesrow", denr_tk[kv]), inc=(hl == c.GQ - 1))
                    for kv in kvs:
                        nb = 4 + (kv % 2)
                        vop(lambda v, q=kv: v.tensor_copy(rsB[q % 2][:, 0:GW],
                                                          PB[q][:, 0:GW]),
                            (f"PB{kv}",), (f"rs1{kv % 2}",))
                        ai3 = c_ai.rearrange("p (h t) -> p h t", h=c.H)[
                            :, kv * c.GQ:(kv + 1) * c.GQ, ci * 128:(ci + 1) * 128]
                        n3 = PB[nb][:, 0:GW].rearrange("p (h t) -> p h t", h=c.GQ)
                        d3 = rsB[kv % 2][:, 0:GW].rearrange("p (h t) -> p h t", h=c.GQ)
                        vop(lambda v, o=ai3, a=n3, b=d3: v.tensor_mul(o, a, b),
                            (f"PB{nb}", f"rs1{kv % 2}"), ("cai",))
                if ci < c.CH - 1:
                    for kv in range(c.KVH):
                        so = (tb * c.KVH + kv) * 128
                        nb = 4 + (kv % 2)
                        mm(PB[kv][:, 0:128],
                           c_kft[:, kv * c.TBW + ci * 128:kv * c.TBW + (ci + 1) * 128],
                           c_vt[:, kv * c.TBW + ci * 128:kv * c.TBW + (ci + 1) * 128],
                           f"PB{kv}", ("ckft", "cvt"))
                        mm(PB[nb][:, 0:128],
                           c_kft[:, kv * c.TBW + ci * 128:kv * c.TBW + (ci + 1) * 128],
                           ones128[:], f"PB{nb}", ("ckft", "ones128"))
                        vop(lambda v, q=so, b=kv: v.tensor_add(
                                S_bf[:, q:q + 128], S_bf[:, q:q + 128], PB[b][:, 0:128]),
                            (f"PB{kv}", "Sbf"), ("Sbf",))
                        vop(lambda v, q=so, b=nb: v.tensor_add(
                                Z_bf[:, q:q + 128], Z_bf[:, q:q + 128], PB[b][:, 0:128]),
                            (f"PB{nb}", "Zbf"), ("Zbf",))
            for dt in range(c.NDT if "C4" not in phases else 0):
                sl = dt % 2
                bank = 2 + sl
                gdma(wqo[sl][:, 0:c.H * 128], flat(ins["wo_pk"][dt]), f"wqo{sl}")
                for h in range(c.H):
                    mm(PB[bank][:, 0:c.TBW], wqo[sl][:, h * 128:(h + 1) * 128],
                       c_ai[:, h * c.TBW:(h + 1) * c.TBW], f"PB{bank}",
                       (f"wqo{sl}", "cai"), start=(h == 0), stop=(h == c.H - 1),
                       inc=(h == c.H - 1))
                xro = xr[:, tb * NT + dt * c.TBW:tb * NT + (dt + 1) * c.TBW]
                xio = x_bf[:, tb * NT + dt * c.TBW:tb * NT + (dt + 1) * c.TBW]
                vop(lambda v, o=xro, b=bank, i=xio:
                        v.tensor_add(o, PB[b][:, 0:c.TBW], i),
                    (f"PB{bank}", "x"), ("xr",))
            if "C5" not in phases:
                ln_stats_and_norm(tb, "xr", None, bk=(4, 5))

        def emit_gate():
            for tb in range(c.B):
                for dt in range(c.NDT):
                    mm(PB[3][0:c.E, 0:c.TBW], gate_sb[:, dt * c.E:(dt + 1) * c.E],
                       xr[:, tb * NT + dt * c.TBW:tb * NT + (dt + 1) * c.TBW], "PB3",
                       ("gate", "xr"), start=(dt == 0), stop=(dt == c.NDT - 1),
                       inc=(dt == c.NDT - 1))
                s.op("scalar", lambda a, t=tb: a.activation(
                        scor[0:c.E, t * c.TBW:(t + 1) * c.TBW], PB[3][0:c.E, 0:c.TBW],
                        AF.Sigmoid),
                     reads=("PB3",), writes=("scor",))
            for tc in range(c.B * c.CH):
                tr(PB[4][:, 0:c.E], scor[0:c.E, tc * 128:(tc + 1) * 128],
                   id_f[0:c.E, 0:c.E], "PB4", ("scor", "idf"))
                vop(lambda v: v.tensor_copy(sct[:], PB[4][:, 0:c.E]), ("PB4",), ("sct",))
                vop(lambda v: v.tensor_add(sbt[:], sct[:], biasb[:]), ("sct", "biasb"), ("sbt",))
                vop(lambda v: v.tensor_reduce(colv[:], sbt[:], AXX, ALU.max), ("sbt",), ("colv",))
                vop(lambda v: v.tensor_single_scalar(is1[:], sbt[:], colv[:, 0:1], ALU.is_ge),
                    ("sbt", "colv"), ("is1",))
                vop(lambda v: v.scalar_tensor_tensor(sbm[:], is1[:], -1e30, sbt[:],
                                                     ALU.mult, ALU.add),
                    ("is1", "sbt"), ("sbm",))
                vop(lambda v: v.tensor_reduce(colv2[:], sbm[:], AXX, ALU.max), ("sbm",), ("colv2",))
                vop(lambda v: v.tensor_single_scalar(m2t[:], sbt[:], colv2[:, 0:1], ALU.is_ge),
                    ("sbt", "colv2"), ("m2t",))
                vop(lambda v: v.tensor_mul(gtk[:], m2t[:], sct[:]), ("m2t", "sct"), ("gtk",))
                tr(PB[5][0:c.E, 0:128], gtk[:, 0:c.E], id_f[:], "PB5", ("gtk", "idf"))
                vop(lambda v, q=tc: v.tensor_copy(G_T[0:c.E, q * 128:(q + 1) * 128],
                                                  PB[5][0:c.E, 0:128]),
                    ("PB5",), ("GT",))

        s.barrier()

        HHT = c.NHT * c.B * c.TBW
        f_H = ar[:, 0:HHT]
        W1C = c.NDT * 128
        f_w1 = [ar[:, HHT + i * W1C:HHT + (i + 1) * W1C] for i in range(2)]
        f_w3 = [ar[:, HHT + 2 * W1C + i * W1C:HHT + 2 * W1C + (i + 1) * W1C] for i in range(2)]
        W2C = c.NHT * 128
        w2o = HHT + 4 * W1C
        f_w2 = [ar[:, w2o + i * W2C:w2o + (i + 1) * W2C] for i in range(2)]

        for ei in range(c.E + 1 if "F" in phases else 0):
            e = ei - 1
            routed = e >= 0
            if ei == 1 and "G" in phases:
                emit_gate()
            if routed:
                for tb in range(c.B):
                    mm(PB[5][:, 0:c.TBW], esel_sb[:, e * 128:(e + 1) * 128],
                       G_T[0:c.E, tb * c.TBW:(tb + 1) * c.TBW], "PB5",
                       ("esel", "GT"))
                    vop(lambda v, t=tb: v.tensor_copy(rsA[t][:, 0:c.TBW], PB[5][:, 0:c.TBW]),
                        ("PB5",), (f"rs0{tb}",))
            for ht in range(c.NHT):
                sl = ht % 2
                w1src = ins["w1_pk"][e, ht] if routed else ins["sw1_pk"][ht]
                w3src = ins["w3_pk"][e, ht] if routed else ins["sw3_pk"][ht]
                gdma(f_w1[sl], flat(w1src), f"w1{sl}")
                gdma(f_w3[sl], flat(w3src), f"w3{sl}")
                for tb in range(c.B):
                    bu = tb
                    bv = 2 + tb
                    x1b = xr[:, tb * NT:(tb + 1) * NT]
                    for dt in range(c.NDT):
                        mm(PB[bu][:, 0:c.TBW], f_w1[sl][:, dt * 128:(dt + 1) * 128],
                           x1b[:, dt * c.TBW:(dt + 1) * c.TBW], f"PB{bu}",
                           (f"w1{sl}", "xr"), start=(dt == 0), stop=(dt == c.NDT - 1),
                           inc=(dt == c.NDT - 1))
                    for dt in range(c.NDT):
                        mm(PB[bv][:, 0:c.TBW], f_w3[sl][:, dt * 128:(dt + 1) * 128],
                           x1b[:, dt * c.TBW:(dt + 1) * c.TBW], f"PB{bv}",
                           (f"w3{sl}", "xr"), start=(dt == 0), stop=(dt == c.NDT - 1),
                           inc=(dt == c.NDT - 1))
                    hsl = f_H[:, (ht * c.B + tb) * c.TBW:(ht * c.B + tb + 1) * c.TBW]
                    if routed:
                        vop(lambda v, o=act_s[tb], b=bu, t=tb:
                                v.tensor_mul(o[:], PB[b][:, 0:c.TBW], rsA[t][:, 0:c.TBW]),
                            (f"PB{bu}", f"rs0{tb}"), (f"act{tb}",))
                        s.op("scalar", lambda a, o=silu_s[tb], i=act_s[tb]:
                                 a.activation(o[:], i[:], AF.Silu),
                             reads=(f"act{tb}",), writes=(f"silu{tb}",))
                        vop(lambda v, o=sq_s[tb], b=bv, t=tb:
                                v.tensor_mul(o[:], PB[b][:, 0:c.TBW], rsA[t][:, 0:c.TBW]),
                            (f"PB{bv}", f"rs0{tb}"), (f"sq{tb}",))
                        vop(lambda v, o=hsl, a=silu_s[tb], b=sq_s[tb]:
                                v.tensor_mul(o, a[:], b[:]),
                            (f"silu{tb}", f"sq{tb}"), ("H",))
                    else:
                        s.op("scalar", lambda a, o=silu_s[tb], b=bu:
                                 a.activation(o[:], PB[b][:, 0:c.TBW], AF.Silu),
                             reads=(f"PB{bu}",), writes=(f"silu{tb}",))
                        vop(lambda v, o=sq_s[tb], b=bv:
                                v.tensor_copy(o[:], PB[b][:, 0:c.TBW]),
                            (f"PB{bv}",), (f"sq{tb}",))
                        vop(lambda v, o=hsl, a=silu_s[tb], b=sq_s[tb]:
                                v.tensor_mul(o, a[:], b[:]),
                            (f"silu{tb}", f"sq{tb}"), ("H",))
            def ws(tb_, dt_, sl_):
                ss = dt_ % 2
                vop(lambda v, o=rsA[ss], i=rsB[sl_]:
                        v.tensor_mul(o[:, 0:c.TBW], i[:, 0:c.TBW], rstd_b[:]),
                    (f"rs1{sl_}", "rstd"), (f"rs0{ss}",))
                sdma_out(out_dram[dt_ * 128:(dt_ + 1) * 128, tb_ * c.TBW:(tb_ + 1) * c.TBW],
                         rsA[ss][:, 0:c.TBW], f"rs0{ss}")

            last = ei == c.E and "L2" in phases
            for dt in range(c.NDT):
                sl = dt % 2
                w2src = ins["w2_pk"][e, dt] if routed else ins["sw2_pk"][dt]
                gdma(f_w2[sl], flat(w2src), f"w2{sl}")
                for tb in range(c.B):
                    bank = 4 + tb
                    for ht in range(c.NHT):
                        mm(PB[bank][:, 0:c.TBW], f_w2[sl][:, ht * 128:(ht + 1) * 128],
                           f_H[:, (ht * c.B + tb) * c.TBW:(ht * c.B + tb + 1) * c.TBW],
                           f"PB{bank}", (f"w2{sl}", "H"), start=(ht == 0),
                           stop=(ht == c.NHT - 1), inc=(ht == c.NHT - 1))
                    rsl = x_bf[:, tb * NT + dt * c.TBW:tb * NT + (dt + 1) * c.TBW]
                    if ei == 0:
                        vop(lambda v, o=rsl, b=bank: v.tensor_copy(o, PB[b][:, 0:c.TBW]),
                            (f"PB{bank}",), ("racc",))
                    else:
                        vop(lambda v, o=rsl, b=bank: v.tensor_add(o, o, PB[b][:, 0:c.TBW]),
                            (f"PB{bank}", "racc"), ("racc",))
                    if last:
                        off = tb * NT + dt * c.TBW
                        vop(lambda v, q=off: v.tensor_add(
                                xr[:, q:q + c.TBW], xr[:, q:q + c.TBW],
                                x_bf[:, q:q + c.TBW]),
                            ("xr", "racc"), ("xr",))
            if last:
                ln_stats_and_norm(0, "xr", ws)
                ln_stats_and_norm(1, "xr", ws)

        s.finish()

        block = ctx.enter_context(nc.Block())
        block.sync(lambda e: [f(e) for f in s.progs["sync"]])
        block.gpsimd(lambda e: [f(e) for f in s.progs["gpsimd"]])
        block.vector(lambda e: [f(e) for f in s.progs["vector"]])
        block.scalar(lambda e: [f(e) for f in s.progs["scalar"]])
        block.tensor(lambda e: [f(e) for f in s.progs["tensor"]])

    return nc


def _rope_tables(cfg):
    hd = cfg.HD
    inv = 1.0 / ROPE_BASE ** (np.arange(0, hd, 2, dtype=np.float64) / hd)
    t = np.arange(cfg.S, dtype=np.float64)
    fr = np.outer(t, inv)
    emb = np.concatenate([fr, fr], -1)
    cosT = np.cos(emb).T.astype(np.float32)
    sinT = np.sin(emb).T.astype(np.float32)
    sinmT = sinT.copy()
    sinmT[: hd // 2] *= -1.0
    sinm2T = np.roll(sinmT, -64, axis=0)
    return cosT, sinm2T


def _pack_weights(cfg, inp):
    c = cfg
    pk = {}
    Wq = inp["L0_attn_Wq"]
    pk["wq_pk"] = np.ascontiguousarray(
        Wq.reshape(c.H, 128, c.NDT, 128).transpose(0, 3, 2, 1))
    Wk = inp["L0_attn_Wk"]
    pk["wk_pk"] = np.ascontiguousarray(
        Wk.reshape(c.KVH, 128, c.NDT, 128).transpose(3, 2, 0, 1))
    Wv = inp["L0_attn_Wv"]
    pk["wv_pk"] = np.ascontiguousarray(
        Wv.reshape(c.KVH, 128, c.NDT, 128).transpose(3, 2, 0, 1))
    Wo = inp["L0_attn_Wo"]
    pk["wo_pk"] = np.ascontiguousarray(
        Wo.reshape(c.NDT, 128, c.H, 128).transpose(0, 3, 2, 1))
    gw = inp["L0_ffn_gate_w"]
    pk["gate_pk"] = np.ascontiguousarray(
        gw.reshape(c.E, c.NDT, 128).transpose(2, 1, 0))
    pk["bias_row"] = np.ascontiguousarray(
        inp["L0_ffn_expert_bias"].reshape(1, c.E))
    for nm, w in (("w1_pk", inp["L0_ffn_w1"]), ("w3_pk", inp["L0_ffn_w3"])):
        pk[nm] = np.ascontiguousarray(
            w.reshape(c.E, c.NHT, 128, c.NDT, 128).transpose(0, 1, 4, 3, 2))
    pk["w2_pk"] = np.ascontiguousarray(
        inp["L0_ffn_w2"].reshape(c.E, c.NDT, 128, c.NHT, 128).transpose(0, 1, 4, 3, 2))
    for nm, w in (("sw1_pk", inp["L0_ffn_sw1"]), ("sw3_pk", inp["L0_ffn_sw3"])):
        pk[nm] = np.ascontiguousarray(
            w.reshape(c.NHT, 128, c.NDT, 128).transpose(0, 3, 2, 1))
    pk["sw2_pk"] = np.ascontiguousarray(
        inp["L0_ffn_sw2"].reshape(c.NDT, 128, c.NHT, 128).transpose(0, 3, 2, 1))
    pk["id_f"] = np.eye(128, dtype=np.float32)
    r64 = np.zeros((128, 128), np.float32)
    for mm_ in range(128):
        r64[(mm_ + 64) % 128, mm_] = 1.0
    pk["r64"] = r64
    es = np.zeros((c.E, c.E * 128), np.float32)
    for e in range(c.E):
        es[e, e * 128:(e + 1) * 128] = 1.0
    pk["esel"] = es
    jj = np.arange(128)
    m1 = (jj[:, None] <= jj[None, :]).astype(np.float32)
    pk["maskg"] = np.ascontiguousarray(np.tile(m1, (1, c.GQ)))
    return pk


def _core_inputs(cfg, inp, pk, cosT, sinmT, core):
    c = cfg
    x = inp["L0_attn_x"]
    starts = [core * c.TBW, (c.NC - 1 - core) * c.TBW]
    d = {}

    def xpack(tb, t0):
        blk = x[tb, t0:t0 + c.TBW, :]
        return np.ascontiguousarray(
            blk.reshape(c.TBW, c.NDT, 128).transpose(2, 1, 0))

    d["xs"] = np.stack([xpack(tb, starts[tb]) for tb in range(c.B)])
    d["cosc"] = np.stack([np.ascontiguousarray(cosT[:, t0:t0 + c.TBW]) for t0 in starts])
    d["sinmc"] = np.stack([np.ascontiguousarray(sinmT[:, t0:t0 + c.TBW]) for t0 in starts])
    pm = np.zeros((2, c.NC), np.float32)
    pm[0, :core] = 1.0
    pm[1, core + 1:] = 1.0
    d["pmask"] = np.ascontiguousarray(
        np.broadcast_to(pm.reshape(1, -1), (128, 2 * c.NC)).astype(np.float32))
    d.update(pk)
    return d


def run(cfg, inp, sim=None, phases=("A", "C", "G", "F", "L2")):
    c = cfg
    nc = build_program(c, phases=phases)
    cosT, sinmT = _rope_tables(c)
    pk = _pack_weights(c, inp)
    in_maps = [_core_inputs(c, inp, pk, cosT, sinmT, core) for core in range(c.NC)]
    runner = sim or (lambda n, m: run_bass_kernel_spmd(n, m, core_ids=list(range(c.NC))))
    res = runner(nc, in_maps)
    out = np.empty((c.B, c.S, c.D), np.float32)
    for core in range(c.NC):
        o = res.results[core]["out"]
        out[0, core * c.TBW:(core + 1) * c.TBW, :] = o[:, 0:c.TBW].T
        s1 = (c.NC - 1 - core) * c.TBW
        out[1, s1:s1 + c.TBW, :] = o[:, c.TBW:].T
    return out


def kernel(**inputs):
    return run(FULL, inputs)


# version 17
# speedup vs baseline: 1.0148x; 1.0143x over previous
import contextlib
import numpy as np
import concourse.bass as bass
import concourse.mybir as mybir
from concourse.bass_utils import run_bass_kernel_spmd

F32 = mybir.dt.float32
BF16 = mybir.dt.bfloat16
AF = mybir.ActivationFunctionType
ALU = mybir.AluOpType
AXX = mybir.AxisListType.X

LN_EPS = 1e-5
ATT_EPS = 1e-6
ROPE_BASE = 10000.0


class Cfg:
    def __init__(self, NC=8, TBW=512, D=2048, H=16, KVH=4, HID=1408, E=8):
        self.NC = NC
        self.TBW = TBW
        self.D = D
        self.H = H
        self.KVH = KVH
        self.HD = 128
        self.HID = HID
        self.E = E
        self.B = 2
        self.NDT = D // 128
        self.NHT = HID // 128
        self.GQ = H // KVH
        self.CH = TBW // 128
        self.NPRE = NC - 1
        self.S = NC * TBW


FULL = Cfg()


class Sched:

    ENGS = ("sync", "gpsimd", "vector", "scalar", "tensor")

    def __init__(self, nc, ctx):
        self.nc = nc
        self.ctx = ctx
        self.sems = {e: ctx.enter_context(nc.semaphore(f"ch_{e}")) for e in self.ENGS}
        self.cnt = {e: 0 for e in self.ENGS}
        self.progs = {e: [] for e in self.ENGS}
        self.wm = {e: {} for e in self.ENGS}
        self.writer = {}
        self.readers = {}

    def _stream_sem(self, key):
        if key not in self.sems:
            self.sems[key] = self.ctx.enter_context(self.nc.semaphore(f"d_{key}"))
            self.cnt[key] = 0
        return self.sems[key]

    def _wait(self, eng, tgt, val):
        if val <= 0 or self.wm[eng].get(tgt, 0) >= val:
            return
        if tgt == eng and val > self.cnt.get(eng, 0):
            return
        assert val <= self.cnt.get(tgt, 0), (
            f"{eng} would wait on {tgt}>={val} before that inc is emitted "
            f"(cnt={self.cnt.get(tgt, 0)}): open accumulation group spans a "
            f"buffer refill -> deadlock")
        self.wm[eng][tgt] = val
        sem = self.sems[tgt]
        self.progs[eng].append(lambda e, s=sem, v=val: e.wait_ge(s, v))

    def _deps(self, eng, reads, writes):
        for t in reads:
            w = self.writer.get(t)
            if w is not None:
                self._wait(eng, w[0], w[1])
        for t in writes:
            w = self.writer.get(t)
            if w is not None:
                self._wait(eng, w[0], w[1])
            for (re, rv) in self.readers.get(t, ()):
                self._wait(eng, re, rv)

    def _record(self, stream, val, reads, writes):
        for t in reads:
            self.readers.setdefault(t, []).append((stream, val))
        for t in writes:
            self.writer[t] = (stream, val)
            self.readers[t] = []

    def op(self, eng, fn, reads=(), writes=(), inc=True):
        self._deps(eng, reads, writes)
        if inc:
            self.cnt[eng] += 1
            val = self.cnt[eng]
            sem = self.sems[eng]
            self.progs[eng].append(lambda e, f=fn, s=sem: f(e).then_inc(s, 1))
        else:
            val = self.cnt[eng] + 1
            self.progs[eng].append(lambda e, f=fn: f(e))
        self._record(eng, val, reads, writes)

    def dma(self, eng, fn, key, reads=(), writes=()):
        self._deps(eng, reads, writes)
        sem = self._stream_sem(key)
        self.cnt[key] += 16
        val = self.cnt[key]
        self.progs[eng].append(lambda e, f=fn, s=sem: f(e).then_inc(s, 16))
        self._record(key, val, reads, writes)

    def counts(self):
        return dict(self.cnt)

    def wait_counts(self, eng, snap):
        for tgt, v in snap.items():
            if tgt != eng:
                self._wait(eng, tgt, v)

    def barrier(self, engines=None):
        engines = engines or self.ENGS
        snap = self.counts()
        for e in engines:
            self.wait_counts(e, snap)

    def finish(self):
        self.wait_counts("sync", self.counts())
        v = self.cnt["sync"]
        if v:
            sem = self.sems["sync"]
            self.progs["sync"].append(lambda e, s=sem, vv=v: e.wait_ge(s, vv))


def build_program(cfg, phases=("A", "C", "G", "F", "L2")):
    c = cfg
    nc = bass.Bass("TRN2", target_bir_lowering=False)

    dr = lambda name, shp: nc.dram_tensor(name, list(shp), F32, kind="ExternalInput").ap()
    ins = {}
    ins["xs"] = dr("xs", (c.B, 128, c.NDT, c.TBW))
    ins["xa_pre"] = dr("xa_pre", (c.NPRE, 128, c.NDT, c.TBW))
    ins["scales"] = dr("scales", (128, c.NPRE * 2))
    ins["cosc"] = dr("cosc", (c.B, 128, c.TBW))
    ins["sinmc"] = dr("sinmc", (c.B, 128, c.TBW))
    ins["cospre"] = dr("cospre", (c.NPRE, 128, c.TBW))
    ins["sinmpre"] = dr("sinmpre", (c.NPRE, 128, c.TBW))
    ins["wq_pk"] = dr("wq_pk", (c.H, 128, c.NDT, 128))
    ins["wk_pk"] = dr("wk_pk", (128, c.NDT, c.KVH, 128))
    ins["wv_pk"] = dr("wv_pk", (128, c.NDT, c.KVH, 128))
    ins["wo_pk"] = dr("wo_pk", (c.NDT, 128, c.H, 128))
    ins["gate_pk"] = dr("gate_pk", (128, c.NDT, c.E))
    ins["bias_row"] = dr("bias_row", (1, c.E))
    ins["w1_pk"] = dr("w1_pk", (c.E, c.NHT, 128, c.NDT, 128))
    ins["w3_pk"] = dr("w3_pk", (c.E, c.NHT, 128, c.NDT, 128))
    ins["w2_pk"] = dr("w2_pk", (c.E, c.NDT, 128, c.NHT, 128))
    ins["sw1_pk"] = dr("sw1_pk", (c.NHT, 128, c.NDT, 128))
    ins["sw3_pk"] = dr("sw3_pk", (c.NHT, 128, c.NDT, 128))
    ins["sw2_pk"] = dr("sw2_pk", (c.NDT, 128, c.NHT, 128))
    ins["id_f"] = dr("id_f", (128, 128))
    ins["r64"] = dr("r64", (128, 128))
    ins["esel"] = dr("esel", (c.E, c.E * 128))
    ins["maskg"] = dr("maskg", (128, c.GQ * 128))

    out_dram = nc.dram_tensor("out", [c.D, c.B * c.TBW], F32, kind="ExternalOutput").ap()

    with contextlib.ExitStack() as ctx:
        sbuf = lambda n, cols, dt: ctx.enter_context(nc.sbuf_tensor(n, [128, cols], dt))
        psum = lambda n, cols, dt: ctx.enter_context(nc.psum_tensor(n, [128, cols], dt))

        NT = c.NDT * c.TBW
        AR = max(2 * NT + 4 * c.TBW,
                 2 * c.H * c.TBW + 4 * c.KVH * c.TBW,
                 c.NHT * c.B * c.TBW + 4 * c.NDT * 128 + 2 * c.NHT * 128)

        x_bf = sbuf("x_bf", c.B * NT, BF16)
        xr = sbuf("xr", c.B * NT, BF16)
        ar = sbuf("ar", AR, BF16)
        wk_sb = sbuf("wk_sb", c.NDT * c.KVH * 128, BF16)
        wv_sb = sbuf("wv_sb", c.NDT * c.KVH * 128, BF16)
        wqo = [sbuf(f"wqo{i}", max(c.NDT, c.H) * 128, BF16) for i in range(2)]
        gate_sb = sbuf("gate_sb", c.NDT * c.E, BF16)
        cosc_sb = sbuf("cosc_sb", c.B * c.TBW, BF16)
        sinmc_sb = sbuf("sinmc_sb", c.B * c.TBW, BF16)
        cpre = [sbuf("cpre0", c.TBW, BF16)]
        spre = [sbuf("spre0", c.TBW, BF16)]
        SCR = max(c.TBW, c.GQ * 128)
        rsA = [sbuf(f"rsA{i}", SCR, F32) for i in range(2)]
        rsB = [sbuf(f"rsB{i}", SCR, F32) for i in range(2)]
        act_s = [sbuf(f"act{i}", c.TBW, F32) for i in range(2)]
        silu_s = [sbuf(f"silu{i}", c.TBW, BF16) for i in range(2)]
        sq_s = [sbuf(f"sq{i}", c.TBW, BF16) for i in range(2)]
        S_bf = sbuf("S_bf", c.B * c.KVH * 128, BF16)
        Z_bf = sbuf("Z_bf", c.B * c.KVH * 128, BF16)
        m_b = sbuf("m_b", c.TBW, F32)
        rstd_b = sbuf("rstd_b", c.TBW, F32)
        G_T = ctx.enter_context(nc.sbuf_tensor("G_T", [c.E, c.B * c.TBW], F32))
        scor = ctx.enter_context(nc.sbuf_tensor("scor", [c.E, c.B * c.TBW], F32))
        amst = sbuf("amst", c.KVH * c.GQ * 128, BF16)
        sct = sbuf("sct", c.E, F32)
        sbt = sbuf("sbt", c.E, F32)
        is1 = sbuf("is1", c.E, F32)
        sbm = sbuf("sbm", c.E, F32)
        m2t = sbuf("m2t", c.E, F32)
        gtk = sbuf("gtk", c.E, F32)
        colv = sbuf("colv", 1, F32)
        colv2 = sbuf("colv2", 1, F32)
        id_f = sbuf("id_f_sb", 128, F32)
        id_bf = sbuf("id_bf", 128, BF16)
        r64_bf = sbuf("r64_bf", 128, BF16)
        maskg_sb = sbuf("maskg_sb", c.GQ * 128, F32)
        ones128 = sbuf("ones128", 128, BF16)
        onescol = sbuf("onescol", 1, BF16)
        ones_row_f = ctx.enter_context(nc.sbuf_tensor("ones_row_f", [1, 128], F32))
        epsc = sbuf("epsc", 1, F32)
        biasb = sbuf("biasb", c.E, F32)
        bias_row = ctx.enter_context(nc.sbuf_tensor("bias_row_sb", [1, c.E], F32))
        esel_sb = ctx.enter_context(nc.sbuf_tensor("esel_sb", [c.E, c.E * 128], F32))
        scales_sb = sbuf("scales_sb", c.NPRE * 2, F32)

        PB = [psum(f"PB_{i}", c.TBW, F32) for i in range(6)]
        PT = [ctx.enter_context(nc.psum_tensor(f"PT_{i}", [128, 1024], BF16))
              for i in range(2)]

        s = Sched(nc, ctx)

        def mm(out_ap, lhs_ap, rhs_ap, bank, reads, start=True, stop=True, inc=True):
            s.op("tensor",
                 lambda t, o=out_ap, l=lhs_ap, r=rhs_ap, a=start, b=stop:
                     t.matmul(o, l, r, start=a, stop=b),
                 reads=reads, writes=(bank,), inc=inc)

        def tr(out_ap, in_ap, ident_ap, bank, reads, inc=True):
            s.op("tensor",
                 lambda t, o=out_ap, i=in_ap, d=ident_ap: t.transpose(o, i, d),
                 reads=reads, writes=(bank,), inc=inc)

        def vop(fn, reads, writes, eng="vector", inc=True):
            s.op(eng, fn, reads=reads, writes=writes, inc=inc)

        def gdma(out_ap, in_ap, wtok):
            s.dma("gpsimd", lambda g, o=out_ap, i=in_ap: g.dma_start(out=o, in_=i),
                  key="L" + wtok, writes=(wtok,))

        def sdma_in(out_ap, in_ap, wtok):
            s.dma("sync", lambda g, o=out_ap, i=in_ap: g.dma_start(out=o, in_=i),
                  key="L" + wtok, writes=(wtok,))

        def sdma_out(out_ap, in_ap, rtok):
            s.dma("sync", lambda g, o=out_ap, i=in_ap: g.dma_start(out=o, in_=i),
                  key="S" + rtok, reads=(rtok,))

        flat = lambda ap: ap.rearrange("p a b -> p (a b)")

        sdma_in(id_f[:], ins["id_f"][:, :], "idf")
        sdma_in(maskg_sb[:], ins["maskg"][:, :], "maskg")
        sdma_in(scales_sb[:], ins["scales"][:, :], "scales")
        sdma_in(bias_row[:], ins["bias_row"][:, :], "bias_row")
        sdma_in(esel_sb[:], ins["esel"][:, :], "esel")
        gdma(id_bf[:], ins["id_f"][:, :], "id_bf")
        gdma(r64_bf[:], ins["r64"][:, :], "r64")
        gdma(wk_sb[:], ins["wk_pk"].rearrange("p a b c -> p (a b c)"), "wk")
        gdma(wv_sb[:], ins["wv_pk"].rearrange("p a b c -> p (a b c)"), "wv")

        def emit_own_loads():
            for tb in range(c.B):
                gdma(cosc_sb[:, tb * c.TBW:(tb + 1) * c.TBW], ins["cosc"][tb], "cosc")
                gdma(sinmc_sb[:, tb * c.TBW:(tb + 1) * c.TBW], ins["sinmc"][tb], "sinmc")
            for tb in range(c.B):
                gdma(x_bf[:, tb * NT:(tb + 1) * NT], flat(ins["xs"][tb]), "x")
            gdma(gate_sb[:], flat(ins["gate_pk"]), "gate")
        vop(lambda v: v.memset(ones128[:], 1.0), (), ("ones128",))
        vop(lambda v: v.memset(onescol[:], 1.0), (), ("onescol",))
        vop(lambda v: v.memset(ones_row_f[:], 1.0), (), ("onesrow",))
        vop(lambda v: v.memset(epsc[:], ATT_EPS), (), ("epsc",))
        vop(lambda v: v.memset(S_bf[:], 0.0), (), ("Sbf",))
        vop(lambda v: v.memset(Z_bf[:], 0.0), (), ("Zbf",))
        mm(PB[5][:, 0:c.E], ones_row_f[0:1, :], bias_row[0:1, :], "PB5",
           ("onesrow", "bias_row"))
        vop(lambda v: v.tensor_copy(biasb[:], PB[5][:, 0:c.E]), ("PB5",), ("biasb",))

        def rope_u(pb_ap, bank_tok, sin_ap, costoks, pipe, W):
            ub = silu_s[pipe]
            vop(lambda v, o=ub, p=pb_ap, sn=sin_ap:
                    v.tensor_mul(o[:, 0:W], p[:], sn[:]),
                (bank_tok,) + costoks, (f"silu{pipe}",))

        def rope_rest(pb_ap, bank_tok, cos_ap, costoks, dst_ap, dsttok, pipe, rbk):
            qr, acts = (rsA if pipe == 0 else rsB)[1], act_s[pipe]
            ub = silu_s[pipe]
            tokq, toka, toku = (f"rs{pipe}1", f"act{pipe}", f"silu{pipe}")
            W = dst_ap.shape[-1]
            s.op("tensor", lambda t, o=PB[rbk][:, 0:W], i=ub:
                     t.matmul(o, r64_bf[:], i[:, 0:W], start=True, stop=True),
                 reads=("r64", toku), writes=(f"PB{rbk}",))
            vop(lambda v, o=qr, p=pb_ap, cs=cos_ap:
                    v.tensor_mul(o[:, 0:W], p[:], cs[:]),
                (bank_tok,) + costoks, (tokq,))
            vop(lambda v, o=qr, r=PB[rbk]:
                    v.tensor_add(o[:, 0:W], o[:, 0:W], r[:, 0:W]),
                (tokq, f"PB{rbk}"), (tokq,))
            s.op("scalar", lambda a, o=acts, i=qr:
                     a.activation(o[:, 0:W], i[:, 0:W], AF.Relu, scale=-1.0),
                 reads=(tokq,), writes=(toka,))
            s.op("scalar", lambda a, o=acts:
                     a.activation(o[:, 0:W], o[:, 0:W], AF.Exp, scale=-1.0),
                 reads=(toka,), writes=(toka,))
            vop(lambda v, o=dst_ap, i=qr, e=acts:
                    v.scalar_tensor_tensor(o, i[:, 0:W], 0.0, e[:, 0:W], ALU.max, ALU.add),
                (tokq, toka), (dsttok,))

        def rope_elu(pb_ap, bank_tok, cos_ap, sin_ap, costoks, dst_ap, dsttok, pipe,
                     rot_bank=2):
            rope_u(pb_ap, bank_tok, sin_ap, costoks, pipe, dst_ap.shape[-1])
            rope_rest(pb_ap, bank_tok, cos_ap, costoks, dst_ap, dsttok, pipe, rot_bank)

        a_xg = [ar[:, i * NT:(i + 1) * NT] for i in range(2)]
        a_kf = ar[:, 2 * NT:2 * NT + c.TBW]
        a_v = ar[:, 2 * NT + c.TBW:2 * NT + 2 * c.TBW]
        a_kft = ar[:, 2 * NT + 2 * c.TBW:2 * NT + 3 * c.TBW]
        a_vt = ar[:, 2 * NT + 3 * c.TBW:2 * NT + 4 * c.TBW]

        for gi in range(c.NPRE if "A" in phases else 0):
            sl = gi % 2
            tsl = 0
            gdma(a_xg[sl], flat(ins["xa_pre"][gi]), f"xg{sl}")
            gdma(cpre[tsl][:], ins["cospre"][gi], f"cp{tsl}")
            gdma(spre[tsl][:], ins["sinmpre"][gi], f"sp{tsl}")
            if gi == 0:
                emit_own_loads()
            for kv in range(c.KVH):
                pk_, pv_ = 2 * (kv % 2), 2 * (kv % 2) + 1
                pipe = kv % 2
                for dt in range(c.NDT):
                    mm(PB[pk_][:, 0:c.TBW], wk_sb[:, (dt * c.KVH + kv) * 128:(dt * c.KVH + kv + 1) * 128],
                       a_xg[sl][:, dt * c.TBW:(dt + 1) * c.TBW], f"PB{pk_}",
                       ("wk", f"xg{sl}"), start=(dt == 0), stop=(dt == c.NDT - 1),
                       inc=(dt == c.NDT - 1))
                rope_u(PB[pk_][:, 0:c.TBW], f"PB{pk_}", spre[tsl][:],
                       (f"cp{tsl}", f"sp{tsl}"), pipe, c.TBW)
                for dt in range(c.NDT):
                    mm(PB[pv_][:, 0:c.TBW], wv_sb[:, (dt * c.KVH + kv) * 128:(dt * c.KVH + kv + 1) * 128],
                       a_xg[sl][:, dt * c.TBW:(dt + 1) * c.TBW], f"PB{pv_}",
                       ("wv", f"xg{sl}"), start=(dt == 0), stop=(dt == c.NDT - 1),
                       inc=(dt == c.NDT - 1))
                rope_rest(PB[pk_][:, 0:c.TBW], f"PB{pk_}", cpre[tsl][:],
                          (f"cp{tsl}", f"sp{tsl}"), a_kf[:, 0:c.TBW], "akf", pipe,
                          2 * ((kv + 1) % 2))
                vop(lambda v, o=a_v, p=PB[pv_]: v.tensor_copy(o[:, 0:c.TBW], p[:, 0:c.TBW]),
                    (f"PB{pv_}",), ("av",))
                for ci in range(c.CH):
                    tr(PT[0][:, 512 + ci * 128:512 + (ci + 1) * 128], a_kf[:, ci * 128:(ci + 1) * 128],
                       id_bf[:], "PT0", ("akf", "id_bf"), inc=(ci == c.CH - 1))
                    tr(PT[1][:, 512 + ci * 128:512 + (ci + 1) * 128], a_v[:, ci * 128:(ci + 1) * 128],
                       id_bf[:], "PT1", ("av", "id_bf"), inc=(ci == c.CH - 1))
                vop(lambda v, o=a_kft, p=PT[0]: v.tensor_copy(o[:, 0:c.TBW], p[:, 512:512 + c.TBW]),
                    ("PT0",), ("akft",))
                vop(lambda v, o=a_vt, p=PT[1]: v.tensor_copy(o[:, 0:c.TBW], p[:, 512:512 + c.TBW]),
                    ("PT1",), ("avt",))
                for ci in range(c.CH):
                    mm(PB[4][:, 0:128], a_kft[:, ci * 128:(ci + 1) * 128],
                       a_vt[:, ci * 128:(ci + 1) * 128], "PB4", ("akft", "avt"),
                       start=(ci == 0), stop=(ci == c.CH - 1), inc=(ci == c.CH - 1))
                for ci in range(c.CH):
                    mm(PB[5][:, 0:128], a_kft[:, ci * 128:(ci + 1) * 128],
                       ones128[:], "PB5", ("akft", "ones128"),
                       start=(ci == 0), stop=(ci == c.CH - 1), inc=(ci == c.CH - 1))
                for tb in range(c.B):
                    scl = scales_sb[:, gi * 2 + tb:gi * 2 + tb + 1]
                    so = (tb * c.KVH + kv) * 128
                    vop(lambda v, o=S_bf, p=PB[4], sc=scl, q=so:
                            v.scalar_tensor_tensor(o[:, q:q + 128], p[:, 0:128], sc,
                                                   o[:, q:q + 128], ALU.mult, ALU.add),
                        ("PB4", "scales", "Sbf"), ("Sbf",))
                    vop(lambda v, o=Z_bf, p=PB[5], sc=scl, q=so:
                            v.scalar_tensor_tensor(o[:, q:q + 128], p[:, 0:128], sc,
                                                   o[:, q:q + 128], ALU.mult, ALU.add),
                        ("PB5", "scales", "Zbf"), ("Zbf",))
        if "A" not in phases or c.NPRE == 0:
            emit_own_loads()
        s.barrier()

        HT = c.H * c.TBW
        c_qf = ar[:, 0:HT]
        c_ai = ar[:, HT:2 * HT]
        KT = c.KVH * c.TBW
        c_kf = ar[:, 2 * HT:2 * HT + KT]
        c_v = ar[:, 2 * HT + KT:2 * HT + 2 * KT]
        c_kft = ar[:, 2 * HT + 2 * KT:2 * HT + 3 * KT]
        c_vt = ar[:, 2 * HT + 3 * KT:2 * HT + 4 * KT]

        def ln_stats_and_norm(tb, src_tok, write_stage, bk=(0, 1)):
            xrb = xr[:, tb * NT:(tb + 1) * NT]
            for dt in range(c.NDT):
                sl = dt % 2
                s.op("scalar", lambda a, o=sq_s[sl], i=xrb, d=dt:
                         a.activation(o[:], i[:, d * c.TBW:(d + 1) * c.TBW], AF.Square),
                     reads=(src_tok,), writes=(f"sq{sl}",))
                mm(PB[bk[0]][:, 0:c.TBW], ones128[:], xrb[:, dt * c.TBW:(dt + 1) * c.TBW],
                   f"PB{bk[0]}", ("ones128", src_tok), start=(dt == 0), stop=(dt == c.NDT - 1))
                mm(PB[bk[1]][:, 0:c.TBW], ones128[:], sq_s[sl][:], f"PB{bk[1]}",
                   ("ones128", f"sq{sl}"), start=(dt == 0), stop=(dt == c.NDT - 1))
            inv_d = 1.0 / c.D
            vop(lambda v: v.tensor_scalar_mul(m_b[:], PB[bk[0]][:, 0:c.TBW], inv_d),
                (f"PB{bk[0]}",), ("m",))
            vop(lambda v: v.tensor_mul(rsA[0][:, 0:c.TBW], m_b[:], m_b[:]), ("m",), ("rs00",))
            vop(lambda v: v.scalar_tensor_tensor(rsA[1][:, 0:c.TBW], PB[bk[1]][:, 0:c.TBW], inv_d,
                                                 rsA[0][:, 0:c.TBW], ALU.mult, ALU.subtract),
                (f"PB{bk[1]}", "rs00"), ("rs01",))
            vop(lambda v: v.tensor_scalar_add(rsA[1][:, 0:c.TBW], rsA[1][:, 0:c.TBW], LN_EPS),
                ("rs01",), ("rs01",))
            vop(lambda v: v.reciprocal(rsA[1][:, 0:c.TBW], rsA[1][:, 0:c.TBW]), ("rs01",), ("rs01",))
            s.op("scalar", lambda a: a.activation(rstd_b[:], rsA[1][:, 0:c.TBW], AF.Sqrt),
                 reads=("rs01",), writes=("rstd",))
            for dt in range(c.NDT):
                sl = dt % 2
                xt = xrb[:, dt * c.TBW:(dt + 1) * c.TBW]
                vop(lambda v, o=rsB[sl], i=xt: v.tensor_sub(o[:, 0:c.TBW], i, m_b[:]),
                    (src_tok, "m"), (f"rs1{sl}",))
                if write_stage is None:
                    vop(lambda v, o=xt, i=rsB[sl]: v.tensor_mul(o, i[:, 0:c.TBW], rstd_b[:]),
                        (f"rs1{sl}", "rstd", src_tok), (src_tok,))
                else:
                    write_stage(tb, dt, sl)

        for tb in range(c.B if "C" in phases else 0):
            xb = x_bf[:, tb * NT:(tb + 1) * NT]
            cosb = cosc_sb[:, tb * c.TBW:(tb + 1) * c.TBW]
            sinb = sinmc_sb[:, tb * c.TBW:(tb + 1) * c.TBW]
            for kv in range(c.KVH if "C1" not in phases else 0):
                for dt in range(c.NDT):
                    mm(PB[0][:, 0:c.TBW], wk_sb[:, (dt * c.KVH + kv) * 128:(dt * c.KVH + kv + 1) * 128],
                       xb[:, dt * c.TBW:(dt + 1) * c.TBW], "PB0", ("wk", "x"),
                       start=(dt == 0), stop=(dt == c.NDT - 1), inc=(dt == c.NDT - 1))
                rope_u(PB[0][:, 0:c.TBW], "PB0", sinb, ("cosc", "sinmc"),
                       kv % 2, c.TBW)
                for dt in range(c.NDT):
                    mm(PB[1][:, 0:c.TBW], wv_sb[:, (dt * c.KVH + kv) * 128:(dt * c.KVH + kv + 1) * 128],
                       xb[:, dt * c.TBW:(dt + 1) * c.TBW], "PB1", ("wv", "x"),
                       start=(dt == 0), stop=(dt == c.NDT - 1), inc=(dt == c.NDT - 1))
                kfs = c_kf[:, kv * c.TBW:(kv + 1) * c.TBW]
                rope_rest(PB[0][:, 0:c.TBW], "PB0", cosb, ("cosc", "sinmc"),
                          kfs, "ckf", kv % 2, 2 + kv % 2)
                vop(lambda v, o=c_v, p=PB[1], q=kv:
                        v.tensor_copy(o[:, q * c.TBW:(q + 1) * c.TBW], p[:, 0:c.TBW]),
                    ("PB1",), ("cv",))
                for ci in range(c.CH):
                    off = kv * c.TBW + ci * 128
                    tr(PT[0][:, 512 + ci * 128:512 + (ci + 1) * 128], c_kf[:, off:off + 128],
                       id_bf[:], "PT0", ("ckf", "id_bf"), inc=(ci == c.CH - 1))
                    tr(PT[1][:, 512 + ci * 128:512 + (ci + 1) * 128], c_v[:, off:off + 128],
                       id_bf[:], "PT1", ("cv", "id_bf"), inc=(ci == c.CH - 1))
                vop(lambda v, o=c_kft, p=PT[0], q=kv:
                        v.tensor_copy(o[:, q * c.TBW:(q + 1) * c.TBW], p[:, 512:512 + c.TBW]),
                    ("PT0",), ("ckft",))
                vop(lambda v, o=c_vt, p=PT[1], q=kv:
                        v.tensor_copy(o[:, q * c.TBW:(q + 1) * c.TBW], p[:, 512:512 + c.TBW]),
                    ("PT1",), ("cvt",))
            pend = None
            for h in range(c.H if "C2" not in phases else 0):
                sl = h % 2
                gdma(wqo[sl][:, 0:c.NDT * 128], flat(ins["wq_pk"][h]), f"wqo{sl}")
                for dt in range(c.NDT):
                    mm(PB[sl][:, 0:c.TBW], wqo[sl][:, dt * 128:(dt + 1) * 128],
                       xb[:, dt * c.TBW:(dt + 1) * c.TBW], f"PB{sl}",
                       (f"wqo{sl}", "x"), start=(dt == 0), stop=(dt == c.NDT - 1),
                       inc=(dt == c.NDT - 1))
                if pend is not None:
                    rope_rest(*pend)
                rope_u(PB[sl][:, 0:c.TBW], f"PB{sl}", sinb, ("cosc", "sinmc"),
                       sl, c.TBW)
                pend = (PB[sl][:, 0:c.TBW], f"PB{sl}", cosb, ("cosc", "sinmc"),
                        c_qf[:, h * c.TBW:(h + 1) * c.TBW], "cqf", sl, 2 + sl)
            if pend is not None and "C2" not in phases:
                rope_rest(*pend)
            GW = c.GQ * 128
            denr_sl = [rsA[0], rsA[1], act_s[0], act_s[1]]
            denr_tk = ["rs00", "rs01", "act0", "act1"]
            for ci in range(c.CH if "C3" not in phases else 0):
                for kv in range(c.KVH):
                    for hl in range(c.GQ):
                        h = kv * c.GQ + hl
                        mm(PB[kv][:, hl * 128:(hl + 1) * 128],
                           c_kf[:, kv * c.TBW + ci * 128:kv * c.TBW + (ci + 1) * 128],
                           c_qf[:, h * c.TBW + ci * 128:h * c.TBW + (ci + 1) * 128],
                           f"PB{kv}", ("ckf", "cqf"), inc=(hl == c.GQ - 1))
                for kv in range(c.KVH):
                    vop(lambda v, q=kv: v.tensor_mul(amst[:, q * GW:(q + 1) * GW],
                                                     PB[q][:, 0:GW], maskg_sb[:]),
                        (f"PB{kv}", "maskg"), (f"amst{kv}",))
                for kv in range(c.KVH):
                    so = (tb * c.KVH + kv) * 128
                    for hl in range(c.GQ):
                        h = kv * c.GQ + hl
                        mm(PB[kv][0:1, hl * 128:(hl + 1) * 128], onescol[:],
                           amst[:, kv * GW + hl * 128:kv * GW + (hl + 1) * 128],
                           f"PB{kv}", ("onescol", f"amst{kv}"), start=True,
                           stop=False, inc=False)
                        mm(PB[kv][0:1, hl * 128:(hl + 1) * 128], Z_bf[:, so:so + 1],
                           c_qf[:, h * c.TBW + ci * 128:h * c.TBW + (ci + 1) * 128],
                           f"PB{kv}", ("Zbf", "cqf"), start=False, stop=True,
                           inc=(hl == c.GQ - 1))
                for kv in range(c.KVH):
                    s.op("scalar", lambda a, d=denr_sl[kv], q=kv: a.activation(
                            d[0:1, 0:GW], PB[q][0:1, 0:GW], AF.Ln,
                            bias=epsc[0:1, 0:1]),
                         reads=(f"PB{kv}", "epsc"), writes=(denr_tk[kv],))
                    s.op("scalar", lambda a, d=denr_sl[kv]: a.activation(
                            d[0:1, 0:GW], d[0:1, 0:GW], AF.Exp, scale=-1.0),
                         reads=(denr_tk[kv],), writes=(denr_tk[kv],))
                for pair in range(0, c.KVH, 2):
                    kvs = [k for k in (pair, pair + 1) if k < c.KVH]
                    for kv in kvs:
                        so = (tb * c.KVH + kv) * 128
                        nb = 4 + (kv % 2)
                        for hl in range(c.GQ):
                            h = kv * c.GQ + hl
                            mm(PB[nb][:, hl * 128:(hl + 1) * 128],
                               c_vt[:, kv * c.TBW + ci * 128:kv * c.TBW + (ci + 1) * 128],
                               amst[:, kv * GW + hl * 128:kv * GW + (hl + 1) * 128],
                               f"PB{nb}", ("cvt", f"amst{kv}"), start=True,
                               stop=False, inc=False)
                            mm(PB[nb][:, hl * 128:(hl + 1) * 128],
                               S_bf[:, so:so + 128],
                               c_qf[:, h * c.TBW + ci * 128:h * c.TBW + (ci + 1) * 128],
                               f"PB{nb}", ("Sbf", "cqf"), start=False, stop=True,
                               inc=(hl == c.GQ - 1))
                    for kv in kvs:
                        for hl in range(c.GQ):
                            mm(PB[kv][:, hl * 128:(hl + 1) * 128], ones_row_f[0:1, :],
                               denr_sl[kv][0:1, hl * 128:(hl + 1) * 128], f"PB{kv}",
                               ("onesrow", denr_tk[kv]), inc=(hl == c.GQ - 1))
                    for kv in kvs:
                        nb = 4 + (kv % 2)
                        vop(lambda v, q=kv: v.tensor_copy(rsB[q % 2][:, 0:GW],
                                                          PB[q][:, 0:GW]),
                            (f"PB{kv}",), (f"rs1{kv % 2}",))
                        ai3 = c_ai.rearrange("p (h t) -> p h t", h=c.H)[
                            :, kv * c.GQ:(kv + 1) * c.GQ, ci * 128:(ci + 1) * 128]
                        n3 = PB[nb][:, 0:GW].rearrange("p (h t) -> p h t", h=c.GQ)
                        d3 = rsB[kv % 2][:, 0:GW].rearrange("p (h t) -> p h t", h=c.GQ)
                        vop(lambda v, o=ai3, a=n3, b=d3: v.tensor_mul(o, a, b),
                            (f"PB{nb}", f"rs1{kv % 2}"), ("cai",))
                if ci < c.CH - 1:
                    for kv in range(c.KVH):
                        so = (tb * c.KVH + kv) * 128
                        nb = 4 + (kv % 2)
                        mm(PB[kv][:, 0:128],
                           c_kft[:, kv * c.TBW + ci * 128:kv * c.TBW + (ci + 1) * 128],
                           c_vt[:, kv * c.TBW + ci * 128:kv * c.TBW + (ci + 1) * 128],
                           f"PB{kv}", ("ckft", "cvt"))
                        mm(PB[nb][:, 0:128],
                           c_kft[:, kv * c.TBW + ci * 128:kv * c.TBW + (ci + 1) * 128],
                           ones128[:], f"PB{nb}", ("ckft", "ones128"))
                        vop(lambda v, q=so, b=kv: v.tensor_add(
                                S_bf[:, q:q + 128], S_bf[:, q:q + 128], PB[b][:, 0:128]),
                            (f"PB{kv}", "Sbf"), ("Sbf",))
                        vop(lambda v, q=so, b=nb: v.tensor_add(
                                Z_bf[:, q:q + 128], Z_bf[:, q:q + 128], PB[b][:, 0:128]),
                            (f"PB{nb}", "Zbf"), ("Zbf",))
            for dt in range(c.NDT if "C4" not in phases else 0):
                sl = dt % 2
                bank = 2 + sl
                gdma(wqo[sl][:, 0:c.H * 128], flat(ins["wo_pk"][dt]), f"wqo{sl}")
                for h in range(c.H):
                    mm(PB[bank][:, 0:c.TBW], wqo[sl][:, h * 128:(h + 1) * 128],
                       c_ai[:, h * c.TBW:(h + 1) * c.TBW], f"PB{bank}",
                       (f"wqo{sl}", "cai"), start=(h == 0), stop=(h == c.H - 1),
                       inc=(h == c.H - 1))
                xro = xr[:, tb * NT + dt * c.TBW:tb * NT + (dt + 1) * c.TBW]
                xio = x_bf[:, tb * NT + dt * c.TBW:tb * NT + (dt + 1) * c.TBW]
                vop(lambda v, o=xro, b=bank, i=xio:
                        v.tensor_add(o, PB[b][:, 0:c.TBW], i),
                    (f"PB{bank}", "x"), ("xr",))
            if "C5" not in phases:
                ln_stats_and_norm(tb, "xr", None, bk=(4, 5))

        def emit_gate():
            for tb in range(c.B):
                for dt in range(c.NDT):
                    mm(PB[3][0:c.E, 0:c.TBW], gate_sb[:, dt * c.E:(dt + 1) * c.E],
                       xr[:, tb * NT + dt * c.TBW:tb * NT + (dt + 1) * c.TBW], "PB3",
                       ("gate", "xr"), start=(dt == 0), stop=(dt == c.NDT - 1),
                       inc=(dt == c.NDT - 1))
                s.op("scalar", lambda a, t=tb: a.activation(
                        scor[0:c.E, t * c.TBW:(t + 1) * c.TBW], PB[3][0:c.E, 0:c.TBW],
                        AF.Sigmoid),
                     reads=("PB3",), writes=("scor",))
            for tc in range(c.B * c.CH):
                tr(PB[4][:, 0:c.E], scor[0:c.E, tc * 128:(tc + 1) * 128],
                   id_f[0:c.E, 0:c.E], "PB4", ("scor", "idf"))
                vop(lambda v: v.tensor_copy(sct[:], PB[4][:, 0:c.E]), ("PB4",), ("sct",))
                vop(lambda v: v.tensor_add(sbt[:], sct[:], biasb[:]), ("sct", "biasb"), ("sbt",))
                vop(lambda v: v.tensor_reduce(colv[:], sbt[:], AXX, ALU.max), ("sbt",), ("colv",))
                vop(lambda v: v.tensor_single_scalar(is1[:], sbt[:], colv[:, 0:1], ALU.is_ge),
                    ("sbt", "colv"), ("is1",))
                vop(lambda v: v.scalar_tensor_tensor(sbm[:], is1[:], -1e30, sbt[:],
                                                     ALU.mult, ALU.add),
                    ("is1", "sbt"), ("sbm",))
                vop(lambda v: v.tensor_reduce(colv2[:], sbm[:], AXX, ALU.max), ("sbm",), ("colv2",))
                vop(lambda v: v.tensor_single_scalar(m2t[:], sbt[:], colv2[:, 0:1], ALU.is_ge),
                    ("sbt", "colv2"), ("m2t",))
                vop(lambda v: v.tensor_mul(gtk[:], m2t[:], sct[:]), ("m2t", "sct"), ("gtk",))
                tr(PB[5][0:c.E, 0:128], gtk[:, 0:c.E], id_f[:], "PB5", ("gtk", "idf"))
                vop(lambda v, q=tc: v.tensor_copy(G_T[0:c.E, q * 128:(q + 1) * 128],
                                                  PB[5][0:c.E, 0:128]),
                    ("PB5",), ("GT",))

        s.barrier()

        HHT = c.NHT * c.B * c.TBW
        f_H = ar[:, 0:HHT]
        W1C = c.NDT * 128
        f_w1 = [ar[:, HHT + i * W1C:HHT + (i + 1) * W1C] for i in range(2)]
        f_w3 = [ar[:, HHT + 2 * W1C + i * W1C:HHT + 2 * W1C + (i + 1) * W1C] for i in range(2)]
        W2C = c.NHT * 128
        w2o = HHT + 4 * W1C
        f_w2 = [ar[:, w2o + i * W2C:w2o + (i + 1) * W2C] for i in range(2)]

        for ei in range(c.E + 1 if "F" in phases else 0):
            e = ei - 1
            routed = e >= 0
            if ei == 1 and "G" in phases:
                emit_gate()
            if routed:
                for tb in range(c.B):
                    mm(PB[5][:, 0:c.TBW], esel_sb[:, e * 128:(e + 1) * 128],
                       G_T[0:c.E, tb * c.TBW:(tb + 1) * c.TBW], "PB5",
                       ("esel", "GT"))
                    vop(lambda v, t=tb: v.tensor_copy(rsA[t][:, 0:c.TBW], PB[5][:, 0:c.TBW]),
                        ("PB5",), (f"rs0{tb}",))
            for ht in range(c.NHT):
                sl = ht % 2
                w1src = ins["w1_pk"][e, ht] if routed else ins["sw1_pk"][ht]
                w3src = ins["w3_pk"][e, ht] if routed else ins["sw3_pk"][ht]
                gdma(f_w1[sl], flat(w1src), f"w1{sl}")
                gdma(f_w3[sl], flat(w3src), f"w3{sl}")
                for tb in range(c.B):
                    bu = tb
                    bv = 2 + tb
                    x1b = xr[:, tb * NT:(tb + 1) * NT]
                    for dt in range(c.NDT):
                        mm(PB[bu][:, 0:c.TBW], f_w1[sl][:, dt * 128:(dt + 1) * 128],
                           x1b[:, dt * c.TBW:(dt + 1) * c.TBW], f"PB{bu}",
                           (f"w1{sl}", "xr"), start=(dt == 0), stop=(dt == c.NDT - 1),
                           inc=(dt == c.NDT - 1))
                    for dt in range(c.NDT):
                        mm(PB[bv][:, 0:c.TBW], f_w3[sl][:, dt * 128:(dt + 1) * 128],
                           x1b[:, dt * c.TBW:(dt + 1) * c.TBW], f"PB{bv}",
                           (f"w3{sl}", "xr"), start=(dt == 0), stop=(dt == c.NDT - 1),
                           inc=(dt == c.NDT - 1))
                    hsl = f_H[:, (ht * c.B + tb) * c.TBW:(ht * c.B + tb + 1) * c.TBW]
                    if routed:
                        vop(lambda v, o=act_s[tb], b=bu, t=tb:
                                v.tensor_mul(o[:], PB[b][:, 0:c.TBW], rsA[t][:, 0:c.TBW]),
                            (f"PB{bu}", f"rs0{tb}"), (f"act{tb}",))
                        s.op("scalar", lambda a, o=silu_s[tb], i=act_s[tb]:
                                 a.activation(o[:], i[:], AF.Silu),
                             reads=(f"act{tb}",), writes=(f"silu{tb}",))
                        vop(lambda v, o=sq_s[tb], b=bv, t=tb:
                                v.tensor_mul(o[:], PB[b][:, 0:c.TBW], rsA[t][:, 0:c.TBW]),
                            (f"PB{bv}", f"rs0{tb}"), (f"sq{tb}",))
                        vop(lambda v, o=hsl, a=silu_s[tb], b=sq_s[tb]:
                                v.tensor_mul(o, a[:], b[:]),
                            (f"silu{tb}", f"sq{tb}"), ("H",))
                    else:
                        s.op("scalar", lambda a, o=silu_s[tb], b=bu:
                                 a.activation(o[:], PB[b][:, 0:c.TBW], AF.Silu),
                             reads=(f"PB{bu}",), writes=(f"silu{tb}",))
                        vop(lambda v, o=sq_s[tb], b=bv:
                                v.tensor_copy(o[:], PB[b][:, 0:c.TBW]),
                            (f"PB{bv}",), (f"sq{tb}",))
                        vop(lambda v, o=hsl, a=silu_s[tb], b=sq_s[tb]:
                                v.tensor_mul(o, a[:], b[:]),
                            (f"silu{tb}", f"sq{tb}"), ("H",))
            def ws(tb_, dt_, sl_):
                ss = dt_ % 2
                vop(lambda v, o=rsA[ss], i=rsB[sl_]:
                        v.tensor_mul(o[:, 0:c.TBW], i[:, 0:c.TBW], rstd_b[:]),
                    (f"rs1{sl_}", "rstd"), (f"rs0{ss}",))
                sdma_out(out_dram[dt_ * 128:(dt_ + 1) * 128, tb_ * c.TBW:(tb_ + 1) * c.TBW],
                         rsA[ss][:, 0:c.TBW], f"rs0{ss}")

            last = ei == c.E and "L2" in phases
            for dt in range(c.NDT):
                sl = dt % 2
                w2src = ins["w2_pk"][e, dt] if routed else ins["sw2_pk"][dt]
                gdma(f_w2[sl], flat(w2src), f"w2{sl}")
                for tb in range(c.B):
                    bank = 4 + tb
                    for ht in range(c.NHT):
                        mm(PB[bank][:, 0:c.TBW], f_w2[sl][:, ht * 128:(ht + 1) * 128],
                           f_H[:, (ht * c.B + tb) * c.TBW:(ht * c.B + tb + 1) * c.TBW],
                           f"PB{bank}", (f"w2{sl}", "H"), start=(ht == 0),
                           stop=(ht == c.NHT - 1), inc=(ht == c.NHT - 1))
                    rsl = x_bf[:, tb * NT + dt * c.TBW:tb * NT + (dt + 1) * c.TBW]
                    if ei == 0:
                        vop(lambda v, o=rsl, b=bank: v.tensor_copy(o, PB[b][:, 0:c.TBW]),
                            (f"PB{bank}",), ("racc",))
                    else:
                        vop(lambda v, o=rsl, b=bank: v.tensor_add(o, o, PB[b][:, 0:c.TBW]),
                            (f"PB{bank}", "racc"), ("racc",))
                    if last:
                        off = tb * NT + dt * c.TBW
                        vop(lambda v, q=off: v.tensor_add(
                                xr[:, q:q + c.TBW], xr[:, q:q + c.TBW],
                                x_bf[:, q:q + c.TBW]),
                            ("xr", "racc"), ("xr",))
            if last:
                ln_stats_and_norm(0, "xr", ws)
                ln_stats_and_norm(1, "xr", ws)

        s.finish()

        block = ctx.enter_context(nc.Block())
        block.sync(lambda e: [f(e) for f in s.progs["sync"]])
        block.gpsimd(lambda e: [f(e) for f in s.progs["gpsimd"]])
        block.vector(lambda e: [f(e) for f in s.progs["vector"]])
        block.scalar(lambda e: [f(e) for f in s.progs["scalar"]])
        block.tensor(lambda e: [f(e) for f in s.progs["tensor"]])

    return nc


def _rope_tables(cfg):
    hd = cfg.HD
    inv = 1.0 / ROPE_BASE ** (np.arange(0, hd, 2, dtype=np.float64) / hd)
    t = np.arange(cfg.S, dtype=np.float64)
    fr = np.outer(t, inv)
    emb = np.concatenate([fr, fr], -1)
    cosT = np.cos(emb).T.astype(np.float32)
    sinT = np.sin(emb).T.astype(np.float32)
    sinmT = sinT.copy()
    sinmT[: hd // 2] *= -1.0
    sinm2T = np.roll(sinmT, -64, axis=0)
    return cosT, sinm2T


def _pack_weights(cfg, inp):
    c = cfg
    pk = {}
    Wq = inp["L0_attn_Wq"]
    pk["wq_pk"] = np.ascontiguousarray(
        Wq.reshape(c.H, 128, c.NDT, 128).transpose(0, 3, 2, 1))
    Wk = inp["L0_attn_Wk"]
    pk["wk_pk"] = np.ascontiguousarray(
        Wk.reshape(c.KVH, 128, c.NDT, 128).transpose(3, 2, 0, 1))
    Wv = inp["L0_attn_Wv"]
    pk["wv_pk"] = np.ascontiguousarray(
        Wv.reshape(c.KVH, 128, c.NDT, 128).transpose(3, 2, 0, 1))
    Wo = inp["L0_attn_Wo"]
    pk["wo_pk"] = np.ascontiguousarray(
        Wo.reshape(c.NDT, 128, c.H, 128).transpose(0, 3, 2, 1))
    gw = inp["L0_ffn_gate_w"]
    pk["gate_pk"] = np.ascontiguousarray(
        gw.reshape(c.E, c.NDT, 128).transpose(2, 1, 0))
    pk["bias_row"] = np.ascontiguousarray(
        inp["L0_ffn_expert_bias"].reshape(1, c.E))
    for nm, w in (("w1_pk", inp["L0_ffn_w1"]), ("w3_pk", inp["L0_ffn_w3"])):
        pk[nm] = np.ascontiguousarray(
            w.reshape(c.E, c.NHT, 128, c.NDT, 128).transpose(0, 1, 4, 3, 2))
    pk["w2_pk"] = np.ascontiguousarray(
        inp["L0_ffn_w2"].reshape(c.E, c.NDT, 128, c.NHT, 128).transpose(0, 1, 4, 3, 2))
    for nm, w in (("sw1_pk", inp["L0_ffn_sw1"]), ("sw3_pk", inp["L0_ffn_sw3"])):
        pk[nm] = np.ascontiguousarray(
            w.reshape(c.NHT, 128, c.NDT, 128).transpose(0, 3, 2, 1))
    pk["sw2_pk"] = np.ascontiguousarray(
        inp["L0_ffn_sw2"].reshape(c.NDT, 128, c.NHT, 128).transpose(0, 3, 2, 1))
    pk["id_f"] = np.eye(128, dtype=np.float32)
    r64 = np.zeros((128, 128), np.float32)
    for mm_ in range(128):
        r64[(mm_ + 64) % 128, mm_] = 1.0
    pk["r64"] = r64
    es = np.zeros((c.E, c.E * 128), np.float32)
    for e in range(c.E):
        es[e, e * 128:(e + 1) * 128] = 1.0
    pk["esel"] = es
    jj = np.arange(128)
    m1 = (jj[:, None] <= jj[None, :]).astype(np.float32)
    pk["maskg"] = np.ascontiguousarray(np.tile(m1, (1, c.GQ)))
    return pk


def _core_inputs(cfg, inp, pk, cosT, sinmT, core):
    c = cfg
    x = inp["L0_attn_x"]
    starts = [core * c.TBW, (c.NC - 1 - core) * c.TBW]
    d = {}

    def xpack(tb, t0):
        blk = x[tb, t0:t0 + c.TBW, :]
        return np.ascontiguousarray(
            blk.reshape(c.TBW, c.NDT, 128).transpose(2, 1, 0))

    d["xs"] = np.stack([xpack(tb, starts[tb]) for tb in range(c.B)])
    pre = [(0, g) for g in range(core)] + [(1, g) for g in range(c.NC - 1 - core)]
    assert len(pre) == c.NPRE
    d["xa_pre"] = (np.stack([xpack(tb, g * c.TBW) for (tb, g) in pre])
                   if pre else np.zeros((0, 128, c.NDT, c.TBW), np.float32))
    sc = np.zeros((c.NPRE, 2), np.float32)
    for i, (tb, g) in enumerate(pre):
        sc[i, tb] = 1.0
    d["scales"] = np.ascontiguousarray(
        np.broadcast_to(sc.reshape(1, -1), (128, c.NPRE * 2)).astype(np.float32))
    d["cosc"] = np.stack([np.ascontiguousarray(cosT[:, t0:t0 + c.TBW]) for t0 in starts])
    d["sinmc"] = np.stack([np.ascontiguousarray(sinmT[:, t0:t0 + c.TBW]) for t0 in starts])
    d["cospre"] = (np.stack([np.ascontiguousarray(cosT[:, g * c.TBW:(g + 1) * c.TBW])
                             for (_, g) in pre])
                   if pre else np.zeros((0, 128, c.TBW), np.float32))
    d["sinmpre"] = (np.stack([np.ascontiguousarray(sinmT[:, g * c.TBW:(g + 1) * c.TBW])
                              for (_, g) in pre])
                    if pre else np.zeros((0, 128, c.TBW), np.float32))
    d.update(pk)
    return d


def run(cfg, inp, sim=None, phases=("A", "C", "G", "F", "L2")):
    c = cfg
    nc = build_program(c, phases=phases)
    cosT, sinmT = _rope_tables(c)
    pk = _pack_weights(c, inp)
    in_maps = [_core_inputs(c, inp, pk, cosT, sinmT, core) for core in range(c.NC)]
    runner = sim or (lambda n, m: run_bass_kernel_spmd(n, m, core_ids=list(range(c.NC))))
    res = runner(nc, in_maps)
    out = np.empty((c.B, c.S, c.D), np.float32)
    for core in range(c.NC):
        o = res.results[core]["out"]
        out[0, core * c.TBW:(core + 1) * c.TBW, :] = o[:, 0:c.TBW].T
        s1 = (c.NC - 1 - core) * c.TBW
        out[1, s1:s1 + c.TBW, :] = o[:, c.TBW:].T
    return out


def kernel(**inputs):
    return run(FULL, inputs)


# version 18
# speedup vs baseline: 1.0540x; 1.0387x over previous
import contextlib
import numpy as np
import concourse.bass as bass
import concourse.mybir as mybir
from concourse.bass_utils import run_bass_kernel_spmd

F32 = mybir.dt.float32
BF16 = mybir.dt.bfloat16
AF = mybir.ActivationFunctionType
ALU = mybir.AluOpType
AXX = mybir.AxisListType.X

LN_EPS = 1e-5
ATT_EPS = 1e-6
ROPE_BASE = 10000.0


class Cfg:
    def __init__(self, NC=8, TBW=512, D=2048, H=16, KVH=4, HID=1408, E=8):
        self.NC = NC
        self.TBW = TBW
        self.D = D
        self.H = H
        self.KVH = KVH
        self.HD = 128
        self.HID = HID
        self.E = E
        self.B = 2
        self.NDT = D // 128
        self.NHT = HID // 128
        self.GQ = H // KVH
        self.CH = TBW // 128
        self.NPRE = NC - 1
        self.S = NC * TBW


FULL = Cfg()


class Sched:

    ENGS = ("sync", "gpsimd", "vector", "scalar", "tensor")

    def __init__(self, nc, ctx):
        self.nc = nc
        self.ctx = ctx
        self.sems = {e: ctx.enter_context(nc.semaphore(f"ch_{e}")) for e in self.ENGS}
        self.cnt = {e: 0 for e in self.ENGS}
        self.progs = {e: [] for e in self.ENGS}
        self.wm = {e: {} for e in self.ENGS}
        self.writer = {}
        self.readers = {}

    def _stream_sem(self, key):
        if key not in self.sems:
            self.sems[key] = self.ctx.enter_context(self.nc.semaphore(f"d_{key}"))
            self.cnt[key] = 0
        return self.sems[key]

    def _wait(self, eng, tgt, val):
        if val <= 0 or self.wm[eng].get(tgt, 0) >= val:
            return
        if tgt == eng and val > self.cnt.get(eng, 0):
            return
        assert val <= self.cnt.get(tgt, 0), (
            f"{eng} would wait on {tgt}>={val} before that inc is emitted "
            f"(cnt={self.cnt.get(tgt, 0)}): open accumulation group spans a "
            f"buffer refill -> deadlock")
        self.wm[eng][tgt] = val
        sem = self.sems[tgt]
        self.progs[eng].append(lambda e, s=sem, v=val: e.wait_ge(s, v))

    def _deps(self, eng, reads, writes):
        for t in reads:
            w = self.writer.get(t)
            if w is not None:
                self._wait(eng, w[0], w[1])
        for t in writes:
            w = self.writer.get(t)
            if w is not None:
                self._wait(eng, w[0], w[1])
            for (re, rv) in self.readers.get(t, ()):
                self._wait(eng, re, rv)

    def _record(self, stream, val, reads, writes):
        for t in reads:
            self.readers.setdefault(t, []).append((stream, val))
        for t in writes:
            self.writer[t] = (stream, val)
            self.readers[t] = []

    def op(self, eng, fn, reads=(), writes=(), inc=True):
        self._deps(eng, reads, writes)
        if inc:
            self.cnt[eng] += 1
            val = self.cnt[eng]
            sem = self.sems[eng]
            self.progs[eng].append(lambda e, f=fn, s=sem: f(e).then_inc(s, 1))
        else:
            val = self.cnt[eng] + 1
            self.progs[eng].append(lambda e, f=fn: f(e))
        self._record(eng, val, reads, writes)

    def dma(self, eng, fn, key, reads=(), writes=()):
        self._deps(eng, reads, writes)
        sem = self._stream_sem(key)
        self.cnt[key] += 16
        val = self.cnt[key]
        self.progs[eng].append(lambda e, f=fn, s=sem: f(e).then_inc(s, 16))
        self._record(key, val, reads, writes)

    def counts(self):
        return dict(self.cnt)

    def wait_counts(self, eng, snap):
        for tgt, v in snap.items():
            if tgt != eng:
                self._wait(eng, tgt, v)

    def barrier(self, engines=None):
        engines = engines or self.ENGS
        snap = self.counts()
        for e in engines:
            self.wait_counts(e, snap)

    def finish(self):
        self.wait_counts("sync", self.counts())
        v = self.cnt["sync"]
        if v:
            sem = self.sems["sync"]
            self.progs["sync"].append(lambda e, s=sem, vv=v: e.wait_ge(s, vv))


def build_program(cfg, phases=("A", "C", "G", "F", "L2")):
    c = cfg
    nc = bass.Bass("TRN2", target_bir_lowering=False)

    dr = lambda name, shp: nc.dram_tensor(name, list(shp), F32, kind="ExternalInput").ap()
    ins = {}
    ins["xs"] = dr("xs", (c.B, 128, c.NDT, c.TBW))
    ins["xa_pre"] = dr("xa_pre", (c.NPRE, 128, c.NDT, c.TBW))
    ins["scales"] = dr("scales", (128, c.NPRE * 2))
    ins["cosc"] = dr("cosc", (c.B, 128, c.TBW))
    ins["sinmc"] = dr("sinmc", (c.B, 128, c.TBW))
    ins["cospre"] = dr("cospre", (c.NPRE, 128, c.TBW))
    ins["sinmpre"] = dr("sinmpre", (c.NPRE, 128, c.TBW))
    ins["wq_pk"] = dr("wq_pk", (c.H, 128, c.NDT, 128))
    ins["wk_pk"] = dr("wk_pk", (128, c.NDT, c.KVH, 128))
    ins["wv_pk"] = dr("wv_pk", (128, c.NDT, c.KVH, 128))
    ins["wo_pk"] = dr("wo_pk", (c.NDT, 128, c.H, 128))
    ins["gate_pk"] = dr("gate_pk", (128, c.NDT, c.E))
    ins["bias_row"] = dr("bias_row", (1, c.E))
    ins["w1_pk"] = dr("w1_pk", (c.E, c.NHT, 128, c.NDT, 128))
    ins["w3_pk"] = dr("w3_pk", (c.E, c.NHT, 128, c.NDT, 128))
    ins["w2_pk"] = dr("w2_pk", (c.E, c.NDT, 128, c.NHT, 128))
    ins["sw1_pk"] = dr("sw1_pk", (c.NHT, 128, c.NDT, 128))
    ins["sw3_pk"] = dr("sw3_pk", (c.NHT, 128, c.NDT, 128))
    ins["sw2_pk"] = dr("sw2_pk", (c.NDT, 128, c.NHT, 128))
    ins["id_f"] = dr("id_f", (128, 128))
    ins["r64"] = dr("r64", (128, 128))
    ins["esel"] = dr("esel", (c.E, c.E * 128))
    ins["maskg"] = dr("maskg", (128, c.GQ * 128))

    out_dram = nc.dram_tensor("out", [c.D, c.B * c.TBW], F32, kind="ExternalOutput").ap()

    with contextlib.ExitStack() as ctx:
        sbuf = lambda n, cols, dt: ctx.enter_context(nc.sbuf_tensor(n, [128, cols], dt))
        psum = lambda n, cols, dt: ctx.enter_context(nc.psum_tensor(n, [128, cols], dt))

        NT = c.NDT * c.TBW
        AR = max(2 * NT + 4 * c.TBW,
                 2 * c.H * c.TBW + 4 * c.KVH * c.TBW,
                 c.NHT * c.B * c.TBW + 4 * c.NDT * 128 + 2 * c.NHT * 128)

        x_bf = sbuf("x_bf", c.B * NT, BF16)
        xr = sbuf("xr", c.B * NT, BF16)
        ar = sbuf("ar", AR, BF16)
        wk_sb = sbuf("wk_sb", c.NDT * c.KVH * 128, BF16)
        wv_sb = sbuf("wv_sb", c.NDT * c.KVH * 128, BF16)
        wqo = [sbuf(f"wqo{i}", max(c.NDT, c.H) * 128, BF16) for i in range(2)]
        gate_sb = sbuf("gate_sb", c.NDT * c.E, BF16)
        cosc_sb = sbuf("cosc_sb", c.B * c.TBW, BF16)
        sinmc_sb = sbuf("sinmc_sb", c.B * c.TBW, BF16)
        cpre = [sbuf("cpre0", c.TBW, BF16)]
        spre = [sbuf("spre0", c.TBW, BF16)]
        SCR = max(c.TBW, c.GQ * 128)
        rsA = [sbuf(f"rsA{i}", SCR, F32) for i in range(2)]
        rsB = [sbuf(f"rsB{i}", SCR, F32) for i in range(2)]
        act_s = [sbuf(f"act{i}", c.TBW, F32) for i in range(2)]
        silu_s = [sbuf(f"silu{i}", c.TBW, BF16) for i in range(2)]
        sq_s = [sbuf(f"sq{i}", c.TBW, BF16) for i in range(2)]
        S_bf = sbuf("S_bf", c.B * c.KVH * 128, BF16)
        Z_bf = sbuf("Z_bf", c.B * c.KVH * 128, BF16)
        m_b = sbuf("m_b", c.TBW, F32)
        rstd_b = sbuf("rstd_b", c.TBW, F32)
        G_T = ctx.enter_context(nc.sbuf_tensor("G_T", [c.E, c.B * c.TBW], F32))
        scor = ctx.enter_context(nc.sbuf_tensor("scor", [c.E, c.B * c.TBW], F32))
        amst = sbuf("amst", c.KVH * c.GQ * 128, BF16)
        sct = sbuf("sct", c.E, F32)
        sbt = sbuf("sbt", c.E, F32)
        is1 = sbuf("is1", c.E, F32)
        sbm = sbuf("sbm", c.E, F32)
        m2t = sbuf("m2t", c.E, F32)
        gtk = sbuf("gtk", c.E, F32)
        colv = sbuf("colv", 1, F32)
        colv2 = sbuf("colv2", 1, F32)
        id_f = sbuf("id_f_sb", 128, F32)
        id_bf = sbuf("id_bf", 128, BF16)
        r64_bf = sbuf("r64_bf", 128, BF16)
        maskg_sb = sbuf("maskg_sb", c.GQ * 128, F32)
        ones128 = sbuf("ones128", 128, BF16)
        onescol = sbuf("onescol", 1, BF16)
        ones_row_f = ctx.enter_context(nc.sbuf_tensor("ones_row_f", [1, 128], F32))
        epsc = sbuf("epsc", 1, F32)
        biasb = sbuf("biasb", c.E, F32)
        bias_row = ctx.enter_context(nc.sbuf_tensor("bias_row_sb", [1, c.E], F32))
        esel_sb = ctx.enter_context(nc.sbuf_tensor("esel_sb", [c.E, c.E * 128], F32))
        scales_sb = sbuf("scales_sb", c.NPRE * 2, F32)

        PB = [psum(f"PB_{i}", c.TBW, F32) for i in range(6)]
        PT = [ctx.enter_context(nc.psum_tensor(f"PT_{i}", [128, 1024], BF16))
              for i in range(2)]

        s = Sched(nc, ctx)

        def mm(out_ap, lhs_ap, rhs_ap, bank, reads, start=True, stop=True, inc=True):
            s.op("tensor",
                 lambda t, o=out_ap, l=lhs_ap, r=rhs_ap, a=start, b=stop:
                     t.matmul(o, l, r, start=a, stop=b),
                 reads=reads, writes=(bank,), inc=inc)

        def tr(out_ap, in_ap, ident_ap, bank, reads, inc=True):
            s.op("tensor",
                 lambda t, o=out_ap, i=in_ap, d=ident_ap: t.transpose(o, i, d),
                 reads=reads, writes=(bank,), inc=inc)

        def vop(fn, reads, writes, eng="vector", inc=True):
            s.op(eng, fn, reads=reads, writes=writes, inc=inc)

        def gdma(out_ap, in_ap, wtok):
            s.dma("gpsimd", lambda g, o=out_ap, i=in_ap: g.dma_start(out=o, in_=i),
                  key="L" + wtok, writes=(wtok,))

        def sdma_in(out_ap, in_ap, wtok):
            s.dma("sync", lambda g, o=out_ap, i=in_ap: g.dma_start(out=o, in_=i),
                  key="L" + wtok, writes=(wtok,))

        def sdma_out(out_ap, in_ap, rtok):
            s.dma("sync", lambda g, o=out_ap, i=in_ap: g.dma_start(out=o, in_=i),
                  key="S" + rtok, reads=(rtok,))

        flat = lambda ap: ap.rearrange("p a b -> p (a b)")

        sdma_in(id_f[:], ins["id_f"][:, :], "idf")
        sdma_in(maskg_sb[:], ins["maskg"][:, :], "maskg")
        sdma_in(scales_sb[:], ins["scales"][:, :], "scales")
        sdma_in(bias_row[:], ins["bias_row"][:, :], "bias_row")
        sdma_in(esel_sb[:], ins["esel"][:, :], "esel")
        gdma(id_bf[:], ins["id_f"][:, :], "id_bf")
        gdma(r64_bf[:], ins["r64"][:, :], "r64")
        gdma(wk_sb[:], ins["wk_pk"].rearrange("p a b c -> p (a b c)"), "wk")
        gdma(wv_sb[:], ins["wv_pk"].rearrange("p a b c -> p (a b c)"), "wv")

        def emit_own_loads():
            for tb in range(c.B):
                gdma(cosc_sb[:, tb * c.TBW:(tb + 1) * c.TBW], ins["cosc"][tb], "cosc")
                gdma(sinmc_sb[:, tb * c.TBW:(tb + 1) * c.TBW], ins["sinmc"][tb], "sinmc")
            for tb in range(c.B):
                gdma(x_bf[:, tb * NT:(tb + 1) * NT], flat(ins["xs"][tb]), "x")
            gdma(gate_sb[:], flat(ins["gate_pk"]), "gate")
        vop(lambda v: v.memset(ones128[:], 1.0), (), ("ones128",))
        vop(lambda v: v.memset(onescol[:], 1.0), (), ("onescol",))
        vop(lambda v: v.memset(ones_row_f[:], 1.0), (), ("onesrow",))
        vop(lambda v: v.memset(epsc[:], ATT_EPS), (), ("epsc",))
        vop(lambda v: v.memset(S_bf[:], 0.0), (), ("Sbf",))
        vop(lambda v: v.memset(Z_bf[:], 0.0), (), ("Zbf",))
        mm(PB[5][:, 0:c.E], ones_row_f[0:1, :], bias_row[0:1, :], "PB5",
           ("onesrow", "bias_row"))
        vop(lambda v: v.tensor_copy(biasb[:], PB[5][:, 0:c.E]), ("PB5",), ("biasb",))

        def rope_elu(pb_ap, bank_tok, cos_ap, sin_ap, costoks, dst_ap, dsttok, pipe,
                     rot_bank=2):
            t1, qr, acts = (rsA if pipe == 0 else rsB)[0], \
                           (rsA if pipe == 0 else rsB)[1], act_s[pipe]
            tok1, tokq, toka = (f"rs{pipe}0", f"rs{pipe}1", f"act{pipe}")
            W = dst_ap.shape[-1]
            vop(lambda v, o=t1, p=pb_ap, sn=sin_ap:
                    v.tensor_mul(o[0:64, 0:W], p[64:128, :], sn[64:128, :]),
                (bank_tok,) + costoks, (tok1,), inc=False)
            vop(lambda v, o=t1, p=pb_ap, sn=sin_ap:
                    v.tensor_mul(o[64:128, 0:W], p[0:64, :], sn[0:64, :]),
                (bank_tok,) + costoks, (tok1,))
            vop(lambda v, o=qr, p=pb_ap, cs=cos_ap:
                    v.tensor_mul(o[:, 0:W], p[:], cs[:]),
                (bank_tok,) + costoks, (tokq,))
            vop(lambda v, o=qr, i=t1: v.tensor_add(o[:, 0:W], o[:, 0:W], i[:, 0:W]),
                (tok1, tokq), (tokq,))
            s.op("scalar", lambda a, o=acts, i=qr:
                     a.activation(o[:, 0:W], i[:, 0:W], AF.Relu, scale=-1.0),
                 reads=(tokq,), writes=(toka,))
            s.op("scalar", lambda a, o=acts:
                     a.activation(o[:, 0:W], o[:, 0:W], AF.Exp, scale=-1.0),
                 reads=(toka,), writes=(toka,))
            vop(lambda v, o=dst_ap, i=qr, e=acts:
                    v.scalar_tensor_tensor(o, i[:, 0:W], 0.0, e[:, 0:W], ALU.max, ALU.add),
                (tokq, toka), (dsttok,))

        a_xg = [ar[:, i * NT:(i + 1) * NT] for i in range(2)]
        a_kf = ar[:, 2 * NT:2 * NT + c.TBW]
        a_v = ar[:, 2 * NT + c.TBW:2 * NT + 2 * c.TBW]
        a_kft = ar[:, 2 * NT + 2 * c.TBW:2 * NT + 3 * c.TBW]
        a_vt = ar[:, 2 * NT + 3 * c.TBW:2 * NT + 4 * c.TBW]

        for gi in range(c.NPRE if "A" in phases else 0):
            sl = gi % 2
            tsl = 0
            gdma(a_xg[sl], flat(ins["xa_pre"][gi]), f"xg{sl}")
            gdma(cpre[tsl][:], ins["cospre"][gi], f"cp{tsl}")
            gdma(spre[tsl][:], ins["sinmpre"][gi], f"sp{tsl}")
            if gi == 0:
                emit_own_loads()
            for kv in range(c.KVH):
                pk_, pv_ = 2 * (kv % 2), 2 * (kv % 2) + 1
                pipe = kv % 2
                for dt in range(c.NDT):
                    mm(PB[pk_][:, 0:c.TBW], wk_sb[:, (dt * c.KVH + kv) * 128:(dt * c.KVH + kv + 1) * 128],
                       a_xg[sl][:, dt * c.TBW:(dt + 1) * c.TBW], f"PB{pk_}",
                       ("wk", f"xg{sl}"), start=(dt == 0), stop=(dt == c.NDT - 1),
                       inc=(dt == c.NDT - 1))
                for dt in range(c.NDT):
                    mm(PB[pv_][:, 0:c.TBW], wv_sb[:, (dt * c.KVH + kv) * 128:(dt * c.KVH + kv + 1) * 128],
                       a_xg[sl][:, dt * c.TBW:(dt + 1) * c.TBW], f"PB{pv_}",
                       ("wv", f"xg{sl}"), start=(dt == 0), stop=(dt == c.NDT - 1),
                       inc=(dt == c.NDT - 1))
                rope_elu(PB[pk_][:, 0:c.TBW], f"PB{pk_}", cpre[tsl][:], spre[tsl][:],
                         (f"cp{tsl}", f"sp{tsl}"), a_kf[:, 0:c.TBW], "akf", pipe,
                         rot_bank=4 + pipe)
                vop(lambda v, o=a_v, p=PB[pv_]: v.tensor_copy(o[:, 0:c.TBW], p[:, 0:c.TBW]),
                    (f"PB{pv_}",), ("av",))
                for ci in range(c.CH):
                    tr(PT[0][:, 512 + ci * 128:512 + (ci + 1) * 128], a_kf[:, ci * 128:(ci + 1) * 128],
                       id_bf[:], "PT0", ("akf", "id_bf"), inc=(ci == c.CH - 1))
                    tr(PT[1][:, 512 + ci * 128:512 + (ci + 1) * 128], a_v[:, ci * 128:(ci + 1) * 128],
                       id_bf[:], "PT1", ("av", "id_bf"), inc=(ci == c.CH - 1))
                vop(lambda v, o=a_kft, p=PT[0]: v.tensor_copy(o[:, 0:c.TBW], p[:, 512:512 + c.TBW]),
                    ("PT0",), ("akft",))
                vop(lambda v, o=a_vt, p=PT[1]: v.tensor_copy(o[:, 0:c.TBW], p[:, 512:512 + c.TBW]),
                    ("PT1",), ("avt",))
                for ci in range(c.CH):
                    mm(PB[4][:, 0:128], a_kft[:, ci * 128:(ci + 1) * 128],
                       a_vt[:, ci * 128:(ci + 1) * 128], "PB4", ("akft", "avt"),
                       start=(ci == 0), stop=(ci == c.CH - 1), inc=(ci == c.CH - 1))
                for ci in range(c.CH):
                    mm(PB[5][:, 0:128], a_kft[:, ci * 128:(ci + 1) * 128],
                       ones128[:], "PB5", ("akft", "ones128"),
                       start=(ci == 0), stop=(ci == c.CH - 1), inc=(ci == c.CH - 1))
                for tb in range(c.B):
                    scl = scales_sb[:, gi * 2 + tb:gi * 2 + tb + 1]
                    so = (tb * c.KVH + kv) * 128
                    vop(lambda v, o=S_bf, p=PB[4], sc=scl, q=so:
                            v.scalar_tensor_tensor(o[:, q:q + 128], p[:, 0:128], sc,
                                                   o[:, q:q + 128], ALU.mult, ALU.add),
                        ("PB4", "scales", "Sbf"), ("Sbf",))
                    vop(lambda v, o=Z_bf, p=PB[5], sc=scl, q=so:
                            v.scalar_tensor_tensor(o[:, q:q + 128], p[:, 0:128], sc,
                                                   o[:, q:q + 128], ALU.mult, ALU.add),
                        ("PB5", "scales", "Zbf"), ("Zbf",))
        if "A" not in phases or c.NPRE == 0:
            emit_own_loads()
        s.barrier()

        HT = c.H * c.TBW
        c_qf = ar[:, 0:HT]
        c_ai = ar[:, HT:2 * HT]
        KT = c.KVH * c.TBW
        c_kf = ar[:, 2 * HT:2 * HT + KT]
        c_v = ar[:, 2 * HT + KT:2 * HT + 2 * KT]
        c_kft = ar[:, 2 * HT + 2 * KT:2 * HT + 3 * KT]
        c_vt = ar[:, 2 * HT + 3 * KT:2 * HT + 4 * KT]

        def ln_stats_and_norm(tb, src_tok, write_stage, bk=(0, 1)):
            xrb = xr[:, tb * NT:(tb + 1) * NT]
            for dt in range(c.NDT):
                sl = dt % 2
                s.op("scalar", lambda a, o=sq_s[sl], i=xrb, d=dt:
                         a.activation(o[:], i[:, d * c.TBW:(d + 1) * c.TBW], AF.Square),
                     reads=(src_tok,), writes=(f"sq{sl}",))
                mm(PB[bk[0]][:, 0:c.TBW], ones128[:], xrb[:, dt * c.TBW:(dt + 1) * c.TBW],
                   f"PB{bk[0]}", ("ones128", src_tok), start=(dt == 0), stop=(dt == c.NDT - 1))
                mm(PB[bk[1]][:, 0:c.TBW], ones128[:], sq_s[sl][:], f"PB{bk[1]}",
                   ("ones128", f"sq{sl}"), start=(dt == 0), stop=(dt == c.NDT - 1))
            inv_d = 1.0 / c.D
            vop(lambda v: v.tensor_scalar_mul(m_b[:], PB[bk[0]][:, 0:c.TBW], inv_d),
                (f"PB{bk[0]}",), ("m",))
            vop(lambda v: v.tensor_mul(rsA[0][:, 0:c.TBW], m_b[:], m_b[:]), ("m",), ("rs00",))
            vop(lambda v: v.scalar_tensor_tensor(rsA[1][:, 0:c.TBW], PB[bk[1]][:, 0:c.TBW], inv_d,
                                                 rsA[0][:, 0:c.TBW], ALU.mult, ALU.subtract),
                (f"PB{bk[1]}", "rs00"), ("rs01",))
            vop(lambda v: v.tensor_scalar_add(rsA[1][:, 0:c.TBW], rsA[1][:, 0:c.TBW], LN_EPS),
                ("rs01",), ("rs01",))
            vop(lambda v: v.reciprocal(rsA[1][:, 0:c.TBW], rsA[1][:, 0:c.TBW]), ("rs01",), ("rs01",))
            s.op("scalar", lambda a: a.activation(rstd_b[:], rsA[1][:, 0:c.TBW], AF.Sqrt),
                 reads=("rs01",), writes=("rstd",))
            for dt in range(c.NDT):
                sl = dt % 2
                xt = xrb[:, dt * c.TBW:(dt + 1) * c.TBW]
                vop(lambda v, o=rsB[sl], i=xt: v.tensor_sub(o[:, 0:c.TBW], i, m_b[:]),
                    (src_tok, "m"), (f"rs1{sl}",))
                if write_stage is None:
                    vop(lambda v, o=xt, i=rsB[sl]: v.tensor_mul(o, i[:, 0:c.TBW], rstd_b[:]),
                        (f"rs1{sl}", "rstd", src_tok), (src_tok,))
                else:
                    write_stage(tb, dt, sl)

        for tb in range(c.B if "C" in phases else 0):
            xb = x_bf[:, tb * NT:(tb + 1) * NT]
            cosb = cosc_sb[:, tb * c.TBW:(tb + 1) * c.TBW]
            sinb = sinmc_sb[:, tb * c.TBW:(tb + 1) * c.TBW]
            for kv in range(c.KVH if "C1" not in phases else 0):
                for dt in range(c.NDT):
                    mm(PB[0][:, 0:c.TBW], wk_sb[:, (dt * c.KVH + kv) * 128:(dt * c.KVH + kv + 1) * 128],
                       xb[:, dt * c.TBW:(dt + 1) * c.TBW], "PB0", ("wk", "x"),
                       start=(dt == 0), stop=(dt == c.NDT - 1), inc=(dt == c.NDT - 1))
                for dt in range(c.NDT):
                    mm(PB[1][:, 0:c.TBW], wv_sb[:, (dt * c.KVH + kv) * 128:(dt * c.KVH + kv + 1) * 128],
                       xb[:, dt * c.TBW:(dt + 1) * c.TBW], "PB1", ("wv", "x"),
                       start=(dt == 0), stop=(dt == c.NDT - 1), inc=(dt == c.NDT - 1))
                kfs = c_kf[:, kv * c.TBW:(kv + 1) * c.TBW]
                rope_elu(PB[0][:, 0:c.TBW], "PB0", cosb, sinb, ("cosc", "sinmc"),
                         kfs, "ckf", kv % 2, rot_bank=2 + kv % 2)
                vop(lambda v, o=c_v, p=PB[1], q=kv:
                        v.tensor_copy(o[:, q * c.TBW:(q + 1) * c.TBW], p[:, 0:c.TBW]),
                    ("PB1",), ("cv",))
                for ci in range(c.CH):
                    off = kv * c.TBW + ci * 128
                    tr(PT[0][:, 512 + ci * 128:512 + (ci + 1) * 128], c_kf[:, off:off + 128],
                       id_bf[:], "PT0", ("ckf", "id_bf"), inc=(ci == c.CH - 1))
                    tr(PT[1][:, 512 + ci * 128:512 + (ci + 1) * 128], c_v[:, off:off + 128],
                       id_bf[:], "PT1", ("cv", "id_bf"), inc=(ci == c.CH - 1))
                vop(lambda v, o=c_kft, p=PT[0], q=kv:
                        v.tensor_copy(o[:, q * c.TBW:(q + 1) * c.TBW], p[:, 512:512 + c.TBW]),
                    ("PT0",), ("ckft",))
                vop(lambda v, o=c_vt, p=PT[1], q=kv:
                        v.tensor_copy(o[:, q * c.TBW:(q + 1) * c.TBW], p[:, 512:512 + c.TBW]),
                    ("PT1",), ("cvt",))
            for h in range(c.H if "C2" not in phases else 0):
                sl = h % 2
                bank = sl
                gdma(wqo[sl][:, 0:c.NDT * 128], flat(ins["wq_pk"][h]), f"wqo{sl}")
                for dt in range(c.NDT):
                    mm(PB[bank][:, 0:c.TBW], wqo[sl][:, dt * 128:(dt + 1) * 128],
                       xb[:, dt * c.TBW:(dt + 1) * c.TBW], f"PB{bank}",
                       (f"wqo{sl}", "x"), start=(dt == 0), stop=(dt == c.NDT - 1),
                       inc=(dt == c.NDT - 1))
                rope_elu(PB[bank][:, 0:c.TBW], f"PB{bank}", cosb, sinb,
                         ("cosc", "sinmc"), c_qf[:, h * c.TBW:(h + 1) * c.TBW], "cqf",
                         sl, rot_bank=2 + sl)
            GW = c.GQ * 128
            denr_sl = [rsA[0], rsA[1], act_s[0], act_s[1]]
            denr_tk = ["rs00", "rs01", "act0", "act1"]
            for ci in range(c.CH if "C3" not in phases else 0):
                for kv in range(c.KVH):
                    for hl in range(c.GQ):
                        h = kv * c.GQ + hl
                        mm(PB[kv][:, hl * 128:(hl + 1) * 128],
                           c_kf[:, kv * c.TBW + ci * 128:kv * c.TBW + (ci + 1) * 128],
                           c_qf[:, h * c.TBW + ci * 128:h * c.TBW + (ci + 1) * 128],
                           f"PB{kv}", ("ckf", "cqf"), inc=(hl == c.GQ - 1))
                for kv in range(c.KVH):
                    vop(lambda v, q=kv: v.tensor_mul(amst[:, q * GW:(q + 1) * GW],
                                                     PB[q][:, 0:GW], maskg_sb[:]),
                        (f"PB{kv}", "maskg"), (f"amst{kv}",))
                for kv in range(c.KVH):
                    so = (tb * c.KVH + kv) * 128
                    for hl in range(c.GQ):
                        h = kv * c.GQ + hl
                        mm(PB[kv][0:1, hl * 128:(hl + 1) * 128], onescol[:],
                           amst[:, kv * GW + hl * 128:kv * GW + (hl + 1) * 128],
                           f"PB{kv}", ("onescol", f"amst{kv}"), start=True,
                           stop=False, inc=False)
                        mm(PB[kv][0:1, hl * 128:(hl + 1) * 128], Z_bf[:, so:so + 1],
                           c_qf[:, h * c.TBW + ci * 128:h * c.TBW + (ci + 1) * 128],
                           f"PB{kv}", ("Zbf", "cqf"), start=False, stop=True,
                           inc=(hl == c.GQ - 1))
                for kv in range(c.KVH):
                    s.op("scalar", lambda a, d=denr_sl[kv], q=kv: a.activation(
                            d[0:1, 0:GW], PB[q][0:1, 0:GW], AF.Ln,
                            bias=epsc[0:1, 0:1]),
                         reads=(f"PB{kv}", "epsc"), writes=(denr_tk[kv],))
                    s.op("scalar", lambda a, d=denr_sl[kv]: a.activation(
                            d[0:1, 0:GW], d[0:1, 0:GW], AF.Exp, scale=-1.0),
                         reads=(denr_tk[kv],), writes=(denr_tk[kv],))
                for pair in range(0, c.KVH, 2):
                    kvs = [k for k in (pair, pair + 1) if k < c.KVH]
                    for kv in kvs:
                        so = (tb * c.KVH + kv) * 128
                        nb = 4 + (kv % 2)
                        for hl in range(c.GQ):
                            h = kv * c.GQ + hl
                            mm(PB[nb][:, hl * 128:(hl + 1) * 128],
                               c_vt[:, kv * c.TBW + ci * 128:kv * c.TBW + (ci + 1) * 128],
                               amst[:, kv * GW + hl * 128:kv * GW + (hl + 1) * 128],
                               f"PB{nb}", ("cvt", f"amst{kv}"), start=True,
                               stop=False, inc=False)
                            mm(PB[nb][:, hl * 128:(hl + 1) * 128],
                               S_bf[:, so:so + 128],
                               c_qf[:, h * c.TBW + ci * 128:h * c.TBW + (ci + 1) * 128],
                               f"PB{nb}", ("Sbf", "cqf"), start=False, stop=True,
                               inc=(hl == c.GQ - 1))
                    for kv in kvs:
                        for hl in range(c.GQ):
                            mm(PB[kv][:, hl * 128:(hl + 1) * 128], ones_row_f[0:1, :],
                               denr_sl[kv][0:1, hl * 128:(hl + 1) * 128], f"PB{kv}",
                               ("onesrow", denr_tk[kv]), inc=(hl == c.GQ - 1))
                    for kv in kvs:
                        nb = 4 + (kv % 2)
                        vop(lambda v, q=kv: v.tensor_copy(rsB[q % 2][:, 0:GW],
                                                          PB[q][:, 0:GW]),
                            (f"PB{kv}",), (f"rs1{kv % 2}",))
                        ai3 = c_ai.rearrange("p (h t) -> p h t", h=c.H)[
                            :, kv * c.GQ:(kv + 1) * c.GQ, ci * 128:(ci + 1) * 128]
                        n3 = PB[nb][:, 0:GW].rearrange("p (h t) -> p h t", h=c.GQ)
                        d3 = rsB[kv % 2][:, 0:GW].rearrange("p (h t) -> p h t", h=c.GQ)
                        vop(lambda v, o=ai3, a=n3, b=d3: v.tensor_mul(o, a, b),
                            (f"PB{nb}", f"rs1{kv % 2}"), ("cai",))
                if ci < c.CH - 1:
                    for kv in range(c.KVH):
                        so = (tb * c.KVH + kv) * 128
                        nb = 4 + (kv % 2)
                        mm(PB[kv][:, 0:128],
                           c_kft[:, kv * c.TBW + ci * 128:kv * c.TBW + (ci + 1) * 128],
                           c_vt[:, kv * c.TBW + ci * 128:kv * c.TBW + (ci + 1) * 128],
                           f"PB{kv}", ("ckft", "cvt"))
                        mm(PB[nb][:, 0:128],
                           c_kft[:, kv * c.TBW + ci * 128:kv * c.TBW + (ci + 1) * 128],
                           ones128[:], f"PB{nb}", ("ckft", "ones128"))
                        vop(lambda v, q=so, b=kv: v.tensor_add(
                                S_bf[:, q:q + 128], S_bf[:, q:q + 128], PB[b][:, 0:128]),
                            (f"PB{kv}", "Sbf"), ("Sbf",))
                        vop(lambda v, q=so, b=nb: v.tensor_add(
                                Z_bf[:, q:q + 128], Z_bf[:, q:q + 128], PB[b][:, 0:128]),
                            (f"PB{nb}", "Zbf"), ("Zbf",))
            for dt in range(c.NDT if "C4" not in phases else 0):
                sl = dt % 2
                bank = 2 + sl
                gdma(wqo[sl][:, 0:c.H * 128], flat(ins["wo_pk"][dt]), f"wqo{sl}")
                for h in range(c.H):
                    mm(PB[bank][:, 0:c.TBW], wqo[sl][:, h * 128:(h + 1) * 128],
                       c_ai[:, h * c.TBW:(h + 1) * c.TBW], f"PB{bank}",
                       (f"wqo{sl}", "cai"), start=(h == 0), stop=(h == c.H - 1),
                       inc=(h == c.H - 1))
                xro = xr[:, tb * NT + dt * c.TBW:tb * NT + (dt + 1) * c.TBW]
                xio = x_bf[:, tb * NT + dt * c.TBW:tb * NT + (dt + 1) * c.TBW]
                vop(lambda v, o=xro, b=bank, i=xio:
                        v.tensor_add(o, PB[b][:, 0:c.TBW], i),
                    (f"PB{bank}", "x"), (f"xr{tb}",))
            if tb == c.B - 1:
                snap_attn = s.counts()
            if "C5" not in phases:
                ln_stats_and_norm(tb, f"xr{tb}", None, bk=(4, 5))

        def emit_gate():
            for tb in range(c.B):
                for dt in range(c.NDT):
                    mm(PB[3][0:c.E, 0:c.TBW], gate_sb[:, dt * c.E:(dt + 1) * c.E],
                       xr[:, tb * NT + dt * c.TBW:tb * NT + (dt + 1) * c.TBW], "PB3",
                       ("gate", f"xr{tb}"), start=(dt == 0), stop=(dt == c.NDT - 1),
                       inc=(dt == c.NDT - 1))
                s.op("scalar", lambda a, t=tb: a.activation(
                        scor[0:c.E, t * c.TBW:(t + 1) * c.TBW], PB[3][0:c.E, 0:c.TBW],
                        AF.Sigmoid),
                     reads=("PB3",), writes=("scor",))
            for tc in range(c.B * c.CH):
                tr(PB[4][:, 0:c.E], scor[0:c.E, tc * 128:(tc + 1) * 128],
                   id_f[0:c.E, 0:c.E], "PB4", ("scor", "idf"))
                vop(lambda v: v.tensor_copy(sct[:], PB[4][:, 0:c.E]), ("PB4",), ("sct",))
                vop(lambda v: v.tensor_add(sbt[:], sct[:], biasb[:]), ("sct", "biasb"), ("sbt",))
                vop(lambda v: v.tensor_reduce(colv[:], sbt[:], AXX, ALU.max), ("sbt",), ("colv",))
                vop(lambda v: v.tensor_single_scalar(is1[:], sbt[:], colv[:, 0:1], ALU.is_ge),
                    ("sbt", "colv"), ("is1",))
                vop(lambda v: v.scalar_tensor_tensor(sbm[:], is1[:], -1e30, sbt[:],
                                                     ALU.mult, ALU.add),
                    ("is1", "sbt"), ("sbm",))
                vop(lambda v: v.tensor_reduce(colv2[:], sbm[:], AXX, ALU.max), ("sbm",), ("colv2",))
                vop(lambda v: v.tensor_single_scalar(m2t[:], sbt[:], colv2[:, 0:1], ALU.is_ge),
                    ("sbt", "colv2"), ("m2t",))
                vop(lambda v: v.tensor_mul(gtk[:], m2t[:], sct[:]), ("m2t", "sct"), ("gtk",))
                tr(PB[5][0:c.E, 0:128], gtk[:, 0:c.E], id_f[:], "PB5", ("gtk", "idf"))
                vop(lambda v, q=tc: v.tensor_copy(G_T[0:c.E, q * 128:(q + 1) * 128],
                                                  PB[5][0:c.E, 0:128]),
                    ("PB5",), ("GT",))

        s.wait_counts("gpsimd", snap_attn)

        HHT = c.NHT * c.B * c.TBW
        f_H = ar[:, 0:HHT]
        W1C = c.NDT * 128
        f_w1 = [ar[:, HHT + i * W1C:HHT + (i + 1) * W1C] for i in range(2)]
        f_w3 = [ar[:, HHT + 2 * W1C + i * W1C:HHT + 2 * W1C + (i + 1) * W1C] for i in range(2)]
        W2C = c.NHT * 128
        w2o = HHT + 4 * W1C
        f_w2 = [ar[:, w2o + i * W2C:w2o + (i + 1) * W2C] for i in range(2)]

        for ei in range(c.E + 1 if "F" in phases else 0):
            e = ei - 1
            routed = e >= 0
            if routed:
                for tb in range(c.B):
                    mm(PB[5][:, 0:c.TBW], esel_sb[:, e * 128:(e + 1) * 128],
                       G_T[0:c.E, tb * c.TBW:(tb + 1) * c.TBW], "PB5",
                       ("esel", "GT"))
                    vop(lambda v, t=tb: v.tensor_copy(rsA[t][:, 0:c.TBW], PB[5][:, 0:c.TBW]),
                        ("PB5",), (f"rs0{tb}",))

            def h_phase(tbs, w_off):
                for ht in range(c.NHT):
                    sl = (ht + w_off) % 2
                    w1src = ins["w1_pk"][e, ht] if routed else ins["sw1_pk"][ht]
                    w3src = ins["w3_pk"][e, ht] if routed else ins["sw3_pk"][ht]
                    gdma(f_w1[sl], flat(w1src), f"w1{sl}")
                    gdma(f_w3[sl], flat(w3src), f"w3{sl}")
                    for tb in tbs:
                        bu = tb
                        bv = 2 + tb
                        x1b = xr[:, tb * NT:(tb + 1) * NT]
                        for dt in range(c.NDT):
                            mm(PB[bu][:, 0:c.TBW], f_w1[sl][:, dt * 128:(dt + 1) * 128],
                               x1b[:, dt * c.TBW:(dt + 1) * c.TBW], f"PB{bu}",
                               (f"w1{sl}", f"xr{tb}"), start=(dt == 0),
                               stop=(dt == c.NDT - 1), inc=(dt == c.NDT - 1))
                        for dt in range(c.NDT):
                            mm(PB[bv][:, 0:c.TBW], f_w3[sl][:, dt * 128:(dt + 1) * 128],
                               x1b[:, dt * c.TBW:(dt + 1) * c.TBW], f"PB{bv}",
                               (f"w3{sl}", f"xr{tb}"), start=(dt == 0),
                               stop=(dt == c.NDT - 1), inc=(dt == c.NDT - 1))
                        hsl = f_H[:, (ht * c.B + tb) * c.TBW:(ht * c.B + tb + 1) * c.TBW]
                        if routed:
                            vop(lambda v, o=act_s[tb], b=bu, t=tb:
                                    v.tensor_mul(o[:], PB[b][:, 0:c.TBW], rsA[t][:, 0:c.TBW]),
                                (f"PB{bu}", f"rs0{tb}"), (f"act{tb}",))
                            s.op("scalar", lambda a, o=silu_s[tb], i=act_s[tb]:
                                     a.activation(o[:], i[:], AF.Silu),
                                 reads=(f"act{tb}",), writes=(f"silu{tb}",))
                            vop(lambda v, o=sq_s[tb], b=bv, t=tb:
                                    v.tensor_mul(o[:], PB[b][:, 0:c.TBW], rsA[t][:, 0:c.TBW]),
                                (f"PB{bv}", f"rs0{tb}"), (f"sq{tb}",))
                            vop(lambda v, o=hsl, a=silu_s[tb], b=sq_s[tb]:
                                    v.tensor_mul(o, a[:], b[:]),
                                (f"silu{tb}", f"sq{tb}"), ("H",))
                        else:
                            s.op("scalar", lambda a, o=silu_s[tb], b=bu:
                                     a.activation(o[:], PB[b][:, 0:c.TBW], AF.Silu),
                                 reads=(f"PB{bu}",), writes=(f"silu{tb}",))
                            vop(lambda v, o=sq_s[tb], b=bv:
                                    v.tensor_copy(o[:], PB[b][:, 0:c.TBW]),
                                (f"PB{bv}",), (f"sq{tb}",))
                            vop(lambda v, o=hsl, a=silu_s[tb], b=sq_s[tb]:
                                    v.tensor_mul(o, a[:], b[:]),
                                (f"silu{tb}", f"sq{tb}"), ("H",))

            if not routed:
                h_phase([0], 0)
                if "G" in phases:
                    emit_gate()
                h_phase([1], c.NHT)
            else:
                h_phase([0, 1], 0)
            def ws(tb_, dt_, sl_):
                ss = dt_ % 2
                vop(lambda v, o=rsA[ss], i=rsB[sl_]:
                        v.tensor_mul(o[:, 0:c.TBW], i[:, 0:c.TBW], rstd_b[:]),
                    (f"rs1{sl_}", "rstd"), (f"rs0{ss}",))
                sdma_out(out_dram[dt_ * 128:(dt_ + 1) * 128, tb_ * c.TBW:(tb_ + 1) * c.TBW],
                         rsA[ss][:, 0:c.TBW], f"rs0{ss}")

            last = ei == c.E and "L2" in phases
            for dt in range(c.NDT):
                sl = dt % 2
                w2src = ins["w2_pk"][e, dt] if routed else ins["sw2_pk"][dt]
                gdma(f_w2[sl], flat(w2src), f"w2{sl}")
                for tb in range(c.B):
                    bank = 4 + tb
                    for ht in range(c.NHT):
                        mm(PB[bank][:, 0:c.TBW], f_w2[sl][:, ht * 128:(ht + 1) * 128],
                           f_H[:, (ht * c.B + tb) * c.TBW:(ht * c.B + tb + 1) * c.TBW],
                           f"PB{bank}", (f"w2{sl}", "H"), start=(ht == 0),
                           stop=(ht == c.NHT - 1), inc=(ht == c.NHT - 1))
                    rsl = x_bf[:, tb * NT + dt * c.TBW:tb * NT + (dt + 1) * c.TBW]
                    if ei == 0:
                        vop(lambda v, o=rsl, b=bank: v.tensor_copy(o, PB[b][:, 0:c.TBW]),
                            (f"PB{bank}",), ("racc",))
                    else:
                        vop(lambda v, o=rsl, b=bank: v.tensor_add(o, o, PB[b][:, 0:c.TBW]),
                            (f"PB{bank}", "racc"), ("racc",))
                    if last:
                        off = tb * NT + dt * c.TBW
                        vop(lambda v, q=off: v.tensor_add(
                                xr[:, q:q + c.TBW], xr[:, q:q + c.TBW],
                                x_bf[:, q:q + c.TBW]),
                            (f"xr{tb}", "racc"), (f"xr{tb}",))
            if last:
                ln_stats_and_norm(0, "xr0", ws)
                ln_stats_and_norm(1, "xr1", ws)

        s.finish()

        block = ctx.enter_context(nc.Block())
        block.sync(lambda e: [f(e) for f in s.progs["sync"]])
        block.gpsimd(lambda e: [f(e) for f in s.progs["gpsimd"]])
        block.vector(lambda e: [f(e) for f in s.progs["vector"]])
        block.scalar(lambda e: [f(e) for f in s.progs["scalar"]])
        block.tensor(lambda e: [f(e) for f in s.progs["tensor"]])

    return nc


def _rope_tables(cfg):
    hd = cfg.HD
    inv = 1.0 / ROPE_BASE ** (np.arange(0, hd, 2, dtype=np.float64) / hd)
    t = np.arange(cfg.S, dtype=np.float64)
    fr = np.outer(t, inv)
    emb = np.concatenate([fr, fr], -1)
    cosT = np.cos(emb).T.astype(np.float32)
    sinT = np.sin(emb).T.astype(np.float32)
    sinmT = sinT.copy()
    sinmT[: hd // 2] *= -1.0
    sinm2T = np.roll(sinmT, -64, axis=0)
    return cosT, sinm2T


def _pack_weights(cfg, inp):
    c = cfg
    pk = {}
    Wq = inp["L0_attn_Wq"]
    pk["wq_pk"] = np.ascontiguousarray(
        Wq.reshape(c.H, 128, c.NDT, 128).transpose(0, 3, 2, 1))
    Wk = inp["L0_attn_Wk"]
    pk["wk_pk"] = np.ascontiguousarray(
        Wk.reshape(c.KVH, 128, c.NDT, 128).transpose(3, 2, 0, 1))
    Wv = inp["L0_attn_Wv"]
    pk["wv_pk"] = np.ascontiguousarray(
        Wv.reshape(c.KVH, 128, c.NDT, 128).transpose(3, 2, 0, 1))
    Wo = inp["L0_attn_Wo"]
    pk["wo_pk"] = np.ascontiguousarray(
        Wo.reshape(c.NDT, 128, c.H, 128).transpose(0, 3, 2, 1))
    gw = inp["L0_ffn_gate_w"]
    pk["gate_pk"] = np.ascontiguousarray(
        gw.reshape(c.E, c.NDT, 128).transpose(2, 1, 0))
    pk["bias_row"] = np.ascontiguousarray(
        inp["L0_ffn_expert_bias"].reshape(1, c.E))
    for nm, w in (("w1_pk", inp["L0_ffn_w1"]), ("w3_pk", inp["L0_ffn_w3"])):
        pk[nm] = np.ascontiguousarray(
            w.reshape(c.E, c.NHT, 128, c.NDT, 128).transpose(0, 1, 4, 3, 2))
    pk["w2_pk"] = np.ascontiguousarray(
        inp["L0_ffn_w2"].reshape(c.E, c.NDT, 128, c.NHT, 128).transpose(0, 1, 4, 3, 2))
    for nm, w in (("sw1_pk", inp["L0_ffn_sw1"]), ("sw3_pk", inp["L0_ffn_sw3"])):
        pk[nm] = np.ascontiguousarray(
            w.reshape(c.NHT, 128, c.NDT, 128).transpose(0, 3, 2, 1))
    pk["sw2_pk"] = np.ascontiguousarray(
        inp["L0_ffn_sw2"].reshape(c.NDT, 128, c.NHT, 128).transpose(0, 3, 2, 1))
    pk["id_f"] = np.eye(128, dtype=np.float32)
    r64 = np.zeros((128, 128), np.float32)
    for mm_ in range(128):
        r64[(mm_ + 64) % 128, mm_] = 1.0
    pk["r64"] = r64
    es = np.zeros((c.E, c.E * 128), np.float32)
    for e in range(c.E):
        es[e, e * 128:(e + 1) * 128] = 1.0
    pk["esel"] = es
    jj = np.arange(128)
    m1 = (jj[:, None] <= jj[None, :]).astype(np.float32)
    pk["maskg"] = np.ascontiguousarray(np.tile(m1, (1, c.GQ)))
    return pk


def _core_inputs(cfg, inp, pk, cosT, sinmT, core):
    c = cfg
    x = inp["L0_attn_x"]
    starts = [core * c.TBW, (c.NC - 1 - core) * c.TBW]
    d = {}

    def xpack(tb, t0):
        blk = x[tb, t0:t0 + c.TBW, :]
        return np.ascontiguousarray(
            blk.reshape(c.TBW, c.NDT, 128).transpose(2, 1, 0))

    d["xs"] = np.stack([xpack(tb, starts[tb]) for tb in range(c.B)])
    pre = [(0, g) for g in range(core)] + [(1, g) for g in range(c.NC - 1 - core)]
    assert len(pre) == c.NPRE
    d["xa_pre"] = (np.stack([xpack(tb, g * c.TBW) for (tb, g) in pre])
                   if pre else np.zeros((0, 128, c.NDT, c.TBW), np.float32))
    sc = np.zeros((c.NPRE, 2), np.float32)
    for i, (tb, g) in enumerate(pre):
        sc[i, tb] = 1.0
    d["scales"] = np.ascontiguousarray(
        np.broadcast_to(sc.reshape(1, -1), (128, c.NPRE * 2)).astype(np.float32))
    d["cosc"] = np.stack([np.ascontiguousarray(cosT[:, t0:t0 + c.TBW]) for t0 in starts])
    d["sinmc"] = np.stack([np.ascontiguousarray(sinmT[:, t0:t0 + c.TBW]) for t0 in starts])
    d["cospre"] = (np.stack([np.ascontiguousarray(cosT[:, g * c.TBW:(g + 1) * c.TBW])
                             for (_, g) in pre])
                   if pre else np.zeros((0, 128, c.TBW), np.float32))
    d["sinmpre"] = (np.stack([np.ascontiguousarray(sinmT[:, g * c.TBW:(g + 1) * c.TBW])
                              for (_, g) in pre])
                    if pre else np.zeros((0, 128, c.TBW), np.float32))
    d.update(pk)
    return d


def run(cfg, inp, sim=None, phases=("A", "C", "G", "F", "L2")):
    c = cfg
    nc = build_program(c, phases=phases)
    cosT, sinmT = _rope_tables(c)
    pk = _pack_weights(c, inp)
    in_maps = [_core_inputs(c, inp, pk, cosT, sinmT, core) for core in range(c.NC)]
    runner = sim or (lambda n, m: run_bass_kernel_spmd(n, m, core_ids=list(range(c.NC))))
    res = runner(nc, in_maps)
    out = np.empty((c.B, c.S, c.D), np.float32)
    for core in range(c.NC):
        o = res.results[core]["out"]
        out[0, core * c.TBW:(core + 1) * c.TBW, :] = o[:, 0:c.TBW].T
        s1 = (c.NC - 1 - core) * c.TBW
        out[1, s1:s1 + c.TBW, :] = o[:, c.TBW:].T
    return out


def kernel(**inputs):
    return run(FULL, inputs)


# version 19
# speedup vs baseline: 1.0666x; 1.0120x over previous
import contextlib
import numpy as np
import concourse.bass as bass
import concourse.mybir as mybir
from concourse.bass_utils import run_bass_kernel_spmd

F32 = mybir.dt.float32
BF16 = mybir.dt.bfloat16
AF = mybir.ActivationFunctionType
ALU = mybir.AluOpType
AXX = mybir.AxisListType.X

LN_EPS = 1e-5
ATT_EPS = 1e-6
ROPE_BASE = 10000.0


class Cfg:
    def __init__(self, NC=8, TBW=512, D=2048, H=16, KVH=4, HID=1408, E=8):
        self.NC = NC
        self.TBW = TBW
        self.D = D
        self.H = H
        self.KVH = KVH
        self.HD = 128
        self.HID = HID
        self.E = E
        self.B = 2
        self.NDT = D // 128
        self.NHT = HID // 128
        self.GQ = H // KVH
        self.CH = TBW // 128
        self.NPRE = NC - 1
        self.S = NC * TBW


FULL = Cfg()


class Sched:

    ENGS = ("sync", "gpsimd", "vector", "scalar", "tensor")

    def __init__(self, nc, ctx):
        self.nc = nc
        self.ctx = ctx
        self.sems = {e: ctx.enter_context(nc.semaphore(f"ch_{e}")) for e in self.ENGS}
        self.cnt = {e: 0 for e in self.ENGS}
        self.progs = {e: [] for e in self.ENGS}
        self.wm = {e: {} for e in self.ENGS}
        self.writer = {}
        self.readers = {}

    def _stream_sem(self, key):
        if key not in self.sems:
            self.sems[key] = self.ctx.enter_context(self.nc.semaphore(f"d_{key}"))
            self.cnt[key] = 0
        return self.sems[key]

    def _wait(self, eng, tgt, val):
        if val <= 0 or self.wm[eng].get(tgt, 0) >= val:
            return
        if tgt == eng and val > self.cnt.get(eng, 0):
            return
        assert val <= self.cnt.get(tgt, 0), (
            f"{eng} would wait on {tgt}>={val} before that inc is emitted "
            f"(cnt={self.cnt.get(tgt, 0)}): open accumulation group spans a "
            f"buffer refill -> deadlock")
        self.wm[eng][tgt] = val
        sem = self.sems[tgt]
        self.progs[eng].append(lambda e, s=sem, v=val: e.wait_ge(s, v))

    def _deps(self, eng, reads, writes):
        for t in reads:
            w = self.writer.get(t)
            if w is not None:
                self._wait(eng, w[0], w[1])
        for t in writes:
            w = self.writer.get(t)
            if w is not None:
                self._wait(eng, w[0], w[1])
            for (re, rv) in self.readers.get(t, ()):
                self._wait(eng, re, rv)

    def _record(self, stream, val, reads, writes):
        for t in reads:
            self.readers.setdefault(t, []).append((stream, val))
        for t in writes:
            self.writer[t] = (stream, val)
            self.readers[t] = []

    def op(self, eng, fn, reads=(), writes=(), inc=True):
        self._deps(eng, reads, writes)
        if inc:
            self.cnt[eng] += 1
            val = self.cnt[eng]
            sem = self.sems[eng]
            self.progs[eng].append(lambda e, f=fn, s=sem: f(e).then_inc(s, 1))
        else:
            val = self.cnt[eng] + 1
            self.progs[eng].append(lambda e, f=fn: f(e))
        self._record(eng, val, reads, writes)

    def dma(self, eng, fn, key, reads=(), writes=()):
        self._deps(eng, reads, writes)
        sem = self._stream_sem(key)
        self.cnt[key] += 16
        val = self.cnt[key]
        self.progs[eng].append(lambda e, f=fn, s=sem: f(e).then_inc(s, 16))
        self._record(key, val, reads, writes)

    def counts(self):
        return dict(self.cnt)

    def wait_counts(self, eng, snap):
        for tgt, v in snap.items():
            if tgt != eng:
                self._wait(eng, tgt, v)

    def barrier(self, engines=None):
        engines = engines or self.ENGS
        snap = self.counts()
        for e in engines:
            self.wait_counts(e, snap)

    def finish(self):
        self.wait_counts("sync", self.counts())
        v = self.cnt["sync"]
        if v:
            sem = self.sems["sync"]
            self.progs["sync"].append(lambda e, s=sem, vv=v: e.wait_ge(s, vv))


def build_program(cfg, phases=("A", "C", "G", "F", "L2")):
    c = cfg
    nc = bass.Bass("TRN2", target_bir_lowering=False)

    dr = lambda name, shp: nc.dram_tensor(name, list(shp), F32, kind="ExternalInput").ap()
    ins = {}
    ins["xs"] = dr("xs", (c.B, 128, c.NDT, c.TBW))
    ins["xa_pre"] = dr("xa_pre", (c.NPRE, 128, c.NDT, c.TBW))
    ins["scales"] = dr("scales", (128, c.NPRE * 2))
    ins["cosc"] = dr("cosc", (c.B, 128, c.TBW))
    ins["sinmc"] = dr("sinmc", (c.B, 128, c.TBW))
    ins["cospre"] = dr("cospre", (c.NPRE, 128, c.TBW))
    ins["sinmpre"] = dr("sinmpre", (c.NPRE, 128, c.TBW))
    ins["wq_pk"] = dr("wq_pk", (c.H, 128, c.NDT, 128))
    ins["wk_pk"] = dr("wk_pk", (128, c.NDT, c.KVH, 128))
    ins["wv_pk"] = dr("wv_pk", (128, c.NDT, c.KVH, 128))
    ins["wo_pk"] = dr("wo_pk", (c.NDT, 128, c.H, 128))
    ins["gate_pk"] = dr("gate_pk", (128, c.NDT, c.E))
    ins["bias_row"] = dr("bias_row", (1, c.E))
    ins["w1_pk"] = dr("w1_pk", (c.E, c.NHT, 128, c.NDT, 128))
    ins["w3_pk"] = dr("w3_pk", (c.E, c.NHT, 128, c.NDT, 128))
    ins["w2_pk"] = dr("w2_pk", (c.E, c.NDT, 128, c.NHT, 128))
    ins["sw1_pk"] = dr("sw1_pk", (c.NHT, 128, c.NDT, 128))
    ins["sw3_pk"] = dr("sw3_pk", (c.NHT, 128, c.NDT, 128))
    ins["sw2_pk"] = dr("sw2_pk", (c.NDT, 128, c.NHT, 128))
    ins["id_f"] = dr("id_f", (128, 128))
    ins["r64"] = dr("r64", (128, 128))
    ins["esel"] = dr("esel", (c.E, c.E * 128))
    ins["maskg"] = dr("maskg", (128, c.GQ * 128))

    out_dram = nc.dram_tensor("out", [c.D, c.B * c.TBW], F32, kind="ExternalOutput").ap()

    with contextlib.ExitStack() as ctx:
        sbuf = lambda n, cols, dt: ctx.enter_context(nc.sbuf_tensor(n, [128, cols], dt))
        psum = lambda n, cols, dt: ctx.enter_context(nc.psum_tensor(n, [128, cols], dt))

        NT = c.NDT * c.TBW
        AR = max(2 * NT + 4 * c.TBW,
                 2 * c.H * c.TBW + 4 * c.KVH * c.TBW,
                 c.NHT * c.B * c.TBW + 4 * c.NDT * 128 + 2 * c.NHT * 128)

        x_bf = sbuf("x_bf", c.B * NT, BF16)
        xr = sbuf("xr", c.B * NT, BF16)
        ar = sbuf("ar", AR, BF16)
        wk_sb = sbuf("wk_sb", c.NDT * c.KVH * 128, BF16)
        wv_sb = sbuf("wv_sb", c.NDT * c.KVH * 128, BF16)
        wqo = [sbuf(f"wqo{i}", max(c.NDT, c.H) * 128, BF16) for i in range(2)]
        gate_sb = sbuf("gate_sb", c.NDT * c.E, BF16)
        cosc_sb = sbuf("cosc_sb", c.B * c.TBW, BF16)
        sinmc_sb = sbuf("sinmc_sb", c.B * c.TBW, BF16)
        cpre = [sbuf("cpre0", c.TBW, BF16)]
        spre = [sbuf("spre0", c.TBW, BF16)]
        SCR = max(c.TBW, c.GQ * 128)
        rsA = [sbuf(f"rsA{i}", SCR, F32) for i in range(2)]
        rsB = [sbuf(f"rsB{i}", SCR, F32) for i in range(2)]
        act_s = [sbuf(f"act{i}", c.TBW, F32) for i in range(2)]
        silu_s = [sbuf(f"silu{i}", c.TBW, BF16) for i in range(2)]
        sq_s = [sbuf(f"sq{i}", c.TBW, BF16) for i in range(2)]
        S_bf = sbuf("S_bf", c.B * c.KVH * 128, BF16)
        Z_bf = sbuf("Z_bf", c.B * c.KVH * 128, BF16)
        m_b = sbuf("m_b", c.TBW, F32)
        rstd_b = sbuf("rstd_b", c.TBW, F32)
        G_T = ctx.enter_context(nc.sbuf_tensor("G_T", [c.E, c.B * c.TBW], F32))
        scor = ctx.enter_context(nc.sbuf_tensor("scor", [c.E, c.B * c.TBW], F32))
        amst = sbuf("amst", c.KVH * c.GQ * 128, BF16)
        sct = sbuf("sct", c.E, F32)
        sbt = sbuf("sbt", c.E, F32)
        is1 = sbuf("is1", c.E, F32)
        sbm = sbuf("sbm", c.E, F32)
        m2t = sbuf("m2t", c.E, F32)
        gtk = sbuf("gtk", c.E, F32)
        colv = sbuf("colv", 1, F32)
        colv2 = sbuf("colv2", 1, F32)
        id_f = sbuf("id_f_sb", 128, F32)
        id_bf = sbuf("id_bf", 128, BF16)
        r64_bf = sbuf("r64_bf", 128, BF16)
        maskg_sb = sbuf("maskg_sb", c.GQ * 128, F32)
        ones128 = sbuf("ones128", 128, BF16)
        onescol = sbuf("onescol", 1, BF16)
        ones_row_f = ctx.enter_context(nc.sbuf_tensor("ones_row_f", [1, 128], F32))
        epsc = sbuf("epsc", 1, F32)
        biasb = sbuf("biasb", c.E, F32)
        bias_row = ctx.enter_context(nc.sbuf_tensor("bias_row_sb", [1, c.E], F32))
        esel_sb = ctx.enter_context(nc.sbuf_tensor("esel_sb", [c.E, c.E * 128], F32))
        scales_sb = sbuf("scales_sb", c.NPRE * 2, F32)

        PB = [psum(f"PB_{i}", c.TBW, F32) for i in range(6)]
        PT = [ctx.enter_context(nc.psum_tensor(f"PT_{i}", [128, 1024], BF16))
              for i in range(2)]

        s = Sched(nc, ctx)

        def mm(out_ap, lhs_ap, rhs_ap, bank, reads, start=True, stop=True, inc=True):
            s.op("tensor",
                 lambda t, o=out_ap, l=lhs_ap, r=rhs_ap, a=start, b=stop:
                     t.matmul(o, l, r, start=a, stop=b),
                 reads=reads, writes=(bank,), inc=inc)

        def tr(out_ap, in_ap, ident_ap, bank, reads, inc=True):
            s.op("tensor",
                 lambda t, o=out_ap, i=in_ap, d=ident_ap: t.transpose(o, i, d),
                 reads=reads, writes=(bank,), inc=inc)

        def vop(fn, reads, writes, eng="vector", inc=True):
            s.op(eng, fn, reads=reads, writes=writes, inc=inc)

        def gdma(out_ap, in_ap, wtok):
            s.dma("gpsimd", lambda g, o=out_ap, i=in_ap: g.dma_start(out=o, in_=i),
                  key="L" + wtok, writes=(wtok,))

        def sdma_in(out_ap, in_ap, wtok):
            s.dma("sync", lambda g, o=out_ap, i=in_ap: g.dma_start(out=o, in_=i),
                  key="L" + wtok, writes=(wtok,))

        def sdma_out(out_ap, in_ap, rtok):
            s.dma("sync", lambda g, o=out_ap, i=in_ap: g.dma_start(out=o, in_=i),
                  key="S" + rtok, reads=(rtok,))

        flat = lambda ap: ap.rearrange("p a b -> p (a b)")

        sdma_in(id_f[:], ins["id_f"][:, :], "idf")
        sdma_in(maskg_sb[:], ins["maskg"][:, :], "maskg")
        sdma_in(scales_sb[:], ins["scales"][:, :], "scales")
        sdma_in(bias_row[:], ins["bias_row"][:, :], "bias_row")
        sdma_in(esel_sb[:], ins["esel"][:, :], "esel")
        gdma(id_bf[:], ins["id_f"][:, :], "id_bf")
        gdma(r64_bf[:], ins["r64"][:, :], "r64")
        gdma(wk_sb[:], ins["wk_pk"].rearrange("p a b c -> p (a b c)"), "wk")
        gdma(wv_sb[:], ins["wv_pk"].rearrange("p a b c -> p (a b c)"), "wv")

        def emit_own_loads():
            for tb in range(c.B):
                gdma(cosc_sb[:, tb * c.TBW:(tb + 1) * c.TBW], ins["cosc"][tb], "cosc")
                gdma(sinmc_sb[:, tb * c.TBW:(tb + 1) * c.TBW], ins["sinmc"][tb], "sinmc")
            for tb in range(c.B):
                gdma(x_bf[:, tb * NT:(tb + 1) * NT], flat(ins["xs"][tb]), "x")
            gdma(gate_sb[:], flat(ins["gate_pk"]), "gate")
        vop(lambda v: v.memset(ones128[:], 1.0), (), ("ones128",))
        vop(lambda v: v.memset(onescol[:], 1.0), (), ("onescol",))
        vop(lambda v: v.memset(ones_row_f[:], 1.0), (), ("onesrow",))
        vop(lambda v: v.memset(epsc[:], ATT_EPS), (), ("epsc",))
        vop(lambda v: v.memset(S_bf[:], 0.0), (), ("Sbf",))
        vop(lambda v: v.memset(Z_bf[:], 0.0), (), ("Zbf",))
        mm(PB[5][:, 0:c.E], ones_row_f[0:1, :], bias_row[0:1, :], "PB5",
           ("onesrow", "bias_row"))
        vop(lambda v: v.tensor_copy(biasb[:], PB[5][:, 0:c.E]), ("PB5",), ("biasb",))

        def rope_elu(pb_ap, bank_tok, cos_ap, sin_ap, costoks, dst_ap, dsttok, pipe,
                     rot_bank=2):
            t1, qr, acts = (rsA if pipe == 0 else rsB)[0], \
                           (rsA if pipe == 0 else rsB)[1], act_s[pipe]
            tok1, tokq, toka = (f"rs{pipe}0", f"rs{pipe}1", f"act{pipe}")
            W = dst_ap.shape[-1]
            vop(lambda v, o=t1, p=pb_ap, sn=sin_ap:
                    v.tensor_mul(o[0:64, 0:W], p[64:128, :], sn[64:128, :]),
                (bank_tok,) + costoks, (tok1,), inc=False)
            vop(lambda v, o=t1, p=pb_ap, sn=sin_ap:
                    v.tensor_mul(o[64:128, 0:W], p[0:64, :], sn[0:64, :]),
                (bank_tok,) + costoks, (tok1,))
            vop(lambda v, o=qr, p=pb_ap, cs=cos_ap:
                    v.tensor_mul(o[:, 0:W], p[:], cs[:]),
                (bank_tok,) + costoks, (tokq,))
            vop(lambda v, o=qr, i=t1: v.tensor_add(o[:, 0:W], o[:, 0:W], i[:, 0:W]),
                (tok1, tokq), (tokq,))
            s.op("scalar", lambda a, o=acts, i=qr:
                     a.activation(o[:, 0:W], i[:, 0:W], AF.Relu, scale=-1.0),
                 reads=(tokq,), writes=(toka,))
            s.op("scalar", lambda a, o=acts:
                     a.activation(o[:, 0:W], o[:, 0:W], AF.Exp, scale=-1.0),
                 reads=(toka,), writes=(toka,))
            vop(lambda v, o=dst_ap, i=qr, e=acts:
                    v.scalar_tensor_tensor(o, i[:, 0:W], 0.0, e[:, 0:W], ALU.max, ALU.add),
                (tokq, toka), (dsttok,))

        a_xg = [ar[:, i * NT:(i + 1) * NT] for i in range(2)]
        a_kf = ar[:, 2 * NT:2 * NT + c.TBW]
        a_v = ar[:, 2 * NT + c.TBW:2 * NT + 2 * c.TBW]
        a_kft = ar[:, 2 * NT + 2 * c.TBW:2 * NT + 3 * c.TBW]
        a_vt = ar[:, 2 * NT + 3 * c.TBW:2 * NT + 4 * c.TBW]

        for gi in range(c.NPRE if "A" in phases else 0):
            sl = gi % 2
            tsl = 0
            gdma(a_xg[sl], flat(ins["xa_pre"][gi]), f"xg{sl}")
            gdma(cpre[tsl][:], ins["cospre"][gi], f"cp{tsl}")
            gdma(spre[tsl][:], ins["sinmpre"][gi], f"sp{tsl}")
            if gi == 0:
                emit_own_loads()
            for kv in range(c.KVH):
                pk_, pv_ = 2 * (kv % 2), 2 * (kv % 2) + 1
                pipe = kv % 2
                for dt in range(c.NDT):
                    mm(PB[pk_][:, 0:c.TBW], wk_sb[:, (dt * c.KVH + kv) * 128:(dt * c.KVH + kv + 1) * 128],
                       a_xg[sl][:, dt * c.TBW:(dt + 1) * c.TBW], f"PB{pk_}",
                       ("wk", f"xg{sl}"), start=(dt == 0), stop=(dt == c.NDT - 1),
                       inc=(dt == c.NDT - 1))
                for dt in range(c.NDT):
                    mm(PB[pv_][:, 0:c.TBW], wv_sb[:, (dt * c.KVH + kv) * 128:(dt * c.KVH + kv + 1) * 128],
                       a_xg[sl][:, dt * c.TBW:(dt + 1) * c.TBW], f"PB{pv_}",
                       ("wv", f"xg{sl}"), start=(dt == 0), stop=(dt == c.NDT - 1),
                       inc=(dt == c.NDT - 1))
                rope_elu(PB[pk_][:, 0:c.TBW], f"PB{pk_}", cpre[tsl][:], spre[tsl][:],
                         (f"cp{tsl}", f"sp{tsl}"), a_kf[:, 0:c.TBW], "akf", pipe,
                         rot_bank=4 + pipe)
                vop(lambda v, o=a_v, p=PB[pv_]: v.tensor_copy(o[:, 0:c.TBW], p[:, 0:c.TBW]),
                    (f"PB{pv_}",), ("av",))
                for ci in range(c.CH):
                    tr(PT[0][:, 512 + ci * 128:512 + (ci + 1) * 128], a_kf[:, ci * 128:(ci + 1) * 128],
                       id_bf[:], "PT0", ("akf", "id_bf"), inc=(ci == c.CH - 1))
                    tr(PT[1][:, 512 + ci * 128:512 + (ci + 1) * 128], a_v[:, ci * 128:(ci + 1) * 128],
                       id_bf[:], "PT1", ("av", "id_bf"), inc=(ci == c.CH - 1))
                vop(lambda v, o=a_kft, p=PT[0]: v.tensor_copy(o[:, 0:c.TBW], p[:, 512:512 + c.TBW]),
                    ("PT0",), ("akft",))
                vop(lambda v, o=a_vt, p=PT[1]: v.tensor_copy(o[:, 0:c.TBW], p[:, 512:512 + c.TBW]),
                    ("PT1",), ("avt",))
                for ci in range(c.CH):
                    mm(PB[4][:, 0:128], a_kft[:, ci * 128:(ci + 1) * 128],
                       a_vt[:, ci * 128:(ci + 1) * 128], "PB4", ("akft", "avt"),
                       start=(ci == 0), stop=(ci == c.CH - 1), inc=(ci == c.CH - 1))
                for ci in range(c.CH):
                    mm(PB[5][:, 0:128], a_kft[:, ci * 128:(ci + 1) * 128],
                       ones128[:], "PB5", ("akft", "ones128"),
                       start=(ci == 0), stop=(ci == c.CH - 1), inc=(ci == c.CH - 1))
                for tb in range(c.B):
                    scl = scales_sb[:, gi * 2 + tb:gi * 2 + tb + 1]
                    so = (tb * c.KVH + kv) * 128
                    vop(lambda v, o=S_bf, p=PB[4], sc=scl, q=so:
                            v.scalar_tensor_tensor(o[:, q:q + 128], p[:, 0:128], sc,
                                                   o[:, q:q + 128], ALU.mult, ALU.add),
                        ("PB4", "scales", "Sbf"), ("Sbf",))
                    vop(lambda v, o=Z_bf, p=PB[5], sc=scl, q=so:
                            v.scalar_tensor_tensor(o[:, q:q + 128], p[:, 0:128], sc,
                                                   o[:, q:q + 128], ALU.mult, ALU.add),
                        ("PB5", "scales", "Zbf"), ("Zbf",))
        if "A" not in phases or c.NPRE == 0:
            emit_own_loads()
        s.barrier()

        HT = c.H * c.TBW
        c_qf = ar[:, 0:HT]
        c_ai = ar[:, HT:2 * HT]
        KT = c.KVH * c.TBW
        c_kf = ar[:, 2 * HT:2 * HT + KT]
        c_v = ar[:, 2 * HT + KT:2 * HT + 2 * KT]
        c_kft = ar[:, 2 * HT + 2 * KT:2 * HT + 3 * KT]
        c_vt = ar[:, 2 * HT + 3 * KT:2 * HT + 4 * KT]

        def ln_sums_and_stats(tb, src_tok, bk=(0, 1)):
            xrb = xr[:, tb * NT:(tb + 1) * NT]
            for dt in range(c.NDT):
                sl = dt % 2
                s.op("scalar", lambda a, o=sq_s[sl], i=xrb, d=dt:
                         a.activation(o[:], i[:, d * c.TBW:(d + 1) * c.TBW], AF.Square),
                     reads=(src_tok,), writes=(f"sq{sl}",))
                mm(PB[bk[0]][:, 0:c.TBW], ones128[:], xrb[:, dt * c.TBW:(dt + 1) * c.TBW],
                   f"PB{bk[0]}", ("ones128", src_tok), start=(dt == 0), stop=(dt == c.NDT - 1))
                mm(PB[bk[1]][:, 0:c.TBW], ones128[:], sq_s[sl][:], f"PB{bk[1]}",
                   ("ones128", f"sq{sl}"), start=(dt == 0), stop=(dt == c.NDT - 1))
            inv_d = 1.0 / c.D
            vop(lambda v: v.tensor_scalar_mul(m_b[:], PB[bk[0]][:, 0:c.TBW], inv_d),
                (f"PB{bk[0]}",), ("m",))
            vop(lambda v: v.tensor_mul(rsA[0][:, 0:c.TBW], m_b[:], m_b[:]), ("m",), ("rs00",))
            vop(lambda v: v.scalar_tensor_tensor(rsA[1][:, 0:c.TBW], PB[bk[1]][:, 0:c.TBW], inv_d,
                                                 rsA[0][:, 0:c.TBW], ALU.mult, ALU.subtract),
                (f"PB{bk[1]}", "rs00"), ("rs01",))
            vop(lambda v: v.tensor_scalar_add(rsA[1][:, 0:c.TBW], rsA[1][:, 0:c.TBW], LN_EPS),
                ("rs01",), ("rs01",))
            vop(lambda v: v.reciprocal(rsA[1][:, 0:c.TBW], rsA[1][:, 0:c.TBW]), ("rs01",), ("rs01",))
            s.op("scalar", lambda a: a.activation(rstd_b[:], rsA[1][:, 0:c.TBW], AF.Sqrt),
                 reads=("rs01",), writes=("rstd",))

        def ln_norm_one(tb, src_tok, write_stage, dt):
            xrb = xr[:, tb * NT:(tb + 1) * NT]
            sl = dt % 2
            xt = xrb[:, dt * c.TBW:(dt + 1) * c.TBW]
            vop(lambda v, o=rsB[sl], i=xt: v.tensor_sub(o[:, 0:c.TBW], i, m_b[:]),
                (src_tok, "m"), (f"rs1{sl}",))
            if write_stage is None:
                vop(lambda v, o=xt, i=rsB[sl]: v.tensor_mul(o, i[:, 0:c.TBW], rstd_b[:]),
                    (f"rs1{sl}", "rstd", src_tok), (src_tok,))
            else:
                write_stage(tb, dt, sl)

        def ln_stats_and_norm(tb, src_tok, write_stage, bk=(0, 1)):
            ln_sums_and_stats(tb, src_tok, bk)
            for dt in range(c.NDT):
                ln_norm_one(tb, src_tok, write_stage, dt)

        for tb in range(c.B if "C" in phases else 0):
            xb = x_bf[:, tb * NT:(tb + 1) * NT]
            cosb = cosc_sb[:, tb * c.TBW:(tb + 1) * c.TBW]
            sinb = sinmc_sb[:, tb * c.TBW:(tb + 1) * c.TBW]
            for kv in range(c.KVH if "C1" not in phases else 0):
                if tb == c.B - 1 and kv == min(2, c.KVH - 1) and "C5" not in phases:
                    for tbn in range(c.B - 1):
                        ln_sums_and_stats(tbn, f"xr{tbn}", bk=(4, 5))
                for dt in range(c.NDT):
                    mm(PB[0][:, 0:c.TBW], wk_sb[:, (dt * c.KVH + kv) * 128:(dt * c.KVH + kv + 1) * 128],
                       xb[:, dt * c.TBW:(dt + 1) * c.TBW], "PB0", ("wk", "x"),
                       start=(dt == 0), stop=(dt == c.NDT - 1), inc=(dt == c.NDT - 1))
                for dt in range(c.NDT):
                    mm(PB[1][:, 0:c.TBW], wv_sb[:, (dt * c.KVH + kv) * 128:(dt * c.KVH + kv + 1) * 128],
                       xb[:, dt * c.TBW:(dt + 1) * c.TBW], "PB1", ("wv", "x"),
                       start=(dt == 0), stop=(dt == c.NDT - 1), inc=(dt == c.NDT - 1))
                kfs = c_kf[:, kv * c.TBW:(kv + 1) * c.TBW]
                rope_elu(PB[0][:, 0:c.TBW], "PB0", cosb, sinb, ("cosc", "sinmc"),
                         kfs, "ckf", kv % 2, rot_bank=2 + kv % 2)
                vop(lambda v, o=c_v, p=PB[1], q=kv:
                        v.tensor_copy(o[:, q * c.TBW:(q + 1) * c.TBW], p[:, 0:c.TBW]),
                    ("PB1",), ("cv",))
                for ci in range(c.CH):
                    off = kv * c.TBW + ci * 128
                    tr(PT[0][:, 512 + ci * 128:512 + (ci + 1) * 128], c_kf[:, off:off + 128],
                       id_bf[:], "PT0", ("ckf", "id_bf"), inc=(ci == c.CH - 1))
                    tr(PT[1][:, 512 + ci * 128:512 + (ci + 1) * 128], c_v[:, off:off + 128],
                       id_bf[:], "PT1", ("cv", "id_bf"), inc=(ci == c.CH - 1))
                vop(lambda v, o=c_kft, p=PT[0], q=kv:
                        v.tensor_copy(o[:, q * c.TBW:(q + 1) * c.TBW], p[:, 512:512 + c.TBW]),
                    ("PT0",), ("ckft",))
                vop(lambda v, o=c_vt, p=PT[1], q=kv:
                        v.tensor_copy(o[:, q * c.TBW:(q + 1) * c.TBW], p[:, 512:512 + c.TBW]),
                    ("PT1",), ("cvt",))
            for h in range(c.H if "C2" not in phases else 0):
                sl = h % 2
                bank = sl
                gdma(wqo[sl][:, 0:c.NDT * 128], flat(ins["wq_pk"][h]), f"wqo{sl}")
                for dt in range(c.NDT):
                    mm(PB[bank][:, 0:c.TBW], wqo[sl][:, dt * 128:(dt + 1) * 128],
                       xb[:, dt * c.TBW:(dt + 1) * c.TBW], f"PB{bank}",
                       (f"wqo{sl}", "x"), start=(dt == 0), stop=(dt == c.NDT - 1),
                       inc=(dt == c.NDT - 1))
                rope_elu(PB[bank][:, 0:c.TBW], f"PB{bank}", cosb, sinb,
                         ("cosc", "sinmc"), c_qf[:, h * c.TBW:(h + 1) * c.TBW], "cqf",
                         sl, rot_bank=2 + sl)
            GW = c.GQ * 128
            denr_sl = [rsA[0], rsA[1], act_s[0], act_s[1]]
            denr_tk = ["rs00", "rs01", "act0", "act1"]
            for ci in range(c.CH if "C3" not in phases else 0):
                for kv in range(c.KVH):
                    for hl in range(c.GQ):
                        h = kv * c.GQ + hl
                        mm(PB[kv][:, hl * 128:(hl + 1) * 128],
                           c_kf[:, kv * c.TBW + ci * 128:kv * c.TBW + (ci + 1) * 128],
                           c_qf[:, h * c.TBW + ci * 128:h * c.TBW + (ci + 1) * 128],
                           f"PB{kv}", ("ckf", "cqf"), inc=(hl == c.GQ - 1))
                for kv in range(c.KVH):
                    vop(lambda v, q=kv: v.tensor_mul(amst[:, q * GW:(q + 1) * GW],
                                                     PB[q][:, 0:GW], maskg_sb[:]),
                        (f"PB{kv}", "maskg"), (f"amst{kv}",))
                for kv in range(c.KVH):
                    so = (tb * c.KVH + kv) * 128
                    for hl in range(c.GQ):
                        h = kv * c.GQ + hl
                        mm(PB[kv][0:1, hl * 128:(hl + 1) * 128], onescol[:],
                           amst[:, kv * GW + hl * 128:kv * GW + (hl + 1) * 128],
                           f"PB{kv}", ("onescol", f"amst{kv}"), start=True,
                           stop=False, inc=False)
                        mm(PB[kv][0:1, hl * 128:(hl + 1) * 128], Z_bf[:, so:so + 1],
                           c_qf[:, h * c.TBW + ci * 128:h * c.TBW + (ci + 1) * 128],
                           f"PB{kv}", ("Zbf", "cqf"), start=False, stop=True,
                           inc=(hl == c.GQ - 1))
                for kv in range(c.KVH):
                    s.op("scalar", lambda a, d=denr_sl[kv], q=kv: a.activation(
                            d[0:1, 0:GW], PB[q][0:1, 0:GW], AF.Ln,
                            bias=epsc[0:1, 0:1]),
                         reads=(f"PB{kv}", "epsc"), writes=(denr_tk[kv],))
                    s.op("scalar", lambda a, d=denr_sl[kv]: a.activation(
                            d[0:1, 0:GW], d[0:1, 0:GW], AF.Exp, scale=-1.0),
                         reads=(denr_tk[kv],), writes=(denr_tk[kv],))
                for pair in range(0, c.KVH, 2):
                    kvs = [k for k in (pair, pair + 1) if k < c.KVH]
                    for kv in kvs:
                        so = (tb * c.KVH + kv) * 128
                        nb = 4 + (kv % 2)
                        for hl in range(c.GQ):
                            h = kv * c.GQ + hl
                            mm(PB[nb][:, hl * 128:(hl + 1) * 128],
                               c_vt[:, kv * c.TBW + ci * 128:kv * c.TBW + (ci + 1) * 128],
                               amst[:, kv * GW + hl * 128:kv * GW + (hl + 1) * 128],
                               f"PB{nb}", ("cvt", f"amst{kv}"), start=True,
                               stop=False, inc=False)
                            mm(PB[nb][:, hl * 128:(hl + 1) * 128],
                               S_bf[:, so:so + 128],
                               c_qf[:, h * c.TBW + ci * 128:h * c.TBW + (ci + 1) * 128],
                               f"PB{nb}", ("Sbf", "cqf"), start=False, stop=True,
                               inc=(hl == c.GQ - 1))
                    for kv in kvs:
                        for hl in range(c.GQ):
                            mm(PB[kv][:, hl * 128:(hl + 1) * 128], ones_row_f[0:1, :],
                               denr_sl[kv][0:1, hl * 128:(hl + 1) * 128], f"PB{kv}",
                               ("onesrow", denr_tk[kv]), inc=(hl == c.GQ - 1))
                    for kv in kvs:
                        nb = 4 + (kv % 2)
                        vop(lambda v, q=kv: v.tensor_copy(rsB[q % 2][:, 0:GW],
                                                          PB[q][:, 0:GW]),
                            (f"PB{kv}",), (f"rs1{kv % 2}",))
                        ai3 = c_ai.rearrange("p (h t) -> p h t", h=c.H)[
                            :, kv * c.GQ:(kv + 1) * c.GQ, ci * 128:(ci + 1) * 128]
                        n3 = PB[nb][:, 0:GW].rearrange("p (h t) -> p h t", h=c.GQ)
                        d3 = rsB[kv % 2][:, 0:GW].rearrange("p (h t) -> p h t", h=c.GQ)
                        vop(lambda v, o=ai3, a=n3, b=d3: v.tensor_mul(o, a, b),
                            (f"PB{nb}", f"rs1{kv % 2}"), ("cai",))
                if ci < c.CH - 1:
                    for kv in range(c.KVH):
                        so = (tb * c.KVH + kv) * 128
                        nb = 4 + (kv % 2)
                        mm(PB[kv][:, 0:128],
                           c_kft[:, kv * c.TBW + ci * 128:kv * c.TBW + (ci + 1) * 128],
                           c_vt[:, kv * c.TBW + ci * 128:kv * c.TBW + (ci + 1) * 128],
                           f"PB{kv}", ("ckft", "cvt"))
                        mm(PB[nb][:, 0:128],
                           c_kft[:, kv * c.TBW + ci * 128:kv * c.TBW + (ci + 1) * 128],
                           ones128[:], f"PB{nb}", ("ckft", "ones128"))
                        vop(lambda v, q=so, b=kv: v.tensor_add(
                                S_bf[:, q:q + 128], S_bf[:, q:q + 128], PB[b][:, 0:128]),
                            (f"PB{kv}", "Sbf"), ("Sbf",))
                        vop(lambda v, q=so, b=nb: v.tensor_add(
                                Z_bf[:, q:q + 128], Z_bf[:, q:q + 128], PB[b][:, 0:128]),
                            (f"PB{nb}", "Zbf"), ("Zbf",))
            for dt in range(c.NDT if "C4" not in phases else 0):
                sl = dt % 2
                bank = 2 + sl
                gdma(wqo[sl][:, 0:c.H * 128], flat(ins["wo_pk"][dt]), f"wqo{sl}")
                for h in range(c.H):
                    mm(PB[bank][:, 0:c.TBW], wqo[sl][:, h * 128:(h + 1) * 128],
                       c_ai[:, h * c.TBW:(h + 1) * c.TBW], f"PB{bank}",
                       (f"wqo{sl}", "cai"), start=(h == 0), stop=(h == c.H - 1),
                       inc=(h == c.H - 1))
                xro = xr[:, tb * NT + dt * c.TBW:tb * NT + (dt + 1) * c.TBW]
                xio = x_bf[:, tb * NT + dt * c.TBW:tb * NT + (dt + 1) * c.TBW]
                vop(lambda v, o=xro, b=bank, i=xio:
                        v.tensor_add(o, PB[b][:, 0:c.TBW], i),
                    (f"PB{bank}", "x"), (f"xr{tb}",))
                if tb == c.B - 1 and "C5" not in phases:
                    for tbn in range(c.B - 1):
                        ln_norm_one(tbn, f"xr{tbn}", None, dt)
            if tb == c.B - 1:
                snap_attn = s.counts()
            if "C5" not in phases and tb == c.B - 1:
                ln_stats_and_norm(tb, f"xr{tb}", None, bk=(4, 5))

        def emit_gate():
            for tb in range(c.B):
                for dt in range(c.NDT):
                    mm(PB[3][0:c.E, 0:c.TBW], gate_sb[:, dt * c.E:(dt + 1) * c.E],
                       xr[:, tb * NT + dt * c.TBW:tb * NT + (dt + 1) * c.TBW], "PB3",
                       ("gate", f"xr{tb}"), start=(dt == 0), stop=(dt == c.NDT - 1),
                       inc=(dt == c.NDT - 1))
                s.op("scalar", lambda a, t=tb: a.activation(
                        scor[0:c.E, t * c.TBW:(t + 1) * c.TBW], PB[3][0:c.E, 0:c.TBW],
                        AF.Sigmoid),
                     reads=("PB3",), writes=("scor",))
            for tc in range(c.B * c.CH):
                tr(PB[4][:, 0:c.E], scor[0:c.E, tc * 128:(tc + 1) * 128],
                   id_f[0:c.E, 0:c.E], "PB4", ("scor", "idf"))
                vop(lambda v: v.tensor_copy(sct[:], PB[4][:, 0:c.E]), ("PB4",), ("sct",))
                vop(lambda v: v.tensor_add(sbt[:], sct[:], biasb[:]), ("sct", "biasb"), ("sbt",))
                vop(lambda v: v.tensor_reduce(colv[:], sbt[:], AXX, ALU.max), ("sbt",), ("colv",))
                vop(lambda v: v.tensor_single_scalar(is1[:], sbt[:], colv[:, 0:1], ALU.is_ge),
                    ("sbt", "colv"), ("is1",))
                vop(lambda v: v.scalar_tensor_tensor(sbm[:], is1[:], -1e30, sbt[:],
                                                     ALU.mult, ALU.add),
                    ("is1", "sbt"), ("sbm",))
                vop(lambda v: v.tensor_reduce(colv2[:], sbm[:], AXX, ALU.max), ("sbm",), ("colv2",))
                vop(lambda v: v.tensor_single_scalar(m2t[:], sbt[:], colv2[:, 0:1], ALU.is_ge),
                    ("sbt", "colv2"), ("m2t",))
                vop(lambda v: v.tensor_mul(gtk[:], m2t[:], sct[:]), ("m2t", "sct"), ("gtk",))
                tr(PB[5][0:c.E, 0:128], gtk[:, 0:c.E], id_f[:], "PB5", ("gtk", "idf"))
                vop(lambda v, q=tc: v.tensor_copy(G_T[0:c.E, q * 128:(q + 1) * 128],
                                                  PB[5][0:c.E, 0:128]),
                    ("PB5",), ("GT",))

        s.wait_counts("gpsimd", snap_attn)

        HHT = c.NHT * c.B * c.TBW
        f_H = ar[:, 0:HHT]
        W1C = c.NDT * 128
        f_w1 = [ar[:, HHT + i * W1C:HHT + (i + 1) * W1C] for i in range(2)]
        f_w3 = [ar[:, HHT + 2 * W1C + i * W1C:HHT + 2 * W1C + (i + 1) * W1C] for i in range(2)]
        W2C = c.NHT * 128
        w2o = HHT + 4 * W1C
        f_w2 = [ar[:, w2o + i * W2C:w2o + (i + 1) * W2C] for i in range(2)]

        for ei in range(c.E + 1 if "F" in phases else 0):
            e = ei - 1
            routed = e >= 0
            if routed:
                for tb in range(c.B):
                    mm(PB[5][:, 0:c.TBW], esel_sb[:, e * 128:(e + 1) * 128],
                       G_T[0:c.E, tb * c.TBW:(tb + 1) * c.TBW], "PB5",
                       ("esel", "GT"))
                    vop(lambda v, t=tb: v.tensor_copy(rsA[t][:, 0:c.TBW], PB[5][:, 0:c.TBW]),
                        ("PB5",), (f"rs0{tb}",))

            def h_phase(tbs, w_off):
                for ht in range(c.NHT):
                    sl = (ht + w_off) % 2
                    w1src = ins["w1_pk"][e, ht] if routed else ins["sw1_pk"][ht]
                    w3src = ins["w3_pk"][e, ht] if routed else ins["sw3_pk"][ht]
                    gdma(f_w1[sl], flat(w1src), f"w1{sl}")
                    gdma(f_w3[sl], flat(w3src), f"w3{sl}")
                    for tb in tbs:
                        bu = tb
                        bv = 2 + tb
                        x1b = xr[:, tb * NT:(tb + 1) * NT]
                        for dt in range(c.NDT):
                            mm(PB[bu][:, 0:c.TBW], f_w1[sl][:, dt * 128:(dt + 1) * 128],
                               x1b[:, dt * c.TBW:(dt + 1) * c.TBW], f"PB{bu}",
                               (f"w1{sl}", f"xr{tb}"), start=(dt == 0),
                               stop=(dt == c.NDT - 1), inc=(dt == c.NDT - 1))
                        for dt in range(c.NDT):
                            mm(PB[bv][:, 0:c.TBW], f_w3[sl][:, dt * 128:(dt + 1) * 128],
                               x1b[:, dt * c.TBW:(dt + 1) * c.TBW], f"PB{bv}",
                               (f"w3{sl}", f"xr{tb}"), start=(dt == 0),
                               stop=(dt == c.NDT - 1), inc=(dt == c.NDT - 1))
                        hsl = f_H[:, (ht * c.B + tb) * c.TBW:(ht * c.B + tb + 1) * c.TBW]
                        if routed:
                            vop(lambda v, o=act_s[tb], b=bu, t=tb:
                                    v.tensor_mul(o[:], PB[b][:, 0:c.TBW], rsA[t][:, 0:c.TBW]),
                                (f"PB{bu}", f"rs0{tb}"), (f"act{tb}",))
                            s.op("scalar", lambda a, o=silu_s[tb], i=act_s[tb]:
                                     a.activation(o[:], i[:], AF.Silu),
                                 reads=(f"act{tb}",), writes=(f"silu{tb}",))
                            vop(lambda v, o=sq_s[tb], b=bv, t=tb:
                                    v.tensor_mul(o[:], PB[b][:, 0:c.TBW], rsA[t][:, 0:c.TBW]),
                                (f"PB{bv}", f"rs0{tb}"), (f"sq{tb}",))
                            vop(lambda v, o=hsl, a=silu_s[tb], b=sq_s[tb]:
                                    v.tensor_mul(o, a[:], b[:]),
                                (f"silu{tb}", f"sq{tb}"), ("H",))
                        else:
                            s.op("scalar", lambda a, o=silu_s[tb], b=bu:
                                     a.activation(o[:], PB[b][:, 0:c.TBW], AF.Silu),
                                 reads=(f"PB{bu}",), writes=(f"silu{tb}",))
                            vop(lambda v, o=sq_s[tb], b=bv:
                                    v.tensor_copy(o[:], PB[b][:, 0:c.TBW]),
                                (f"PB{bv}",), (f"sq{tb}",))
                            vop(lambda v, o=hsl, a=silu_s[tb], b=sq_s[tb]:
                                    v.tensor_mul(o, a[:], b[:]),
                                (f"silu{tb}", f"sq{tb}"), ("H",))

            if not routed:
                h_phase([0], 0)
                if "G" in phases:
                    emit_gate()
                h_phase([1], c.NHT)
            else:
                h_phase([0, 1], 0)
            def ws(tb_, dt_, sl_):
                ss = dt_ % 2
                vop(lambda v, o=rsA[ss], i=rsB[sl_]:
                        v.tensor_mul(o[:, 0:c.TBW], i[:, 0:c.TBW], rstd_b[:]),
                    (f"rs1{sl_}", "rstd"), (f"rs0{ss}",))
                sdma_out(out_dram[dt_ * 128:(dt_ + 1) * 128, tb_ * c.TBW:(tb_ + 1) * c.TBW],
                         rsA[ss][:, 0:c.TBW], f"rs0{ss}")

            last = ei == c.E and "L2" in phases
            for dt in range(c.NDT):
                sl = dt % 2
                w2src = ins["w2_pk"][e, dt] if routed else ins["sw2_pk"][dt]
                gdma(f_w2[sl], flat(w2src), f"w2{sl}")
                for tb in range(c.B):
                    bank = 4 + tb
                    for ht in range(c.NHT):
                        mm(PB[bank][:, 0:c.TBW], f_w2[sl][:, ht * 128:(ht + 1) * 128],
                           f_H[:, (ht * c.B + tb) * c.TBW:(ht * c.B + tb + 1) * c.TBW],
                           f"PB{bank}", (f"w2{sl}", "H"), start=(ht == 0),
                           stop=(ht == c.NHT - 1), inc=(ht == c.NHT - 1))
                    rsl = x_bf[:, tb * NT + dt * c.TBW:tb * NT + (dt + 1) * c.TBW]
                    if ei == 0:
                        vop(lambda v, o=rsl, b=bank: v.tensor_copy(o, PB[b][:, 0:c.TBW]),
                            (f"PB{bank}",), ("racc",))
                    else:
                        vop(lambda v, o=rsl, b=bank: v.tensor_add(o, o, PB[b][:, 0:c.TBW]),
                            (f"PB{bank}", "racc"), ("racc",))
                    if last:
                        off = tb * NT + dt * c.TBW
                        vop(lambda v, q=off: v.tensor_add(
                                xr[:, q:q + c.TBW], xr[:, q:q + c.TBW],
                                x_bf[:, q:q + c.TBW]),
                            (f"xr{tb}", "racc"), (f"xr{tb}",))
            if last:
                ln_stats_and_norm(0, "xr0", ws)
                ln_stats_and_norm(1, "xr1", ws)

        s.finish()

        block = ctx.enter_context(nc.Block())
        block.sync(lambda e: [f(e) for f in s.progs["sync"]])
        block.gpsimd(lambda e: [f(e) for f in s.progs["gpsimd"]])
        block.vector(lambda e: [f(e) for f in s.progs["vector"]])
        block.scalar(lambda e: [f(e) for f in s.progs["scalar"]])
        block.tensor(lambda e: [f(e) for f in s.progs["tensor"]])

    return nc


def _rope_tables(cfg):
    hd = cfg.HD
    inv = 1.0 / ROPE_BASE ** (np.arange(0, hd, 2, dtype=np.float64) / hd)
    t = np.arange(cfg.S, dtype=np.float64)
    fr = np.outer(t, inv)
    emb = np.concatenate([fr, fr], -1)
    cosT = np.cos(emb).T.astype(np.float32)
    sinT = np.sin(emb).T.astype(np.float32)
    sinmT = sinT.copy()
    sinmT[: hd // 2] *= -1.0
    sinm2T = np.roll(sinmT, -64, axis=0)
    return cosT, sinm2T


def _pack_weights(cfg, inp):
    c = cfg
    pk = {}
    Wq = inp["L0_attn_Wq"]
    pk["wq_pk"] = np.ascontiguousarray(
        Wq.reshape(c.H, 128, c.NDT, 128).transpose(0, 3, 2, 1))
    Wk = inp["L0_attn_Wk"]
    pk["wk_pk"] = np.ascontiguousarray(
        Wk.reshape(c.KVH, 128, c.NDT, 128).transpose(3, 2, 0, 1))
    Wv = inp["L0_attn_Wv"]
    pk["wv_pk"] = np.ascontiguousarray(
        Wv.reshape(c.KVH, 128, c.NDT, 128).transpose(3, 2, 0, 1))
    Wo = inp["L0_attn_Wo"]
    pk["wo_pk"] = np.ascontiguousarray(
        Wo.reshape(c.NDT, 128, c.H, 128).transpose(0, 3, 2, 1))
    gw = inp["L0_ffn_gate_w"]
    pk["gate_pk"] = np.ascontiguousarray(
        gw.reshape(c.E, c.NDT, 128).transpose(2, 1, 0))
    pk["bias_row"] = np.ascontiguousarray(
        inp["L0_ffn_expert_bias"].reshape(1, c.E))
    for nm, w in (("w1_pk", inp["L0_ffn_w1"]), ("w3_pk", inp["L0_ffn_w3"])):
        pk[nm] = np.ascontiguousarray(
            w.reshape(c.E, c.NHT, 128, c.NDT, 128).transpose(0, 1, 4, 3, 2))
    pk["w2_pk"] = np.ascontiguousarray(
        inp["L0_ffn_w2"].reshape(c.E, c.NDT, 128, c.NHT, 128).transpose(0, 1, 4, 3, 2))
    for nm, w in (("sw1_pk", inp["L0_ffn_sw1"]), ("sw3_pk", inp["L0_ffn_sw3"])):
        pk[nm] = np.ascontiguousarray(
            w.reshape(c.NHT, 128, c.NDT, 128).transpose(0, 3, 2, 1))
    pk["sw2_pk"] = np.ascontiguousarray(
        inp["L0_ffn_sw2"].reshape(c.NDT, 128, c.NHT, 128).transpose(0, 3, 2, 1))
    pk["id_f"] = np.eye(128, dtype=np.float32)
    r64 = np.zeros((128, 128), np.float32)
    for mm_ in range(128):
        r64[(mm_ + 64) % 128, mm_] = 1.0
    pk["r64"] = r64
    es = np.zeros((c.E, c.E * 128), np.float32)
    for e in range(c.E):
        es[e, e * 128:(e + 1) * 128] = 1.0
    pk["esel"] = es
    jj = np.arange(128)
    m1 = (jj[:, None] <= jj[None, :]).astype(np.float32)
    pk["maskg"] = np.ascontiguousarray(np.tile(m1, (1, c.GQ)))
    return pk


def _core_inputs(cfg, inp, pk, cosT, sinmT, core):
    c = cfg
    x = inp["L0_attn_x"]
    starts = [core * c.TBW, (c.NC - 1 - core) * c.TBW]
    d = {}

    def xpack(tb, t0):
        blk = x[tb, t0:t0 + c.TBW, :]
        return np.ascontiguousarray(
            blk.reshape(c.TBW, c.NDT, 128).transpose(2, 1, 0))

    d["xs"] = np.stack([xpack(tb, starts[tb]) for tb in range(c.B)])
    pre = [(0, g) for g in range(core)] + [(1, g) for g in range(c.NC - 1 - core)]
    assert len(pre) == c.NPRE
    d["xa_pre"] = (np.stack([xpack(tb, g * c.TBW) for (tb, g) in pre])
                   if pre else np.zeros((0, 128, c.NDT, c.TBW), np.float32))
    sc = np.zeros((c.NPRE, 2), np.float32)
    for i, (tb, g) in enumerate(pre):
        sc[i, tb] = 1.0
    d["scales"] = np.ascontiguousarray(
        np.broadcast_to(sc.reshape(1, -1), (128, c.NPRE * 2)).astype(np.float32))
    d["cosc"] = np.stack([np.ascontiguousarray(cosT[:, t0:t0 + c.TBW]) for t0 in starts])
    d["sinmc"] = np.stack([np.ascontiguousarray(sinmT[:, t0:t0 + c.TBW]) for t0 in starts])
    d["cospre"] = (np.stack([np.ascontiguousarray(cosT[:, g * c.TBW:(g + 1) * c.TBW])
                             for (_, g) in pre])
                   if pre else np.zeros((0, 128, c.TBW), np.float32))
    d["sinmpre"] = (np.stack([np.ascontiguousarray(sinmT[:, g * c.TBW:(g + 1) * c.TBW])
                              for (_, g) in pre])
                    if pre else np.zeros((0, 128, c.TBW), np.float32))
    d.update(pk)
    return d


def run(cfg, inp, sim=None, phases=("A", "C", "G", "F", "L2")):
    c = cfg
    nc = build_program(c, phases=phases)
    cosT, sinmT = _rope_tables(c)
    pk = _pack_weights(c, inp)
    in_maps = [_core_inputs(c, inp, pk, cosT, sinmT, core) for core in range(c.NC)]
    runner = sim or (lambda n, m: run_bass_kernel_spmd(n, m, core_ids=list(range(c.NC))))
    res = runner(nc, in_maps)
    out = np.empty((c.B, c.S, c.D), np.float32)
    for core in range(c.NC):
        o = res.results[core]["out"]
        out[0, core * c.TBW:(core + 1) * c.TBW, :] = o[:, 0:c.TBW].T
        s1 = (c.NC - 1 - core) * c.TBW
        out[1, s1:s1 + c.TBW, :] = o[:, c.TBW:].T
    return out


def kernel(**inputs):
    return run(FULL, inputs)
